# version 08
# baseline (speedup 1.0000x reference)
import numpy as np
import ml_dtypes
from contextlib import ExitStack

import concourse.bass as bass
import concourse.mybir as mybir
from concourse.bass_utils import run_bass_kernel_spmd

F32 = mybir.dt.float32
BF16 = mybir.dt.bfloat16
I8 = mybir.dt.int8
AF = mybir.ActivationFunctionType
ALU = mybir.AluOpType
NPBF = ml_dtypes.bfloat16

ENG = ("sync", "tensor", "vector", "scalar", "gpsimd")

TPC = 2048
D = 2048
H = 4
DK = 512
DV = 256
KD = 2048
VD = 1024
LOW = 16
INTER = 8192
CH = 128
TS = 1024
NT1 = TPC // TS
NCH1 = TS // CH
TF = 512
NT2 = TPC // TF
NKC = D // 128
NJ = INTER // 128
SCALE = 1.0 / float(np.sqrt(512.0))
EPS = 1e-5
INV16 = 1.0 / 16.0
AXX = mybir.AxisListType.X


class Prog:

    def __init__(self, nc):
        self.nc = nc
        self.ops = []
        self.counts = {}
        self.sem_order = []
        self.dma_sems = {}

    def _sem(self, name):
        if name not in self.counts:
            self.counts[name] = 0
            self.sem_order.append(name)
        return name

    def do(self, engine, fns, waits=(), inc=None, inc_n=1, noskip=False):
        if fns is None:
            fns = []
        if not isinstance(fns, (list, tuple)):
            fns = [fns]
        ws = [w for w in waits if w is not None]
        tok = None
        if inc is not None:
            self._sem(inc)
            self.counts[inc] += inc_n
            tok = (inc, self.counts[inc], engine, inc_n)
        self.ops.append((engine, list(fns), ws, (inc, inc_n) if inc else None,
                         noskip))
        return tok

    def dma(self, queue, out, in_, waits=(), sem=None, **kw):
        sem = sem or ("q_" + queue)
        self.dma_sems.setdefault(sem, queue)
        return self.do(
            queue,
            lambda e, out=out, in_=in_, kw=kw: e.dma_start(out=out, in_=in_, **kw),
            waits, inc=sem, inc_n=16,
        )

    def fence(self, engine):
        return self.do(engine, [], inc={"vector": "vv", "scalar": "sc",
                                        "tensor": "pe", "gpsimd": "gg",
                                        "sync": "sy"}[engine])

    def barrier(self):
        for sem, q in self.dma_sems.items():
            if self.counts.get(sem, 0) > 0:
                self.do(q, [], waits=[(sem, self.counts[sem], "none", 16)])
        toks = [self.do(e, [], inc="bar") for e in ENG]
        for e in ENG:
            self.do(e, [], waits=toks)

    def emit(self):
        nc = self.nc
        with ExitStack() as stx:
            sems = {n: stx.enter_context(nc.semaphore(n)) for n in self.sem_order}
            waited = {e: {} for e in ENG}
            with nc.Block() as block:
                def mk(engname):
                    def body(eng):
                        for (e, fns, ws, inc, noskip) in self.ops:
                            if e != engname:
                                continue
                            need = {}
                            for (sem, val, prod, n) in ws:
                                if waited[engname].get(sem, 0) >= val:
                                    continue
                                if need.get(sem, 0) < val:
                                    need[sem] = val
                            for sem, val in need.items():
                                eng.wait_ge(sems[sem], val)
                                waited[engname][sem] = val
                            inst = None
                            for fn in fns:
                                inst = fn(eng)
                            if inc is not None:
                                if inst is None:
                                    inst = eng.nop()
                                inst.then_inc(sems[inc[0]], inc[1])
                    return body
                block.sync(mk("sync"))
                block.tensor(mk("tensor"))
                block.vector(mk("vector"))
                block.scalar(mk("scalar"))
                block.gpsimd(mk("gpsimd"))


def build(upto="full", dump=(), lim=None):
    nc = bass.Bass("TRN2", target_bir_lowering=False)
    P = Prog(nc)
    dump = set(dump)

    dx = nc.dram_tensor("xs", [TPC, D], F32, kind="ExternalInput").ap()
    dwq = nc.dram_tensor("wq", [D, KD], F32, kind="ExternalInput").ap()
    dwk = nc.dram_tensor("wk", [D, KD], F32, kind="ExternalInput").ap()
    dwv = nc.dram_tensor("wv", [D, VD], F32, kind="ExternalInput").ap()
    dwg = nc.dram_tensor("wg", [D, VD], F32, kind="ExternalInput").ap()
    dwgk1 = nc.dram_tensor("wgk1", [D, LOW], F32, kind="ExternalInput").ap()
    dwgk2 = nc.dram_tensor("wgk2", [LOW, KD], F32, kind="ExternalInput").ap()
    dwo = nc.dram_tensor("wo", [VD, D], F32, kind="ExternalInput").ap()
    dbgk2 = nc.dram_tensor("bgk2pf", [128, 16], F32, kind="ExternalInput").ap()
    dwnorm = nc.dram_tensor("wnormr", [1, DV], F32, kind="ExternalInput").ap()
    dsg = nc.dram_tensor("sgpf", [128, NJ], F32, kind="ExternalInput").ap()
    dsu = nc.dram_tensor("supf", [128, NJ], F32, kind="ExternalInput").ap()
    dsdn = nc.dram_tensor("sdnr", [1, D], F32, kind="ExternalInput").ap()
    dwg8 = nc.dram_tensor("wg8t", [NJ, 128, 16 * 128], I8, kind="ExternalInput").ap()
    dwu8 = nc.dram_tensor("wu8t", [NJ, 128, 16 * 128], I8, kind="ExternalInput").ap()
    dwd8 = nc.dram_tensor("wd8", [64, 128, D], I8, kind="ExternalInput").ap()
    dident = nc.dram_tensor("identb", [128, 128], BF16, kind="ExternalInput").ap()
    dmask = nc.dram_tensor("maskut", [128, 128], BF16, kind="ExternalInput").ap()
    dsel01 = nc.dram_tensor("sel01", [128, 3], F32, kind="ExternalInput").ap()
    dselsc = nc.dram_tensor("selsc", [128, 3], F32, kind="ExternalInput").ap()
    dselbi = nc.dram_tensor("selbi", [128, 3], F32, kind="ExternalInput").ap()

    dy = nc.dram_tensor("y", [TPC, D], F32, kind="ExternalOutput").ap()

    dxbf = nc.dram_tensor("xbf", [TPC, D], BF16).ap()
    dwqbf = nc.dram_tensor("wqbf", [H, 4, 128, 16 * 128], BF16).ap()
    dwkbf = nc.dram_tensor("wkbf", [H, 4, 128, 16 * 128], BF16).ap()
    dwvbf = nc.dram_tensor("wvbf", [H, 128, 16 * 256], BF16).ap()
    dwgbf = nc.dram_tensor("wgbf", [H, 128, 16 * 256], BF16).ap()
    dwobf = nc.dram_tensor("wobf", [4, 128, 8 * 512], BF16).ap()
    dwgfb = nc.dram_tensor("wgfb", [NJ, 128, 16 * 128], BF16).ap()
    dwufb = nc.dram_tensor("wufb", [NJ, 128, 16 * 128], BF16).ap()
    dwdfb = nc.dram_tensor("wdfb", [64, 128, D], BF16).ap()
    dqd = nc.dram_tensor("qd_scr", [NT1, H, 128, 4, TS], BF16).ap()
    dol = nc.dram_tensor("ol_scr", [NT1, H, 128, NCH1, DV], BF16).ap()
    dgs = nc.dram_tensor("g_scr", [NT1, H, 128, NCH1, DV], BF16).ap()
    dccinA = nc.dram_tensor("ccinA", [128, 16], F32).ap()
    dccoutA = nc.dram_tensor("ccoutA", [4 * 128, 16], F32).ap()
    dccinZ = nc.dram_tensor("ccinZ", [128, 4096], BF16).ap()
    dccoutZ = nc.dram_tensor("ccoutZ", [4 * 128, 4096], BF16).ap()

    ddump = {}
    U16 = mybir.dt.uint16

    def dumpten(name, shape, dtype=F32):
        if name not in ddump:
            if dtype == BF16:
                t = nc.dram_tensor("dbg_" + name, shape, U16,
                                   kind="ExternalOutput").ap()
                ddump[name] = t.bitcast(BF16)
            else:
                ddump[name] = nc.dram_tensor(
                    "dbg_" + name, shape, dtype, kind="ExternalOutput").ap()
        return ddump[name]

    st = ExitStack()
    e = st.enter_context

    sS = e(nc.sbuf_tensor("sS", [128, 16, DV], BF16))
    sId = e(nc.sbuf_tensor("sId", [128, 128], BF16))
    sMask = e(nc.sbuf_tensor("sMask", [128, 128], BF16))
    sWnB = e(nc.sbuf_tensor("sWnB", [128, DV], BF16))
    sGk1 = e(nc.sbuf_tensor("sGk1", [128, 16, LOW], BF16))
    sGk2 = e(nc.sbuf_tensor("sGk2", [LOW, KD], BF16))
    sBgk2 = e(nc.sbuf_tensor("sBgk2", [128, 16], F32))
    sSg = e(nc.sbuf_tensor("sSg", [128, NJ], F32))
    sSu = e(nc.sbuf_tensor("sSu", [128, NJ], F32))
    sSel01 = e(nc.sbuf_tensor("sSel01", [128, 3], F32))
    sSelsc = e(nc.sbuf_tensor("sSelsc", [128, 3], F32))
    sSelbi = e(nc.sbuf_tensor("sSelbi", [128, 3], F32))
    sPrev = e(nc.sbuf_tensor("sPrev", [128, 16], F32))
    sBl = e(nc.sbuf_tensor("sBl", [128, 16, NCH1], F32))
    sEbl = e(nc.sbuf_tensor("sEbl", [128, 16, NCH1], F32))
    sPexp = e(nc.sbuf_tensor("sPexp", [128, 16, NCH1], F32))
    sZb = e(nc.sbuf_tensor("sZb", [128, 128], F32))
    sOneB = e(nc.sbuf_tensor("sOneB", [1, 512], BF16))
    sEight = e(nc.sbuf_tensor("sEight", [128, NCH1], F32))
    sI1 = e(nc.sbuf_tensor("sI1", [128, 4, NCH1], F32))
    sI2 = e(nc.sbuf_tensor("sI2", [128, 4, NCH1], F32))
    sEps = e(nc.sbuf_tensor("sEps", [128, 1], F32))
    sW8f = e(nc.sbuf_tensor("sW8f", [128, 2, D], I8))
    sW8bf = e(nc.sbuf_tensor("sW8bf", [128, 2, D], BF16))
    sSdnP = e(nc.sbuf_tensor("sSdnP", [128, D], BF16))

    pb = [e(nc.psum_tensor(f"pb{i}", [128, 512], F32)) for i in range(8)]
    pb_free = [None] * 8

    def V(fn, waits=()):
        return P.do("vector", fn, waits, inc="vv")

    def Vs(fn, waits=()):
        return P.do("vector", fn, waits, inc="vv", noskip=True)

    def SC(fn, waits=()):
        return P.do("scalar", fn, waits, inc="sc")

    def GP(fn, waits=()):
        return P.do("gpsimd", fn, waits, inc="gg")

    def MM(out_ap, pairs, waits=()):
        n = len(pairs)
        fns = []
        for i, (l, r) in enumerate(pairs):
            fns.append(lambda eng, l=l, r=r, s=(i == 0), t=(i == n - 1), o=out_ap:
                       eng.matmul(o, l, r, start=s, stop=t))
        return P.do("tensor", fns, waits, inc="pe")

    def TR(out_ap, in_ap, waits=()):
        return P.do("tensor",
                    lambda eng, o=out_ap, i=in_ap: eng.transpose(o, i, sId[:]),
                    waits, inc="pe")

    t_id = P.dma("sync", sId[:], dident, sem="c_id")
    t_mask = P.dma("sync", sMask[:], dmask, sem="c_mk")
    t_bgk2 = P.dma("sync", sBgk2[:], dbgk2, sem="c_bg")
    t_sg = P.dma("sync", sSg[:], dsg, sem="c_sg")
    t_su = P.dma("sync", sSu[:], dsu, sem="c_su")
    t_sel = P.dma("sync", sSel01[:], dsel01, sem="c_s0")
    t_selsc = P.dma("sync", sSelsc[:], dselsc, sem="c_s1")
    t_selbi = P.dma("sync", sSelbi[:], dselbi, sem="c_s2")
    t_zb = V(lambda e_: e_.memset(sZb[:], 0.0))
    t_ones = V(lambda e_: e_.memset(sOneB[:], 1.0))
    t_prev = V(lambda e_: e_.memset(sPrev[:], 0.0))
    t_s0 = V(lambda e_: e_.memset(sS.ap().rearrange("p a b -> p (a b)"), 0.0))
    t_eps = V(lambda e_: e_.memset(sEps[:], EPS))

    st0 = ExitStack()
    e0 = st0.enter_context
    sXst = [e0(nc.sbuf_tensor(f"sXst{i}", [128, 2, D], F32)) for i in range(2)]
    sXbfs = [e0(nc.sbuf_tensor(f"sXbfs{i}", [128, 2, D], BF16)) for i in range(2)]
    sWbig = [e0(nc.sbuf_tensor(f"sWbig{i}", [128, 8, 512], F32)) for i in range(2)]
    sWcol = [e0(nc.sbuf_tensor(f"sWcol{i}", [128, 16, 512], BF16)) for i in range(2)]
    sW8 = [e0(nc.sbuf_tensor(f"sW8_{i}", [128, 2, D], I8)) for i in range(2)]
    sW8b = [e0(nc.sbuf_tensor(f"sW8b{i}", [128, 2, D], BF16)) for i in range(2)]
    sRow = e0(nc.sbuf_tensor("sRow", [1, D], F32))
    sRowB = e0(nc.sbuf_tensor("sRowB", [1, D], BF16))
    sGkSt = e0(nc.sbuf_tensor("sGkSt", [128, 16 * LOW], F32))
    sGk2St = e0(nc.sbuf_tensor("sGk2St", [LOW, KD], F32))

    t = P.dma("sync", sGkSt.ap().rearrange("p (k f) -> p k f", k=16),
              dwgk1.rearrange("(k p) f -> p k f", p=128), sem="c_g1")
    t_gk1 = V(lambda e_: e_.tensor_copy(
        out=sGk1.ap().rearrange("p a b -> p (a b)"), in_=sGkSt[:]), [t])
    t = P.dma("sync", sGk2St[:], dwgk2, sem="c_g2")
    t_gk2 = V(lambda e_: e_.tensor_copy(out=sGk2[:], in_=sGk2St[:]), [t])

    t = P.dma("sync", sRow[0:1, 0:DV], dwnorm, sem="c_wn")
    t_wnrow = V(lambda e_: e_.tensor_copy(out=sRowB[0:1, 0:DV],
                                          in_=sRow[0:1, 0:DV]), [t, t_ones])
    t = MM(pb[0][:, 0:DV], [(sOneB[0:1, 0:128], sRowB[0:1, 0:DV])], [t_wnrow])
    t_wnb = V(lambda e_: e_.tensor_copy(out=sWnB[:], in_=pb[0][:, 0:DV]), [t])
    pb_free[0] = t_wnb

    t = P.dma("sync", sRow[:], dsdn, [t_wnrow], sem="c_sd")
    t = V(lambda e_: e_.tensor_copy(out=sRowB[:], in_=sRow[:]), [t, t_wnb])
    t_sdnb = None
    for i in range(4):
        tm = MM(pb[1][:, 0:512],
                [(sOneB[0:1, 0:128], sRowB[0:1, i * 512:(i + 1) * 512])],
                [t, pb_free[1]])
        t_sdnb = V(lambda e_, i=i: e_.tensor_copy(
            out=sSdnP[:, i * 512:(i + 1) * 512], in_=pb[1][:, 0:512]), [tm])
        pb_free[1] = t_sdnb

    dxr = dx.rearrange("(i p) m -> p i m", p=128)
    dxbr = dxbf.rearrange("(i p) m -> p i m", p=128)
    t_xbf2 = []
    prev_st = [None, None]
    for i in range(8):
        b = i % 2
        tl = P.dma("sync", sXst[b].ap(), dxr[:, i * 2:(i + 1) * 2, :],
                   [prev_st[b]], sem=f"sxl{b}")
        tc = SC(lambda e_, b=b: e_.copy(
            out=sXbfs[b].ap().rearrange("p a b -> p (a b)"),
            in_=sXst[b].ap().rearrange("p a b -> p (a b)")), [tl])
        ts_ = P.dma("gpsimd" if b == 0 else "scalar",
                    dxbr[:, i * 2:(i + 1) * 2, :], sXbfs[b].ap(), [tc],
                    sem=f"sxs{b}")
        prev_st[b] = ts_
        t_xbf2.append(ts_)
    t_xbf = [t_xbf2[i // 2] for i in range(16)]

    t_awb = {}
    prev_big = [None, None]
    prev_col = [None, None]
    _blk = [0]

    def cast_big(tag, wsrc, col0, ncol, scale, outs):
        i = _blk[0]
        _blk[0] += 1
        cb = i % 2
        wr = wsrc.rearrange("(k p) m -> p k m", p=128)
        tcs = []
        for hh in range(2):
            tl = P.dma("sync",
                       sWbig[cb].ap()[:, hh * 4:(hh + 1) * 4, 0:ncol]
                       if False else sWbig[cb].ap()[:, 0:8, 0:ncol],
                       wr[:, hh * 8:(hh + 1) * 8, col0:col0 + ncol],
                       [prev_big[cb] if hh == 0 else tcs[-1]], sem=f"swl{cb}")
            eng = "vector" if i % 2 == 0 else "gpsimd"
            tc = P.do(eng, lambda e_, cb=cb, hh=hh, ncol=ncol, sc_=scale:
                      (e_.tensor_copy(
                          out=sWcol[cb].ap()[:, hh * 8:(hh + 1) * 8, 0:ncol],
                          in_=sWbig[cb].ap()[:, 0:8, 0:ncol])
                       if sc_ is None else e_.tensor_scalar(
                          out=sWcol[cb].ap()[:, hh * 8:(hh + 1) * 8, 0:ncol],
                          in0=sWbig[cb].ap()[:, 0:8, 0:ncol],
                          scalar1=sc_, scalar2=None, op0=ALU.mult)),
                      [tl, prev_col[cb]],
                      inc="vv" if eng == "vector" else "gg")
            tcs.append(tc)
            prev_big[cb] = tc
        stoks = []
        for (oap, srcsl) in outs:
            stoks.append(P.dma("gpsimd" if i % 2 == 0 else "scalar", oap,
                               sWcol[cb].ap()[:, :, srcsl[0]:srcsl[1]],
                               tcs + ([stoks[-1]] if stoks else [prev_col[cb]]),
                               sem=f"sws{cb}"))
        prev_col[cb] = stoks[-1]
        t_awb[tag] = stoks
        return stoks

    for h in range(H):
        cast_big(("q", h), dwq, h * 512, 512, SCALE,
                 [(dwqbf[h, d2].rearrange("p (a b) -> p a b", a=16),
                   (d2 * 128, (d2 + 1) * 128)) for d2 in range(4)])
        cast_big(("k", h), dwk, h * 512, 512, None,
                 [(dwkbf[h, d2].rearrange("p (a b) -> p a b", a=16),
                   (d2 * 128, (d2 + 1) * 128)) for d2 in range(4)])
        cast_big(("v", h), dwv, h * 256, 256, None,
                 [(dwvbf[h].rearrange("p (a b) -> p a b", a=16), (0, 256))])
        cast_big(("g", h), dwg, h * 256, 256, None,
                 [(dwgbf[h].rearrange("p (a b) -> p a b", a=16), (0, 256))])
    for og in range(4):
        i = _blk[0]
        _blk[0] += 1
        cb = i % 2
        wr = dwo.rearrange("(k p) m -> p k m", p=128)
        tl = P.dma("sync", sWbig[cb].ap(),
                   wr[:, :, og * 512:(og + 1) * 512],
                   [prev_big[cb]], sem=f"swl{cb}")
        tc = V(lambda e_, cb=cb: e_.tensor_copy(
            out=sWcol[cb].ap()[:, 0:8, :].rearrange("p a b -> p (a b)"),
            in_=sWbig[cb].ap().rearrange("p a b -> p (a b)")),
            [tl, prev_col[cb]])
        prev_big[cb] = tc
        ts_ = P.dma("gpsimd" if i % 2 == 0 else "scalar",
                    dwobf[og].rearrange("p (a b) -> p a b", a=8),
                    sWcol[cb].ap()[:, 0:8, :], [tc], sem=f"sws{cb}")
        prev_col[cb] = ts_
        t_awb[("o", og)] = [ts_]

    t_ffnw = {}
    dgr = dwg8.rearrange("j p m -> p j m")
    dur = dwu8.rearrange("j p m -> p j m")
    dgfr = dwgfb.rearrange("j p m -> p j m")
    dufr = dwufb.rearrange("j p m -> p j m")
    ddr = dwd8.rearrange("k p m -> p k m")
    ddfr = dwdfb.rearrange("k p m -> p k m")
    _fst = {"ld": None, "cast": None, "st": None}

    def _ffn_gu_step(jj):
        j = jj * 2
        tl = P.dma("sync", sW8f.ap(), dgr[:, j:j + 2, :], [_fst["cast"]],
                   sem="s8l")
        tc = GP(lambda e_: e_.tensor_copy(
            out=sW8bf.ap().rearrange("p a b -> p (a b)"),
            in_=sW8f.ap().rearrange("p a b -> p (a b)")), [tl, _fst["st"]])
        ts_ = P.dma("gpsimd", dgfr[:, j:j + 2, :], sW8bf.ap(), [tc], sem="s8s")
        tl2 = P.dma("sync", sW8f.ap(), dur[:, j:j + 2, :], [tc], sem="s8l")
        tc2 = GP(lambda e_: e_.tensor_copy(
            out=sW8bf.ap().rearrange("p a b -> p (a b)"),
            in_=sW8f.ap().rearrange("p a b -> p (a b)")), [tl2, ts_])
        ts2 = P.dma("gpsimd", dufr[:, j:j + 2, :], sW8bf.ap(), [tc2], sem="s8s")
        _fst["cast"] = tc2
        _fst["st"] = ts2
        t_ffnw[("g", j)] = ts_
        t_ffnw[("g", j + 1)] = ts_
        t_ffnw[("u", j)] = ts2
        t_ffnw[("u", j + 1)] = ts2

    def _ffn_d_step(kk2):
        k = kk2 * 2
        tl = P.dma("sync", sW8f.ap(), ddr[:, k:k + 2, :], [_fst["cast"]],
                   sem="s8l")
        tc = None
        for k2 in range(2):
            tc = GP(lambda e_, k2=k2: e_.tensor_tensor(
                out=sW8bf[:, k2, :], in0=sW8f[:, k2, :], in1=sSdnP[:],
                op=ALU.mult), [tl, _fst["st"], t_sdnb, tc])
        ts_ = P.dma("gpsimd", ddfr[:, k:k + 2, :], sW8bf.ap(), [tc], sem="s8s")
        _fst["cast"] = tc
        _fst["st"] = ts_
        t_ffnw[("d", k)] = ts_
        t_ffnw[("d", k + 1)] = ts_

    ffn_steps = ([(lambda jj=jj: _ffn_gu_step(jj)) for jj in range(NJ // 2)] +
                 [(lambda kk2=kk2: _ffn_d_step(kk2)) for kk2 in range(32)])
    ffn_steps.reverse()

    st0.close()
    P.barrier()

    if upto == "p0":
        for sem, q in P.dma_sems.items():
            if P.counts.get(sem, 0) > 0:
                P.do(q, [], waits=[(sem, P.counts[sem], "none", 16)])
        st.close()
        P.emit()
        return nc

    st1 = ExitStack()
    e1 = st1.enter_context
    sXfm = e1(nc.sbuf_tensor("sXfm", [128, NKC, TS], BF16))
    sEp = e1(nc.sbuf_tensor("sEp", [128, 4, TS], BF16))
    sEm = e1(nc.sbuf_tensor("sEm", [128, 4, TS], BF16))
    sQe = e1(nc.sbuf_tensor("sQe", [128, 4, TS], BF16))
    sKd = e1(nc.sbuf_tensor("sKd", [128, 4, TS], BF16))
    sQF = e1(nc.sbuf_tensor("sQF", [128, 16, CH], F32))
    sXf32 = e1(nc.sbuf_tensor("sXf32", [128, NKC, CH], F32))
    sEpF = e1(nc.sbuf_tensor("sEpF", [128, 4, CH], F32))
    sEmF = e1(nc.sbuf_tensor("sEmF", [128, 4, CH], F32))
    sKF = e1(nc.sbuf_tensor("sKF", [128, 16, CH], F32))
    sIdF = e1(nc.sbuf_tensor("sIdF", [128, 128], F32))
    sQd = e1(nc.sbuf_tensor("sQd", [128, 4, TS], BF16))
    sV1 = e1(nc.sbuf_tensor("sV1", [128, NCH1, DV], BF16))
    sG1 = e1(nc.sbuf_tensor("sG1", [128, NCH1, DV], BF16))
    sOl = e1(nc.sbuf_tensor("sOl", [128, NCH1, DV], BF16))
    sGb = e1(nc.sbuf_tensor("sGb", [128, TS], F32))
    sBb = e1(nc.sbuf_tensor("sBb", [128, TS], F32))
    sSig = e1(nc.sbuf_tensor("sSig", [128, 512], F32))
    sT1 = e1(nc.sbuf_tensor("sT1", [LOW, TS], BF16))
    sAT = e1(nc.sbuf_tensor("sAT", [128, 128], BF16))
    sKkf = e1(nc.sbuf_tensor("sKkf", [128, 4, 128], BF16))
    sKkT = e1(nc.sbuf_tensor("sKkT", [128, 512], BF16))
    sDiag = e1(nc.sbuf_tensor("sDiag", [128, 4, 128], BF16))
    sWqs = [e1(nc.sbuf_tensor(f"sWqs{i}", [128, 16, 128], BF16)) for i in range(2)]
    sWks = [e1(nc.sbuf_tensor(f"sWks{i}", [128, 16, 128], BF16)) for i in range(2)]
    sWvs = [e1(nc.sbuf_tensor(f"sWvs{i}", [128, 16, 256], BF16)) for i in range(1)]
    sWgs = [e1(nc.sbuf_tensor(f"sWgs{i}", [128, 16, 256], BF16)) for i in range(1)]

    tok_xfm_free = None
    tok_wq_slot = [None, None]
    tok_wk_slot = [None, None]
    tok_wv_slot = [None, None]
    tok_wg_slot = [None, None]
    tok_spill_qd = None
    tok_spill_ol = None
    tok_spill_g = None
    tok_Sevac = [None] * 16
    tok_t1 = None
    tok_prep = None
    prev_vf = None
    prev_sf = None

    _ntau, _nh, _stg = (lim + (99,))[:3] if lim else (NT1, H, 99)
    for tau in range(_ntau):
        t_xfm = []
        for fc in range(NKC):
            w = [t_xbf[tau * 8 + s_] for s_ in range(8)]
            if fc == 0:
                w = w + [tok_xfm_free]
            t_xfm.append(P.dma(
                "sync", sXfm[:, fc, :],
                dxbf[tau * TS:(tau + 1) * TS, fc * 128:(fc + 1) * 128],
                w, transpose=True, sem="sxf"))

        for half in range(2):
            tm = MM(pb[half][0:LOW, 0:512],
                    [(sGk1[:, k, :], sXfm[:, k, half * 512:(half + 1) * 512])
                     for k in range(NKC)],
                    t_xfm + [t_gk1, pb_free[half], tok_t1])
            tok_t1 = V(lambda e_, half=half: e_.tensor_copy(
                out=sT1[:, half * 512:(half + 1) * 512],
                in_=pb[half][0:LOW, 0:512]), [tm])
            pb_free[half] = tok_t1

        if tau == 0:
            tidf = V(lambda e_: e_.tensor_copy(out=sIdF[:], in_=sId[:]), [t_id])
            tok_qkF = [None] * 16
            if True:
                sXtmF = sGb.ap()[:, 0:512]
                sWqkF = sBb.ap().rearrange("p (k m) -> p k m", k=8)
                if True:
                    tprev_e = None
                    txl = None
                    for piece in range(4):
                        txl = P.dma("sync", sXtmF,
                                    dx[0:128, piece * 512:(piece + 1) * 512],
                                    [txl if piece == 0 else ttr], sem="sxfz")
                        for f2 in range(4):
                            fc = piece * 4 + f2
                            ttr = P.do("tensor",
                                       lambda e_, f2=f2: e_.transpose(
                                           pb[0][:, 0:128],
                                           sXtmF[:, f2 * 128:(f2 + 1) * 128],
                                           sIdF[:]),
                                       [txl, tidf, pb_free[0]], inc="pe")
                            tev = V(lambda e_, fc=fc: e_.tensor_copy(
                                out=sXf32[:, fc, :], in_=pb[0][:, 0:128]),
                                [ttr, tprev_e])
                            pb_free[0] = tev
                            tprev_e = tev
                if True:
                    twf = None
                    for (wsrc, dstT, scl) in ((dwq, sQF, SCALE), (dwk, sKF, None)):
                        for hd in range(16):
                            tmf = None
                            for kh in range(2):
                                twf = P.dma(
                                    "sync", sWqkF,
                                    wsrc.rearrange("(k p) m -> p k m", p=128)
                                    [:, kh * 8:(kh + 1) * 8,
                                     hd * 128:(hd + 1) * 128],
                                    [twf], sem="swfz")
                                tmf = P.do("tensor", [
                                    (lambda e_, k=k, kh=kh, s_=(kh == 0 and k == 0),
                                     t_=(kh == 1 and k == 7):
                                     e_.matmul(pb[1][:, 0:CH], sWqkF[:, k, :],
                                               sXf32[:, kh * 8 + k, :],
                                               start=s_, stop=t_))
                                    for k in range(8)],
                                    [twf, tprev_e, pb_free[1] if kh == 0 else None],
                                    inc="pe")
                                twf = tmf
                            if scl is None:
                                tevf = V(lambda e_, hd=hd, dstT=dstT: e_.tensor_copy(
                                    out=dstT[:, hd, :], in_=pb[1][:, 0:CH]),
                                    [tmf, tok_qkF[hd]])
                            else:
                                tevf = V(lambda e_, hd=hd, dstT=dstT, scl=scl:
                                         e_.tensor_scalar(
                                    out=dstT[:, hd, :], in0=pb[1][:, 0:CH],
                                    scalar1=scl, scalar2=None, op0=ALU.mult),
                                    [tmf, tok_qkF[hd]])
                            pb_free[1] = tevf
                            tok_qkF[hd] = tevf
            tok_prep = tok_qkF[15]

        if tau == 0 and "p1" in dump:
            P.dma("gpsimd", dumpten("xf32", [128, NKC, CH]), sXf32.ap(),
                  [tprev_e], sem="sdmp")
            P.dma("gpsimd", dumpten("qFr", [128, 16, CH]), sQF.ap(),
                  [tok_qkF[hd] for hd in range(16)], sem="sdmp")

        for h in range(_nh):
            hd0 = h * 4

            tws_q = [None] * 4
            tws_k = [None] * 4
            for d in (0, 1):
                tws_q[d] = P.dma("sync", sWqs[d % 2].ap(), dwqbf[h, d],
                                 [tok_wq_slot[d % 2]] + t_awb[("q", h)],
                                 sem=f"swq{d % 2}")
                tws_k[d] = P.dma("sync", sWks[d % 2].ap(), dwkbf[h, d],
                                 [tok_wk_slot[d % 2]] + t_awb[("k", h)],
                                 sem=f"swk{d % 2}")
            slvg = 0
            tws_v = P.dma("sync", sWvs[slvg].ap(), dwvbf[h],
                          [tok_wv_slot[slvg]] + t_awb[("v", h)], sem=f"swv{slvg}")
            tws_g = P.dma("sync", sWgs[slvg].ap(), dwgbf[h],
                          [tok_wg_slot[slvg]] + t_awb[("g", h)], sem=f"swg{slvg}")

            if _stg < 1:
                continue
            tok_ep = {}
            tok_em = {}
            tok_ebl = {}
            tok_bl = {}
            for d in range(4):
                hd = hd0 + d
                tg = []
                for half in range(2):
                    bank = 2 + half
                    tm = MM(pb[bank][:, 0:512],
                            [(sGk2[:, hd * 128:(hd + 1) * 128],
                              sT1[:, half * 512:(half + 1) * 512])],
                            [tok_t1, t_gk2, pb_free[bank]])
                    tsg = SC(lambda e_, bank=bank, hd=hd: e_.activation(
                        out=sSig[:], in_=pb[bank][:, 0:512], func=AF.Sigmoid,
                        bias=sBgk2[:, hd:hd + 1], scale=1.0),
                        [tm, t_bgk2, prev_vf, prev_sf])
                    pb_free[bank] = tsg
                    tln = SC(lambda e_, half=half: e_.activation(
                        out=sGb[:, half * 512:(half + 1) * 512], in_=sSig[:],
                        func=AF.Ln),
                        [tsg, tok_prep if tau == 0 else None])
                    tg.append(tln)
                tsc = []
                for c in range(NCH1):
                    tsc.append(Vs(lambda e_, c=c: e_.tensor_tensor_scan(
                        out=sBb[:, c * 128:(c + 1) * 128],
                        data0=sGb[:, c * 128:(c + 1) * 128],
                        data1=sZb[:], initial=0.0, op0=ALU.add, op1=ALU.add),
                        [tg[c // 4], t_zb] + ([tsc[-1]] if tsc else [])))
                tok_ep[d] = SC(lambda e_, d=d: e_.activation(
                    out=sEp[:, d, :], in_=sBb[:], func=AF.Exp, scale=INV16), tsc)
                tok_em[d] = SC(lambda e_, d=d: e_.activation(
                    out=sEm[:, d, :], in_=sBb[:], func=AF.Exp, scale=-INV16), tsc)
                if tau == 0:
                    tok_epF = SC(lambda e_, d=d: e_.activation(
                        out=sEpF[:, d, :], in_=sBb[:, 0:CH], func=AF.Exp,
                        scale=INV16), tsc)
                    tok_emF = SC(lambda e_, d=d: e_.activation(
                        out=sEmF[:, d, :], in_=sBb[:, 0:CH], func=AF.Exp,
                        scale=-INV16), tsc)
                tbl = Vs(lambda e_, hd=hd: e_.tensor_copy(
                    out=sBl[:, hd, :], in_=sBb[:, 127::128]), tsc)
                tok_ebl[d] = SC(lambda e_, hd=hd: e_.activation(
                    out=sEbl[:, hd, :], in_=sBl[:, hd, :], func=AF.Exp,
                    scale=INV16), [tbl])
                tok_bl[d] = tbl

            blsl = sBl.ap()[:, hd0:hd0 + 4, :]
            w0 = [tok_bl[d] for d in range(4)] + [t_prev]
            tI = V(lambda e_, blsl=blsl: e_.tensor_copy(
                out=sI1.ap()[:, :, 0:1], in_=blsl[:, :, 0:1]), w0)
            tI = V(lambda e_, blsl=blsl: e_.tensor_add(
                out=sI1.ap()[:, :, 1:8], in0=blsl[:, :, 1:8],
                in1=blsl[:, :, 0:7]), w0 + [tI])
            tI = V(lambda e_: e_.tensor_copy(
                out=sI2.ap()[:, :, 0:2], in_=sI1.ap()[:, :, 0:2]), [tI])
            tI = V(lambda e_: e_.tensor_add(
                out=sI2.ap()[:, :, 2:8], in0=sI1.ap()[:, :, 2:8],
                in1=sI1.ap()[:, :, 0:6]), [tI])
            tI = V(lambda e_: e_.tensor_copy(
                out=sI1.ap()[:, :, 0:4], in_=sI2.ap()[:, :, 0:4]), [tI])
            tI = V(lambda e_: e_.tensor_add(
                out=sI1.ap()[:, :, 4:8], in0=sI2.ap()[:, :, 4:8],
                in1=sI2.ap()[:, :, 0:4]), [tI])
            tI = V(lambda e_, blsl=blsl: e_.tensor_sub(
                out=sI2.ap().rearrange("p a b -> p (a b)"),
                in0=sI1.ap().rearrange("p a b -> p (a b)"),
                in1=blsl.rearrange("p a b -> p (a b)")), [tI])
            tprevn = []
            for dd in range(4):
                hd = hd0 + dd
                tI2 = V(lambda e_, dd=dd, hd=hd: e_.tensor_scalar(
                    out=sI2.ap()[:, dd, :], in0=sI2.ap()[:, dd, :],
                    scalar1=sPrev[:, hd:hd + 1], scalar2=None, op0=ALU.add),
                    [tI])
                tprevn.append((hd, tI2))
            tok_pexp_h = SC(lambda e_, hd0=hd0: e_.activation(
                out=sPexp.ap()[:, hd0:hd0 + 4, :],
                in_=sI2.ap().rearrange("p a b -> p (a b)"),
                func=AF.Exp, scale=INV16), [t for _, t in tprevn])
            for dd in range(4):
                hd = hd0 + dd
                t_prev = V(lambda e_, dd=dd, hd=hd: e_.tensor_add(
                    out=sPrev[:, hd:hd + 1], in0=sI2.ap()[:, dd, 7:8],
                    in1=sBl[:, hd, 7:8]),
                    [tprevn[dd][1], tok_pexp_h, t_prev])

            if tau == 0:
                for d in range(4):
                    hd = hd0 + d
                    tok_qkF[hd] = V(lambda e_, hd=hd, d=d: e_.tensor_tensor(
                        out=sQF[:, hd, :], in0=sQF[:, hd, :],
                        in1=sEpF[:, d, :], op=ALU.mult),
                        [tok_qkF[hd], tok_epF])
                    tok_qkF[hd] = V(lambda e_, hd=hd, d=d: e_.tensor_tensor(
                        out=sKF[:, hd, :], in0=sKF[:, hd, :],
                        in1=sEmF[:, d, :], op=ALU.mult),
                        [tok_qkF[hd], tok_emF])

            if _stg < 2:
                continue
            tok_qe = {}
            tok_kd = {}
            last_qd = []
            for d in range(4):
                sl = d % 2
                for half in range(2):
                    bank = (d * 2 + half) % 2
                    tm = MM(pb[bank][:, 0:512],
                            [(sWqs[sl][:, k, :],
                              sXfm[:, k, half * 512:(half + 1) * 512])
                             for k in range(NKC)],
                            [tws_q[d], pb_free[bank]])
                    te = V(lambda e_, bank=bank, d=d, half=half: e_.tensor_tensor(
                        out=sQe[:, d, half * 512:(half + 1) * 512],
                        in0=pb[bank][:, 0:512],
                        in1=sEp[:, d, half * 512:(half + 1) * 512], op=ALU.mult),
                        [tm, tok_ep[d]])
                    tok_qe[(d, half)] = te
                    pb_free[bank] = te
                    for cc_ in range(4):
                        c = half * 4 + cc_
                        last_qd.append(V(lambda e_, d=d, c=c, hd=hd0 + d:
                                         e_.tensor_scalar(
                            out=sQd[:, d, c * 128:(c + 1) * 128],
                            in0=sQe[:, d, c * 128:(c + 1) * 128],
                            scalar1=sPexp[:, hd, c:c + 1],
                            scalar2=None, op0=ALU.mult),
                            [te, tok_pexp_h, tok_spill_qd]))
                    bank2 = 2 + (d * 2 + half) % 2
                    tm2 = MM(pb[bank2][:, 0:512],
                             [(sWks[sl][:, k, :],
                               sXfm[:, k, half * 512:(half + 1) * 512])
                              for k in range(NKC)],
                             [tws_k[d], pb_free[bank2]])
                    te2 = V(lambda e_, bank2=bank2, d=d, half=half: e_.tensor_tensor(
                        out=sKd[:, d, half * 512:(half + 1) * 512],
                        in0=pb[bank2][:, 0:512],
                        in1=sEm[:, d, half * 512:(half + 1) * 512], op=ALU.mult),
                        [tm2, tok_em[d]])
                    tok_kd[(d, half)] = te2
                    pb_free[bank2] = te2
                if d + 2 < 4:
                    tws_q[d + 2] = P.dma("sync", sWqs[sl].ap(), dwqbf[h, d + 2],
                                         [tm] + t_awb[("q", h)], sem=f"swq{sl}")
                    tws_k[d + 2] = P.dma("sync", sWks[sl].ap(), dwkbf[h, d + 2],
                                         [tm2] + t_awb[("k", h)], sem=f"swk{sl}")
            tok_wq_slot[0] = tok_qe[(2, 1)]
            tok_wq_slot[1] = tok_qe[(3, 1)]
            tok_wk_slot[0] = tok_kd[(2, 1)]
            tok_wk_slot[1] = tok_kd[(3, 1)]

            tok_spill_qd = P.dma("gpsimd", dqd[tau, h],
                                 sQd.ap().rearrange("p a b -> p (a b)"), last_qd,
                                 sem="sqd")

            if _stg < 3:
                continue
            tok_v = {}
            tok_gp = {}
            for c in range(NCH1):
                bank = c % 2
                tm = MM(pb[bank][:, 0:256],
                        [(sXfm[:, k, c * 128:(c + 1) * 128], sWvs[slvg][:, k, :])
                         for k in range(NKC)],
                        [tws_v, pb_free[bank]])
                tok_v[c] = SC(lambda e_, bank=bank, c=c: e_.copy(
                    out=sV1[:, c, :], in_=pb[bank][:, 0:256]), [tm])
                pb_free[bank] = tok_v[c]
                bank2 = 2 + c % 2
                tm2 = MM(pb[bank2][:, 0:256],
                         [(sXfm[:, k, c * 128:(c + 1) * 128], sWgs[slvg][:, k, :])
                          for k in range(NKC)],
                         [tws_g, pb_free[bank2]])
                tok_gp[c] = SC(lambda e_, bank2=bank2, c=c: e_.copy(
                    out=sG1[:, c, :], in_=pb[bank2][:, 0:256]),
                    [tm2, tok_spill_g])
                pb_free[bank2] = tok_gp[c]
                if c == NCH1 - 1:
                    tok_wv_slot[slvg] = tok_v[c]
                    tok_wg_slot[slvg] = tok_gp[c]
                    last_vg_mm = tm2
            tok_spill_g = P.dma("gpsimd", dgs[tau, h],
                                sG1.ap().rearrange("p a b -> p (a b)"),
                                [tok_gp[c] for c in range(NCH1)], sem="sgl")

            if _stg < 4:
                P.fence("vector"); P.fence("scalar")
                continue
            tok_olc = []
            tok_kkT_evac = None
            tok_Ard = None
            for c in range(NCH1):
                if tau == 0 and c == 0:
                    tA = MM(pb[2][:, 0:128],
                            [(sKF[:, hd0 + d, :], sQF[:, hd0 + d, :])
                             for d in range(4)],
                            [tok_qkF[hd0 + d] for d in range(4)] + [pb_free[2]])
                else:
                    tA = MM(pb[2][:, 0:128],
                            [(sKd[:, d, c * 128:(c + 1) * 128],
                              sQe[:, d, c * 128:(c + 1) * 128]) for d in range(4)],
                            [tok_kd[(d, c // 4)] for d in range(4)] +
                            [tok_qe[(d, c // 4)] for d in range(4)] + [pb_free[2]])
                tAm = V(lambda e_: e_.tensor_tensor(
                    out=sAT[:], in0=pb[2][:, 0:128], in1=sMask[:], op=ALU.mult),
                    [tA, t_mask, tok_Ard])
                pb_free[2] = tAm
                if _stg < 5:
                    continue
                tO = MM(pb[3][:, 0:256],
                        [(sAT[:], sV1[:, c, :])] +
                        [(sQe[:, d, c * 128:(c + 1) * 128], sS[:, hd0 + d, :])
                         for d in range(4)],
                        [tAm, tok_v[c], pb_free[3], t_s0] +
                        [tok_Sevac[hd0 + d] for d in range(4)])
                tOe = SC(lambda e_, c=c: e_.copy(
                    out=sOl[:, c, :], in_=pb[3][:, 0:256]),
                    [tO, tok_spill_ol])
                pb_free[3] = tOe
                tok_olc.append(tOe)
                tok_Ard = tO
                if _stg < 6:
                    continue
                tkk = []
                tdg = []
                for d in range(4):
                    tkk.append(V(lambda e_, d=d, c=c, hd=hd0 + d: e_.tensor_scalar(
                        out=sKkf[:, d, :], in0=sKd[:, d, c * 128:(c + 1) * 128],
                        scalar1=sEbl[:, hd, c:c + 1], scalar2=None, op0=ALU.mult),
                        [tok_kd[(d, c // 4)], tok_ebl[d], tok_kkT_evac]))
                    tdg.append(V(lambda e_, d=d, c=c, hd=hd0 + d: e_.tensor_scalar(
                        out=sDiag[:, d, :], in0=sId[:],
                        scalar1=sEbl[:, hd, c:c + 1], scalar2=None, op0=ALU.mult),
                        [t_id, tok_ebl[d]]))
                pbkk = pb[6].ap().bitcast(BF16)
                ttr = []
                for d in range(4):
                    ttr.append(TR(pbkk[:, d * 128:(d + 1) * 128], sKkf[:, d, :],
                                  [tkk[d], pb_free[6]]))
                tok_kkT_evac = V(lambda e_: e_.tensor_copy(
                    out=sKkT[:], in_=pbkk[:, 0:512]), [ttr[-1]])
                pb_free[6] = tok_kkT_evac
                if _stg < 7:
                    continue
                smm = []
                for d in range(4):
                    bank = 4 + d // 2
                    off = (d % 2) * 256
                    smm.append(MM(
                        pb[bank][:, off:off + 256],
                        [(sDiag[:, d, :], sS[:, hd0 + d, :]),
                         (sKkT[:, d * 128:(d + 1) * 128], sV1[:, c, :])],
                        [tdg[d], tok_kkT_evac, tO, tok_Sevac[hd0 + d],
                         pb_free[bank]]))
                for d in range(4):
                    bank = 4 + d // 2
                    off = (d % 2) * 256
                    if d < 2:
                        tok_Sevac[hd0 + d] = V(
                            lambda e_, bank=bank, off=off, hd=hd0 + d:
                            e_.tensor_copy(out=sS[:, hd, :],
                                           in_=pb[bank][:, off:off + 256]),
                            [smm[0], smm[1]])
                    else:
                        tok_Sevac[hd0 + d] = SC(
                            lambda e_, bank=bank, off=off, hd=hd0 + d:
                            e_.copy(out=sS[:, hd, :],
                                    in_=pb[bank][:, off:off + 256]),
                            [smm[2], smm[3]])
                pb_free[4] = tok_Sevac[hd0 + 1]
                pb_free[5] = tok_Sevac[hd0 + 3]

            tok_spill_ol = P.dma("gpsimd", dol[tau, h],
                                 sOl.ap().rearrange("p a b -> p (a b)"), tok_olc,
                                 sem="sol")

            prev_vf = P.fence("vector")
            prev_sf = P.fence("scalar")
            for _ in range(12):
                if ffn_steps:
                    ffn_steps.pop()()
            if h == H - 1:
                tok_xfm_free = last_vg_mm

        if "p1" in dump and tau == 0:
            P.dma("gpsimd", dumpten("qF", [128, 16, CH]), sQF.ap(),
                  [tok_qkF[hd] for hd in range(16)], sem="sdmp")
            P.dma("gpsimd", dumpten("kF", [128, 16, CH]), sKF.ap(),
                  [tok_qkF[hd] for hd in range(16)], sem="sdmp")
            wfin = [prev_vf, prev_sf, tok_spill_qd, tok_spill_ol, tok_spill_g]
            P.dma("gpsimd", dumpten("qe", [128, 4, TS], BF16), sQe.ap(), wfin, sem="sdmp")
            P.dma("gpsimd", dumpten("kd", [128, 4, TS], BF16), sKd.ap(), wfin, sem="sdmp")
            P.dma("gpsimd", dumpten("v1", [128, NCH1, DV], BF16), sV1.ap(), wfin, sem="sdmp")
            P.dma("gpsimd", dumpten("g1", [128, NCH1, DV], BF16), sG1.ap(), wfin, sem="sdmp")
            P.dma("gpsimd", dumpten("ol", [128, NCH1, DV], BF16), sOl.ap(), wfin, sem="sdmp")
            P.dma("gpsimd", dumpten("qd", [128, 4, TS], BF16), sQd.ap(), wfin, sem="sdmp")
            P.dma("gpsimd", dumpten("S1", [128, 16, DV], BF16), sS.ap(), wfin, sem="sdmp")
            P.dma("gpsimd", dumpten("prev", [128, 16]), sPrev[:], wfin, sem="sdmp")
            P.dma("gpsimd", dumpten("ebl", [128, 16, NCH1]), sEbl.ap(), wfin, sem="sdmp")
            P.dma("gpsimd", dumpten("pexp", [128, 16, NCH1]), sPexp.ap(), wfin, sem="sdmp")
            P.dma("gpsimd", dumpten("bl", [128, 16, NCH1]), sBl.ap(), wfin, sem="sdmp")
            P.dma("gpsimd", dumpten("bb", [128, TS]), sBb[:], wfin, sem="sdmp")

    while ffn_steps:
        ffn_steps.pop()()
    last_S = list(tok_Sevac)
    t_cca = P.dma("gpsimd", dccinA, sPrev[:], [t_prev], sem="scc")
    t_ccz = P.dma("gpsimd", dccinZ,
                  sS.ap().rearrange("p a b -> p (a b)"), last_S + [t_cca],
                  sem="scc")

    if "z" in dump:
        P.dma("gpsimd", dumpten("A", [128, 16]), sPrev[:], [t_prev], sem="sdmp")
        P.dma("gpsimd", dumpten("Z", [128, 16, DV], BF16), sS.ap(), last_S, sem="sdmp")

    if upto == "p1":
        for sem, q in P.dma_sems.items():
            if P.counts.get(sem, 0) > 0:
                P.do(q, [], waits=[(sem, P.counts[sem], "none", 16)])
        st1.close()
        st.close()
        P.emit()
        return nc

    t_cc = P.do("gpsimd",
                lambda e_: e_.collective_compute(
                    "AllGather", ALU.bypass,
                    replica_groups=[[0, 1, 2, 3], [4, 5, 6, 7]],
                    ins=[dccinA.opt()], outs=[dccoutA.opt()]),
                [t_cca], inc="cc")
    t_cc = P.do("gpsimd",
                lambda e_: e_.collective_compute(
                    "AllGather", ALU.bypass,
                    replica_groups=[[0, 1, 2, 3], [4, 5, 6, 7]],
                    ins=[dccinZ.opt()], outs=[dccoutZ.opt()]),
                [t_ccz, t_cc], inc="cc")

    st1.close()
    P.barrier()

    stset = ExitStack()
    es = stset.enter_context
    sAga = es(nc.sbuf_tensor("sAga", [128, 4, 16], F32))
    sC = es(nc.sbuf_tensor("sC", [128, 3, 16], F32))
    sD = es(nc.sbuf_tensor("sD", [128, 16], F32))
    sDmc = es(nc.sbuf_tensor("sDmc", [128, 16], F32))
    sCoef = es(nc.sbuf_tensor("sCoef", [128, 3, 16], F32))
    sZld = [es(nc.sbuf_tensor(f"sZld{i}", [128, 4096], BF16)) for i in range(2)]

    t_aga = P.dma("gpsimd", sAga[:],
                  dccoutA.rearrange("(r p) m -> p r m", p=128), [t_cc], sem="sag")
    t_c = V(lambda e_: e_.tensor_copy(out=sC[:, 0, :], in_=sAga[:, 0, :]), [t_aga])
    t_c = V(lambda e_: e_.tensor_add(out=sC[:, 1, :], in0=sC[:, 0, :],
                                     in1=sAga[:, 1, :]), [t_c])
    t_c = V(lambda e_: e_.tensor_add(out=sC[:, 2, :], in0=sC[:, 1, :],
                                     in1=sAga[:, 2, :]), [t_c])
    t_d = V(lambda e_: e_.tensor_scalar(
        out=sD[:], in0=sAga[:, 0, :], scalar1=sSel01[:, 0:1], scalar2=None,
        op0=ALU.mult), [t_aga, t_sel])
    for i in (1, 2):
        t_d = V(lambda e_, i=i: e_.scalar_tensor_tensor(
            out=sD[:], in0=sAga[:, i, :], scalar=sSel01[:, i:i + 1], in1=sD[:],
            op0=ALU.mult, op1=ALU.add), [t_d])
    t_coef = []
    for i in range(3):
        tdm = V(lambda e_, i=i: e_.tensor_sub(
            out=sDmc[:], in0=sD[:], in1=sC[:, i, :]),
            [t_d, t_c, t_coef[-1] if t_coef else None])
        tcp = SC(lambda e_, i=i: e_.activation(
            out=sCoef[:, i, :], in_=sDmc[:], func=AF.Exp,
            scale=sSelsc[:, i:i + 1], bias=sSelbi[:, i:i + 1]),
            [tdm, t_selsc, t_selbi])
        t_coef.append(V(lambda e_, i=i: e_.tensor_scalar(
            out=sCoef[:, i, :], in0=sCoef[:, i, :],
            scalar1=sSel01[:, i:i + 1], scalar2=None, op0=ALU.mult), [tcp]))

    t_sacc = V(lambda e_: e_.memset(sS.ap().rearrange("p a b -> p (a b)"), 0.0),
               [t_cc])
    t_zld = [None, None]
    for i in range(3):
        b = i % 2
        t_zld[b] = P.dma("gpsimd", sZld[b][:],
                         dccoutZ[i * 128:(i + 1) * 128, :],
                         [t_cc, t_zld[b], t_sacc if i < 2 else None],
                         sem=f"sz{b}")
        for hd in range(16):
            t_sacc = V(lambda e_, i=i, hd=hd, b=b: e_.scalar_tensor_tensor(
                out=sS[:, hd, :], in0=sZld[b][:, hd * 256:(hd + 1) * 256],
                scalar=sCoef[:, i, hd:hd + 1], in1=sS[:, hd, :],
                op0=ALU.mult, op1=ALU.add),
                [t_zld[b], t_coef[i], t_sacc])
    tok_Sin = t_sacc

    if "sin" in dump:
        P.dma("gpsimd", dumpten("Sin", [128, 16, DV], BF16), sS.ap(), [tok_Sin], sem="sdmp")
    if upto == "cc":
        for sem, q in P.dma_sems.items():
            if P.counts.get(sem, 0) > 0:
                P.do(q, [], waits=[(sem, P.counts[sem], "none", 16)])
        stset.close()
        st.close()
        P.emit()
        return nc

    stset.close()
    P.barrier()

    st2 = ExitStack()
    e2 = st2.enter_context
    sX1tm = e2(nc.sbuf_tensor("sX1tm", [128, 4, D], BF16))
    sX1fm = e2(nc.sbuf_tensor("sX1fm", [128, 16, TF], BF16))
    sR2 = e2(nc.sbuf_tensor("sR2", [128, 4, D], BF16))
    sSqD = e2(nc.sbuf_tensor("sSqD", [128, D], BF16))
    sSsq = e2(nc.sbuf_tensor("sSsq", [128, 16], F32))
    sRstd = e2(nc.sbuf_tensor("sRstd", [128, 16], F32))
    sVar = e2(nc.sbuf_tensor("sVar", [128, 16], F32))
    sSd = e2(nc.sbuf_tensor("sSd", [128, 16], F32))
    sLNa = e2(nc.sbuf_tensor("sLNa", [128, 4, 4], F32))
    sLNs = e2(nc.sbuf_tensor("sLNs", [128, 4], F32))
    sLNq = e2(nc.sbuf_tensor("sLNq", [128, 4], F32))
    sLNm = e2(nc.sbuf_tensor("sLNm", [128, 4], F32))
    sLNr = e2(nc.sbuf_tensor("sLNr", [128, 4], F32))
    sLNt = e2(nc.sbuf_tensor("sLNt", [128, 4], F32))
    sLNv = e2(nc.sbuf_tensor("sLNv", [128, 4], F32))
    sWgj = [e2(nc.sbuf_tensor(f"sWgj{i}", [128, 16, 128], BF16)) for i in range(2)]
    sWuj = [e2(nc.sbuf_tensor(f"sWuj{i}", [128, 16, 128], BF16)) for i in range(2)]
    sWdk = [e2(nc.sbuf_tensor(f"sWdk{i}", [128, 1024], BF16)) for i in range(2)]
    sOut = [e2(nc.sbuf_tensor(f"sOut{i}", [128, D], F32)) for i in range(2)]

    slot_wgj = [None, None]
    slot_wuj = [None, None]
    slot_wdk = [None, None]
    slot_out = [None, None]
    tok_prevB = []

    for t2 in range(NT2):
        tau = t2 // 2
        lc0 = (t2 % 2) * 4
        tk0 = t2 * TF

        stA = ExitStack()
        eA = stA.enter_context
        sQds = [eA(nc.sbuf_tensor(f"sQds{i}_{t2}", [128, 4, TF], BF16)) for i in range(2)]
        sOls = [eA(nc.sbuf_tensor(f"sOls{i}_{t2}", [128, 4, DV], BF16)) for i in range(2)]
        sGs = [eA(nc.sbuf_tensor(f"sGs{i}_{t2}", [128, 4, DV], BF16)) for i in range(2)]
        sOfull = eA(nc.sbuf_tensor(f"sOfull_{t2}", [128, 16, DV], BF16))
        sSgw = eA(nc.sbuf_tensor(f"sSgw_{t2}", [128, 16, DV], BF16))
        sTmp = eA(nc.sbuf_tensor(f"sTmp_{t2}", [128, DV], BF16))
        sOtm = eA(nc.sbuf_tensor(f"sOtm_{t2}", [128, 4, VD], BF16))
        sOfm = eA(nc.sbuf_tensor(f"sOfm_{t2}", [128, 8, TF], BF16))
        sXtm2 = eA(nc.sbuf_tensor(f"sXtm2_{t2}", [128, 4, D], BF16))
        sWos = [eA(nc.sbuf_tensor(f"sWos{i}_{t2}", [128, 8, 512], BF16)) for i in range(2)]

        slot_qd2 = [None, None]
        slot_ol2 = [None, None]
        slot_g2 = [None, None]
        t_sq = []
        t_ofull = {}
        t_sgw = {}
        for h in range(H):
            b = h % 2
            tqd = P.dma("gpsimd", sQds[b].ap(),
                        dqd[tau, h][:, :, (t2 % 2) * TF:(t2 % 2) * TF + TF],
                        [slot_qd2[b], tok_spill_qd] + tok_prevB, sem=f"rq{b}")
            tol = P.dma("gpsimd", sOls[b].ap(), dol[tau, h][:, lc0:lc0 + 4, :],
                        [slot_ol2[b], tok_spill_ol] + tok_prevB, sem=f"ro{b}")
            tgl = P.dma("gpsimd", sGs[b].ap(), dgs[tau, h][:, lc0:lc0 + 4, :],
                        [slot_g2[b], tok_spill_g] + tok_prevB, sem=f"rg{b}")
            last_corr = None
            for tc in range(4):
                i = h * 4 + tc
                bank = tc % 2
                tm = MM(pb[bank][:, 0:256],
                        [(sQds[b][:, d, tc * 128:(tc + 1) * 128],
                          sS[:, h * 4 + d, :]) for d in range(4)],
                        [tqd, tok_Sin, pb_free[bank]])
                last_corr = tm
                tof = V(lambda e_, bank=bank, i=i, b=b, tc=tc:
                        e_.scalar_tensor_tensor(
                    out=sOfull[:, i, :], in0=pb[bank][:, 0:256], scalar=0.0,
                    in1=sOls[b][:, tc, :], op0=ALU.bypass, op1=ALU.add),
                    [tm, tol])
                pb_free[bank] = tof
                t_ofull[i] = tof
                t_sq.append(SC(lambda e_, i=i: e_.activation(
                    out=sSqD[:, 0:DV], in_=sOfull[:, i, :], func=AF.Square,
                    accum_out=sSsq[:, i:i + 1]), [tof]))
                tsg = SC(lambda e_, i=i, b=b, tc=tc: e_.activation(
                    out=sSgw[:, i, :], in_=sGs[b][:, tc, :], func=AF.Silu),
                    [tgl])
                t_sgw[i] = V(lambda e_, i=i: e_.tensor_tensor(
                    out=sSgw[:, i, :], in0=sSgw[:, i, :], in1=sWnB[:],
                    op=ALU.mult), [tsg, t_wnb])
            slot_qd2[b] = last_corr
            slot_ol2[b] = t_ofull[h * 4 + 3]
            slot_g2[b] = t_sgw[h * 4 + 3]

        tv_ = V(lambda e_: e_.tensor_scalar(
            out=sVar[:], in0=sSsq[:], scalar1=1.0 / DV, scalar2=None,
            op0=ALU.mult), t_sq)
        ts_ = SC(lambda e_: e_.activation(out=sSd[:], in_=sVar[:],
                                          func=AF.Ln, bias=sEps[:, 0:1], scale=1.0),
                 [tv_, t_eps])
        t_rstd = SC(lambda e_: e_.activation(out=sRstd[:], in_=sSd[:],
                                             func=AF.Exp, scale=-0.5), [ts_])

        t_otm = {}
        prev_otm = None
        for h in range(H):
            for tc in range(4):
                i = h * 4 + tc
                tt = V(lambda e_, i=i: e_.tensor_scalar(
                    out=sTmp[:], in0=sOfull[:, i, :], scalar1=sRstd[:, i:i + 1],
                    scalar2=None, op0=ALU.mult), [t_rstd, t_ofull[i], prev_otm])
                t_otm[(tc, h)] = V(lambda e_, i=i, tc=tc, h=h: e_.tensor_tensor(
                    out=sOtm[:, tc, h * DV:(h + 1) * DV], in0=sTmp[:],
                    in1=sSgw[:, i, :], op=ALU.mult), [tt, t_sgw[i]])
                prev_otm = t_otm[(tc, h)]

        t_ofm = {}
        for tc in range(4):
            pbk = pb[6 + tc % 2].ap().bitcast(BF16)
            ttr = []
            for v in range(8):
                ttr.append(TR(pbk[:, v * 128:(v + 1) * 128],
                              sOtm[:, tc, v * 128:(v + 1) * 128],
                              [t_otm[(tc, hh)] for hh in range(H)] +
                              [pb_free[6 + tc % 2]]))
            t_ofm[tc] = V(lambda e_, tc=tc, pbk=pbk: e_.tensor_copy(
                out=sOfm.ap()[:, 0:8, tc * 128:(tc + 1) * 128],
                in_=pbk[:, 0:1024].rearrange("p (a b) -> p a b", a=8)),
                [ttr[-1]])
            pb_free[6 + tc % 2] = t_ofm[tc]

        t_xtm = []
        for tc in range(4):
            t_xtm.append(P.dma(
                "sync", sXtm2[:, tc, :],
                dxbf[tk0 + tc * 128: tk0 + (tc + 1) * 128, :],
                [t_xbf[(tk0 + tc * 128) // 128]] + tok_prevB, sem="rxt"))

        t_x1 = {}
        slot_wo = [None, None]
        for og in range(4):
            b = og % 2
            two = P.dma("sync", sWos[b].ap(), dwobf[og],
                        [slot_wo[b]] + t_awb[("o", og)] + tok_prevB,
                        sem=f"rwo{b}")
            last_womm = None
            for tc in range(4):
                bank = (og * 4 + tc) % 4
                tm = MM(pb[bank][:, 0:512],
                        [(sOfm[:, k, tc * 128:(tc + 1) * 128], sWos[b][:, k, :])
                         for k in range(8)],
                        [t_ofm[tc], two, pb_free[bank]])
                last_womm = tm
                te = V(lambda e_, bank=bank, tc=tc, og=og: e_.scalar_tensor_tensor(
                    out=sX1tm[:, tc, og * 512:(og + 1) * 512],
                    in0=pb[bank][:, 0:512], scalar=0.0,
                    in1=sXtm2[:, tc, og * 512:(og + 1) * 512],
                    op0=ALU.bypass, op1=ALU.add,
                    accum_out=sLNa[:, tc, og:og + 1]),
                    [tm] + t_xtm)
                pb_free[bank] = te
                t_x1[(og, tc)] = te
            slot_wo[b] = last_womm

        t_ln1 = []
        for tc in range(4):
            tsum = V(lambda e_, tc=tc: e_.reduce_sum(
                out=sLNs[:, tc:tc + 1], in_=sLNa[:, tc, :], axis=AXX),
                [t_x1[(og2, tc)] for og2 in range(4)])
            tssq = SC(lambda e_, tc=tc: e_.activation(
                out=sSqD[:], in_=sX1tm[:, tc, :], func=AF.Square,
                accum_out=sLNq[:, tc:tc + 1]),
                [t_x1[(og2, tc)] for og2 in range(4)] + [t_sq[-1]])
            tm_ = V(lambda e_, tc=tc: e_.tensor_scalar(
                out=sLNm[:, tc:tc + 1], in0=sLNs[:, tc:tc + 1],
                scalar1=1.0 / D, scalar2=None, op0=ALU.mult), [tsum])
            tms = V(lambda e_, tc=tc: e_.tensor_mul(
                out=sLNt[:, tc:tc + 1], in0=sLNm[:, tc:tc + 1],
                in1=sLNm[:, tc:tc + 1]), [tm_])
            tvq = V(lambda e_, tc=tc: e_.tensor_scalar(
                out=sLNv[:, tc:tc + 1], in0=sLNq[:, tc:tc + 1],
                scalar1=1.0 / D, scalar2=None, op0=ALU.mult), [tssq])
            tvar = V(lambda e_, tc=tc: e_.tensor_sub(
                out=sLNr[:, tc:tc + 1], in0=sLNv[:, tc:tc + 1],
                in1=sLNt[:, tc:tc + 1]), [tvq, tms])
            tsq2 = SC(lambda e_, tc=tc: e_.activation(
                out=sLNt[:, tc:tc + 1], in_=sLNr[:, tc:tc + 1], func=AF.Ln,
                bias=sEps[:, 0:1], scale=1.0), [tvar, t_eps])
            trc = SC(lambda e_, tc=tc: e_.activation(
                out=sLNr[:, tc:tc + 1], in_=sLNt[:, tc:tc + 1], func=AF.Exp,
                scale=-0.5), [tsq2])
            t_ln1.append(V(lambda e_, tc=tc: e_.tensor_scalar(
                out=sX1tm[:, tc, :], in0=sX1tm[:, tc, :],
                scalar1=sLNm[:, tc:tc + 1], scalar2=sLNr[:, tc:tc + 1],
                op0=ALU.subtract, op1=ALU.mult), [trc]))

        if "x1" in dump and t2 == 0:
            P.dma("gpsimd", dumpten("wnb", [128, DV], BF16), sWnB[:], [t_wnb], sem="sdmp")
            P.dma("gpsimd", dumpten("ols1", [128, 4, DV], BF16), sOls[1].ap(),
                  list(t_ofull.values()), sem="sdmp")
            P.dma("gpsimd", dumpten("gs0", [128, 4, DV], BF16), sGs[0].ap(),
                  list(t_sgw.values()), sem="sdmp")
            P.dma("gpsimd", dumpten("rstd", [128, 16]), sRstd[:],
                  [t_rstd] + list(t_otm.values()), sem="sdmp")
            P.dma("gpsimd", dumpten("sgw", [128, 16, DV], BF16), sSgw.ap(),
                  list(t_sgw.values()) + list(t_otm.values()), sem="sdmp")
            P.dma("gpsimd", dumpten("ssq", [128, 16]), sSsq[:], t_sq + [t_rstd], sem="sdmp")
            P.dma("gpsimd", dumpten("x1tm", [128, 4, D], BF16), sX1tm.ap(), t_ln1, sem="sdmp")
            P.dma("gpsimd", dumpten("ofull", [128, 16, DV], BF16), sOfull.ap(),
                  list(t_otm.values()), sem="sdmp")
            P.dma("gpsimd", dumpten("otm", [128, 4, VD], BF16), sOtm.ap(),
                  list(t_otm.values()) + [t_ofm[3]], sem="sdmp")

        t_x1fm = []
        for tc in range(4):
            for half in range(2):
                pbk = pb[6 + half].ap().bitcast(BF16)
                ttr = []
                for f in range(8):
                    fc = half * 8 + f
                    ttr.append(TR(pbk[:, f * 128:(f + 1) * 128],
                                  sX1tm[:, tc, fc * 128:(fc + 1) * 128],
                                  [t_ln1[tc], pb_free[6 + half]]))
                tev = V(lambda e_, tc=tc, half=half, pbk=pbk: e_.tensor_copy(
                    out=sX1fm.ap()[:, half * 8:(half + 1) * 8,
                                   tc * 128:(tc + 1) * 128],
                    in_=pbk[:, 0:1024].rearrange("p (a b) -> p a b", a=8)),
                    [ttr[-1]])
                pb_free[6 + half] = tev
                t_x1fm.append(tev)

        stA.close()

        stB = ExitStack()
        eB = stB.enter_context
        sFuse = eB(nc.sbuf_tensor(f"sFuse_{t2}", [128, NJ, TF], BF16))
        sSilu = eB(nc.sbuf_tensor(f"sSilu_{t2}", [128, TF], BF16))

        t_fuse = {}
        prev_tfu = None
        for j in range(NJ):
            b = j % 2
            twg = P.dma("sync", sWgj[b].ap(), dwgfb[j],
                        [slot_wgj[b], t_ffnw[("g", j)]], sem=f"rwg{b}")
            twu = P.dma("sync", sWuj[b].ap(), dwufb[j],
                        [slot_wuj[b], t_ffnw[("u", j)]], sem=f"rwu{b}")
            bank = (j % 2) * 2
            bank2 = bank + 1
            tmg = MM(pb[bank][:, 0:512],
                     [(sWgj[b][:, k, :], sX1fm[:, k, :]) for k in range(NKC)],
                     t_x1fm + [twg, pb_free[bank]])
            tmu = MM(pb[bank2][:, 0:512],
                     [(sWuj[b][:, k, :], sX1fm[:, k, :]) for k in range(NKC)],
                     [twu, pb_free[bank2]])
            slot_wgj[b] = tmg
            slot_wuj[b] = tmu
            tsl = SC(lambda e_, bank=bank, j=j: e_.activation(
                out=sSilu[:], in_=pb[bank][:, 0:512], func=AF.Silu,
                scale=sSg[:, j:j + 1]), [tmg, t_sg, prev_tfu])
            pb_free[bank] = tsl
            tfu = V(lambda e_, bank2=bank2, j=j: e_.scalar_tensor_tensor(
                out=sFuse[:, j, :], in0=pb[bank2][:, 0:512],
                scalar=sSu[:, j:j + 1], in1=sSilu[:],
                op0=ALU.mult, op1=ALU.mult), [tmu, tsl, t_su])
            pb_free[bank2] = tfu
            t_fuse[j] = tfu
            prev_tfu = tfu
        all_fuse = [t_fuse[j] for j in range(NJ)]

        t_r2 = {}
        for ps in range(2):
            groups = [(tc, ps * 2 + og) for og in range(2) for tc in range(4)]
            smm = [None] * 8
            for k in range(64):
                b = k % 2
                twd = P.dma("sync", sWdk[b][:],
                            dwdfb[k][:, ps * 1024:(ps + 1) * 1024],
                            [slot_wdk[b], t_ffnw[("d", k)]], sem=f"rwd{b}")
                for gi, (tc, og) in enumerate(groups):
                    w = [twd]
                    if k == 0:
                        w = w + all_fuse + [pb_free[gi]]
                    l = sFuse[:, k, tc * 128:(tc + 1) * 128]
                    r = sWdk[b].ap()[:, (og - ps * 2) * 512:(og - ps * 2 + 1) * 512]
                    smm[gi] = P.do(
                        "tensor",
                        lambda e_, l=l, r=r, o=pb[gi], s=(k == 0), tt=(k == 63):
                        e_.matmul(o[:, 0:512], l, r, start=s, stop=tt),
                        w, inc="pe")
                slot_wdk[b] = smm[7]
            for gi, (tc, og) in enumerate(groups):
                te = V(lambda e_, gi=gi, tc=tc, og=og: e_.scalar_tensor_tensor(
                    out=sR2[:, tc, og * 512:(og + 1) * 512], in0=pb[gi][:, 0:512],
                    scalar=0.0, in1=sX1tm[:, tc, og * 512:(og + 1) * 512],
                    op0=ALU.bypass, op1=ALU.add,
                    accum_out=sLNa[:, tc, og:og + 1]),
                    [smm[gi], t_ln1[tc]])
                pb_free[gi] = te
                t_r2[(og, tc)] = te

        t_store = []
        for tc in range(4):
            tsum = V(lambda e_, tc=tc: e_.reduce_sum(
                out=sLNs[:, tc:tc + 1], in_=sLNa[:, tc, :], axis=AXX),
                [t_r2[(og2, tc)] for og2 in range(4)])
            tssq = SC(lambda e_, tc=tc: e_.activation(
                out=sSqD[:], in_=sR2[:, tc, :], func=AF.Square,
                accum_out=sLNq[:, tc:tc + 1]),
                [t_r2[(og2, tc)] for og2 in range(4)])
            tm_ = V(lambda e_, tc=tc: e_.tensor_scalar(
                out=sLNm[:, tc:tc + 1], in0=sLNs[:, tc:tc + 1],
                scalar1=1.0 / D, scalar2=None, op0=ALU.mult), [tsum])
            tms = V(lambda e_, tc=tc: e_.tensor_mul(
                out=sLNt[:, tc:tc + 1], in0=sLNm[:, tc:tc + 1],
                in1=sLNm[:, tc:tc + 1]), [tm_])
            tvq = V(lambda e_, tc=tc: e_.tensor_scalar(
                out=sLNv[:, tc:tc + 1], in0=sLNq[:, tc:tc + 1],
                scalar1=1.0 / D, scalar2=None, op0=ALU.mult), [tssq])
            tvar = V(lambda e_, tc=tc: e_.tensor_sub(
                out=sLNr[:, tc:tc + 1], in0=sLNv[:, tc:tc + 1],
                in1=sLNt[:, tc:tc + 1]), [tvq, tms])
            tsq2 = SC(lambda e_, tc=tc: e_.activation(
                out=sLNt[:, tc:tc + 1], in_=sLNr[:, tc:tc + 1], func=AF.Ln,
                bias=sEps[:, 0:1], scale=1.0), [tvar, t_eps])
            trc = SC(lambda e_, tc=tc: e_.activation(
                out=sLNr[:, tc:tc + 1], in_=sLNt[:, tc:tc + 1], func=AF.Exp,
                scale=-0.5), [tsq2])
            b = tc % 2
            tap = V(lambda e_, tc=tc, b=b: e_.tensor_scalar(
                out=sOut[b][:], in0=sR2[:, tc, :],
                scalar1=sLNm[:, tc:tc + 1], scalar2=sLNr[:, tc:tc + 1],
                op0=ALU.subtract, op1=ALU.mult), [trc, slot_out[b]])
            tst = P.dma("sync", dy[tk0 + tc * 128: tk0 + (tc + 1) * 128, :],
                        sOut[b][:], [tap], sem=f"ry{b}")
            slot_out[b] = tst
            t_store.append(tst)

        if "ln2" in dump and t2 == 0:
            P.dma("gpsimd", dumpten("r2", [128, 4, D], BF16), sR2.ap(),
                  list(t_r2.values()) + t_store, sem="sdmp")
            P.dma("gpsimd", dumpten("ln2a", [128, 4, 4]), sLNa.ap(),
                  list(t_r2.values()) + t_store, sem="sdmp")
            P.dma("gpsimd", dumpten("ln2s", [128, 4]), sLNs[:], t_store, sem="sdmp")
            P.dma("gpsimd", dumpten("ln2q", [128, 4]), sLNq[:], t_store, sem="sdmp")
            P.dma("gpsimd", dumpten("ln2m", [128, 4]), sLNm[:], t_store, sem="sdmp")
            P.dma("gpsimd", dumpten("ln2r", [128, 4]), sLNr[:], t_store, sem="sdmp")
            P.dma("gpsimd", dumpten("fuse0", [128, 8, TF], BF16),
                  sFuse.ap()[:, 0:8, :], list(t_fuse.values()), sem="sdmp")
        stB.close()
        tok_prevB = [smm[7], t_fuse[NJ - 1]]

    for sem, q in P.dma_sems.items():
        if P.counts.get(sem, 0) > 0:
            P.do(q, [], waits=[(sem, P.counts[sem], "none", 16)])
    P.do("sync", [], waits=t_store)

    st2.close()
    st.close()
    P.emit()
    return nc


_BUILD_CACHE = {}


def _get_nc():
    if "nc" not in _BUILD_CACHE:
        _BUILD_CACHE["nc"] = build("full")
    return _BUILD_CACHE["nc"]


def prep_shared(i):
    d = {}
    d["wq"] = np.ascontiguousarray(np.asarray(i["L0_attn_Wq"], dtype=np.float32))
    d["wk"] = np.ascontiguousarray(np.asarray(i["L0_attn_Wk"], dtype=np.float32))
    d["wv"] = np.ascontiguousarray(np.asarray(i["L0_attn_Wv"], dtype=np.float32))
    d["wg"] = np.ascontiguousarray(np.asarray(i["L0_attn_Wg"], dtype=np.float32))
    d["wgk1"] = np.ascontiguousarray(np.asarray(i["L0_attn_Wgk1"], dtype=np.float32))
    d["wgk2"] = np.ascontiguousarray(np.asarray(i["L0_attn_Wgk2"], dtype=np.float32))
    d["wo"] = np.ascontiguousarray(np.asarray(i["L0_attn_Wo"], dtype=np.float32))
    d["bgk2pf"] = np.ascontiguousarray(
        np.asarray(i["L0_attn_bgk2"], dtype=np.float32).reshape(16, 128).T)
    d["wnormr"] = np.ascontiguousarray(
        np.asarray(i["L0_attn_Wnorm"], dtype=np.float32).reshape(1, DV))
    d["sgpf"] = np.ascontiguousarray(
        np.asarray(i["L0_ffn_s_gate"], dtype=np.float32).reshape(NJ, 128).T)
    d["supf"] = np.ascontiguousarray(
        np.asarray(i["L0_ffn_s_up"], dtype=np.float32).reshape(NJ, 128).T)
    d["sdnr"] = np.ascontiguousarray(
        np.asarray(i["L0_ffn_s_down"], dtype=np.float32).reshape(1, D))
    wgt = np.asarray(i["L0_ffn_w_gate"])
    d["wg8t"] = np.ascontiguousarray(
        wgt.reshape(16, 128, NJ, 128).transpose(2, 1, 0, 3)
        .reshape(NJ, 128, 16 * 128))
    wut = np.asarray(i["L0_ffn_w_up"])
    d["wu8t"] = np.ascontiguousarray(
        wut.reshape(16, 128, NJ, 128).transpose(2, 1, 0, 3)
        .reshape(NJ, 128, 16 * 128))
    d["wd8"] = np.ascontiguousarray(
        np.asarray(i["L0_ffn_w_down"]).reshape(64, 128, D))
    d["identb"] = np.eye(128, dtype=NPBF)
    s_ = np.arange(128)[:, None]
    t_ = np.arange(128)[None, :]
    d["maskut"] = (s_ <= t_).astype(NPBF)
    return d


def make_in_maps(ins):
    shared = prep_shared(ins)
    x = np.asarray(ins["L0_attn_x"], dtype=np.float32)
    in_maps = []
    for c in range(8):
        r = c % 4
        m = dict(shared)
        m["xs"] = np.ascontiguousarray(x[c // 4, r * TPC:(r + 1) * TPC, :])
        sel = (np.arange(3) < r).astype(np.float32)
        m["sel01"] = np.ascontiguousarray(
            np.broadcast_to(sel, (128, 3)).astype(np.float32))
        m["selsc"] = np.ascontiguousarray(
            np.broadcast_to(sel * INV16, (128, 3)).astype(np.float32))
        m["selbi"] = np.ascontiguousarray(
            np.broadcast_to(-(1.0 - sel) * 100.0, (128, 3)).astype(np.float32))
        in_maps.append(m)
    return in_maps


def kernel(L0_attn_x, L0_attn_Wq, L0_attn_Wk, L0_attn_Wv, L0_attn_Wgk1,
           L0_attn_Wgk2, L0_attn_bgk2, L0_attn_Wg, L0_attn_Wnorm, L0_attn_Wo,
           L0_ffn_w_gate, L0_ffn_w_up, L0_ffn_w_down, L0_ffn_s_gate,
           L0_ffn_s_up, L0_ffn_s_down):
    ins = dict(L0_attn_x=L0_attn_x, L0_attn_Wq=L0_attn_Wq, L0_attn_Wk=L0_attn_Wk,
               L0_attn_Wv=L0_attn_Wv, L0_attn_Wgk1=L0_attn_Wgk1,
               L0_attn_Wgk2=L0_attn_Wgk2, L0_attn_bgk2=L0_attn_bgk2,
               L0_attn_Wg=L0_attn_Wg, L0_attn_Wnorm=L0_attn_Wnorm,
               L0_attn_Wo=L0_attn_Wo, L0_ffn_w_gate=L0_ffn_w_gate,
               L0_ffn_w_up=L0_ffn_w_up, L0_ffn_w_down=L0_ffn_w_down,
               L0_ffn_s_gate=L0_ffn_s_gate, L0_ffn_s_up=L0_ffn_s_up,
               L0_ffn_s_down=L0_ffn_s_down)
    in_maps = make_in_maps(ins)
    nc = _get_nc()
    res = run_bass_kernel_spmd(nc, in_maps, core_ids=list(range(8)))
    out = np.empty((2, 8192, D), dtype=np.float32)
    for c in range(8):
        out[c // 4, (c % 4) * TPC:(c % 4 + 1) * TPC, :] = res.results[c]["y"]
    return out


# version 10
# speedup vs baseline: 1.0965x; 1.0965x over previous
import numpy as np
import ml_dtypes
from contextlib import ExitStack

import concourse.bass as bass
import concourse.mybir as mybir
from concourse.bass_utils import run_bass_kernel_spmd

F32 = mybir.dt.float32
BF16 = mybir.dt.bfloat16
I8 = mybir.dt.int8
AF = mybir.ActivationFunctionType
ALU = mybir.AluOpType
NPBF = ml_dtypes.bfloat16

ENG = ("sync", "tensor", "vector", "scalar", "gpsimd")

TPC = 2048
D = 2048
H = 4
DK = 512
DV = 256
KD = 2048
VD = 1024
LOW = 16
INTER = 8192
CH = 128
TS = 1024
NT1 = TPC // TS
NCH1 = TS // CH
TF = 512
NT2 = TPC // TF
NKC = D // 128
NJ = INTER // 128
SCALE = 1.0 / float(np.sqrt(512.0))
EPS = 1e-5
INV16 = 1.0 / 16.0
AXX = mybir.AxisListType.X


class Prog:

    def __init__(self, nc):
        self.nc = nc
        self.ops = []
        self.counts = {}
        self.sem_order = []
        self.dma_sems = {}

    def _sem(self, name):
        if name not in self.counts:
            self.counts[name] = 0
            self.sem_order.append(name)
        return name

    def do(self, engine, fns, waits=(), inc=None, inc_n=1, noskip=False):
        if fns is None:
            fns = []
        if not isinstance(fns, (list, tuple)):
            fns = [fns]
        ws = [w for w in waits if w is not None]
        tok = None
        if inc is not None:
            self._sem(inc)
            self.counts[inc] += inc_n
            tok = (inc, self.counts[inc], engine, inc_n)
        self.ops.append((engine, list(fns), ws, (inc, inc_n) if inc else None,
                         noskip))
        return tok

    def dma(self, queue, out, in_, waits=(), sem=None, **kw):
        sem = sem or ("q_" + queue)
        self.dma_sems.setdefault(sem, queue)
        return self.do(
            queue,
            lambda e, out=out, in_=in_, kw=kw: e.dma_start(out=out, in_=in_, **kw),
            waits, inc=sem, inc_n=16,
        )

    def fence(self, engine):
        return self.do(engine, [], inc={"vector": "vv", "scalar": "sc",
                                        "tensor": "pe", "gpsimd": "gg",
                                        "sync": "sy"}[engine])

    def barrier(self):
        for sem, q in self.dma_sems.items():
            if self.counts.get(sem, 0) > 0:
                self.do(q, [], waits=[(sem, self.counts[sem], "none", 16)])
        toks = [self.do(e, [], inc="bar") for e in ENG]
        for e in ENG:
            self.do(e, [], waits=toks)

    def emit(self):
        nc = self.nc
        with ExitStack() as stx:
            sems = {n: stx.enter_context(nc.semaphore(n)) for n in self.sem_order}
            waited = {e: {} for e in ENG}
            with nc.Block() as block:
                def mk(engname):
                    def body(eng):
                        for (e, fns, ws, inc, noskip) in self.ops:
                            if e != engname:
                                continue
                            need = {}
                            for (sem, val, prod, n) in ws:
                                if waited[engname].get(sem, 0) >= val:
                                    continue
                                if need.get(sem, 0) < val:
                                    need[sem] = val
                            for sem, val in need.items():
                                eng.wait_ge(sems[sem], val)
                                waited[engname][sem] = val
                            inst = None
                            for fn in fns:
                                inst = fn(eng)
                            if inc is not None:
                                if inst is None:
                                    inst = eng.nop()
                                inst.then_inc(sems[inc[0]], inc[1])
                    return body
                block.sync(mk("sync"))
                block.tensor(mk("tensor"))
                block.vector(mk("vector"))
                block.scalar(mk("scalar"))
                block.gpsimd(mk("gpsimd"))


def build(upto="full", dump=(), lim=None):
    nc = bass.Bass("TRN2", target_bir_lowering=False)
    P = Prog(nc)
    dump = set(dump)

    dx = nc.dram_tensor("xs", [TPC, D], F32, kind="ExternalInput").ap()
    dwq = nc.dram_tensor("wq", [D, KD], F32, kind="ExternalInput").ap()
    dwk = nc.dram_tensor("wk", [D, KD], F32, kind="ExternalInput").ap()
    dwv = nc.dram_tensor("wv", [D, VD], F32, kind="ExternalInput").ap()
    dwg = nc.dram_tensor("wg", [D, VD], F32, kind="ExternalInput").ap()
    dwgk1 = nc.dram_tensor("wgk1", [D, LOW], F32, kind="ExternalInput").ap()
    dwgk2 = nc.dram_tensor("wgk2", [LOW, KD], F32, kind="ExternalInput").ap()
    dwo = nc.dram_tensor("wo", [VD, D], F32, kind="ExternalInput").ap()
    dbgk2 = nc.dram_tensor("bgk2pf", [128, 16], F32, kind="ExternalInput").ap()
    dwnorm = nc.dram_tensor("wnormr", [1, DV], F32, kind="ExternalInput").ap()
    dsg = nc.dram_tensor("sgpf", [128, NJ], F32, kind="ExternalInput").ap()
    dsu = nc.dram_tensor("supf", [128, NJ], F32, kind="ExternalInput").ap()
    dsdn = nc.dram_tensor("sdnr", [1, D], F32, kind="ExternalInput").ap()
    dwg8 = nc.dram_tensor("wg8t", [NJ, 128, 16 * 128], I8, kind="ExternalInput").ap()
    dwu8 = nc.dram_tensor("wu8t", [NJ, 128, 16 * 128], I8, kind="ExternalInput").ap()
    dwd8 = nc.dram_tensor("wd8", [64, 128, D], I8, kind="ExternalInput").ap()
    dident = nc.dram_tensor("identb", [128, 128], BF16, kind="ExternalInput").ap()
    dmask = nc.dram_tensor("maskut", [128, 128], BF16, kind="ExternalInput").ap()
    dsel01 = nc.dram_tensor("sel01", [128, 3], F32, kind="ExternalInput").ap()
    dselsc = nc.dram_tensor("selsc", [128, 3], F32, kind="ExternalInput").ap()
    dselbi = nc.dram_tensor("selbi", [128, 3], F32, kind="ExternalInput").ap()

    dy = nc.dram_tensor("y", [TPC, D], F32, kind="ExternalOutput").ap()

    dxbf = nc.dram_tensor("xbf", [TPC, D], BF16).ap()
    dwqbf = nc.dram_tensor("wqbf", [H, 4, 128, 16 * 128], BF16).ap()
    dwkbf = nc.dram_tensor("wkbf", [H, 4, 128, 16 * 128], BF16).ap()
    dwvbf = nc.dram_tensor("wvbf", [H, 128, 16 * 256], BF16).ap()
    dwgbf = nc.dram_tensor("wgbf", [H, 128, 16 * 256], BF16).ap()
    dwobf = nc.dram_tensor("wobf", [4, 128, 8 * 512], BF16).ap()
    dwgfb = nc.dram_tensor("wgfb", [NJ, 128, 16 * 128], BF16).ap()
    dwufb = nc.dram_tensor("wufb", [NJ, 128, 16 * 128], BF16).ap()
    dwdfb = nc.dram_tensor("wdfb", [64, 128, D], BF16).ap()
    dqd = nc.dram_tensor("qd_scr", [NT1, H, 128, 4, TS], BF16).ap()
    dol = nc.dram_tensor("ol_scr", [NT1, H, 128, NCH1, DV], BF16).ap()
    dgs = nc.dram_tensor("g_scr", [NT1, H, 128, NCH1, DV], BF16).ap()
    dccinA = nc.dram_tensor("ccinA", [128, 16], F32).ap()
    dccoutA = nc.dram_tensor("ccoutA", [4 * 128, 16], F32).ap()
    dccinZ = nc.dram_tensor("ccinZ", [128, 4096], BF16).ap()
    dccoutZ = nc.dram_tensor("ccoutZ", [4 * 128, 4096], BF16).ap()

    ddump = {}
    U16 = mybir.dt.uint16

    def dumpten(name, shape, dtype=F32):
        if name not in ddump:
            if dtype == BF16:
                t = nc.dram_tensor("dbg_" + name, shape, U16,
                                   kind="ExternalOutput").ap()
                ddump[name] = t.bitcast(BF16)
            else:
                ddump[name] = nc.dram_tensor(
                    "dbg_" + name, shape, dtype, kind="ExternalOutput").ap()
        return ddump[name]

    st = ExitStack()
    e = st.enter_context

    sS = e(nc.sbuf_tensor("sS", [128, 16, DV], BF16))
    sId = e(nc.sbuf_tensor("sId", [128, 128], BF16))
    sMask = e(nc.sbuf_tensor("sMask", [128, 128], BF16))
    sWnB = e(nc.sbuf_tensor("sWnB", [128, DV], BF16))
    sGk1 = e(nc.sbuf_tensor("sGk1", [128, 16, LOW], BF16))
    sGk2 = e(nc.sbuf_tensor("sGk2", [LOW, KD], BF16))
    sBgk2 = e(nc.sbuf_tensor("sBgk2", [128, 16], F32))
    sSg = e(nc.sbuf_tensor("sSg", [128, NJ], F32))
    sSu = e(nc.sbuf_tensor("sSu", [128, NJ], F32))
    sSel01 = e(nc.sbuf_tensor("sSel01", [128, 3], F32))
    sSelsc = e(nc.sbuf_tensor("sSelsc", [128, 3], F32))
    sSelbi = e(nc.sbuf_tensor("sSelbi", [128, 3], F32))
    sPrev = e(nc.sbuf_tensor("sPrev", [128, 16], F32))
    sBl = e(nc.sbuf_tensor("sBl", [128, 16, NCH1], F32))
    sEbl = e(nc.sbuf_tensor("sEbl", [128, 16, NCH1], F32))
    sPexp = e(nc.sbuf_tensor("sPexp", [128, 16, NCH1], F32))
    sZb = e(nc.sbuf_tensor("sZb", [128, 128], F32))
    sOneB = e(nc.sbuf_tensor("sOneB", [1, 512], BF16))
    sEight = e(nc.sbuf_tensor("sEight", [128, NCH1], F32))
    sI1 = e(nc.sbuf_tensor("sI1", [128, 4, NCH1], F32))
    sI2 = e(nc.sbuf_tensor("sI2", [128, 4, NCH1], F32))
    sEps = e(nc.sbuf_tensor("sEps", [128, 1], F32))

    pb = [e(nc.psum_tensor(f"pb{i}", [128, 512], F32)) for i in range(8)]
    pb_free = [None] * 8

    def V(fn, waits=()):
        return P.do("vector", fn, waits, inc="vv")

    def Vs(fn, waits=()):
        return P.do("vector", fn, waits, inc="vv", noskip=True)

    def SC(fn, waits=()):
        return P.do("scalar", fn, waits, inc="sc")

    def GP(fn, waits=()):
        return P.do("gpsimd", fn, waits, inc="gg")

    def MM(out_ap, pairs, waits=()):
        n = len(pairs)
        fns = []
        for i, (l, r) in enumerate(pairs):
            fns.append(lambda eng, l=l, r=r, s=(i == 0), t=(i == n - 1), o=out_ap:
                       eng.matmul(o, l, r, start=s, stop=t))
        return P.do("tensor", fns, waits, inc="pe")

    def TR(out_ap, in_ap, waits=()):
        return P.do("tensor",
                    lambda eng, o=out_ap, i=in_ap: eng.transpose(o, i, sId[:]),
                    waits, inc="pe")

    t_id = P.dma("sync", sId[:], dident, sem="c_id")
    t_mask = P.dma("sync", sMask[:], dmask, sem="c_mk")
    t_bgk2 = P.dma("sync", sBgk2[:], dbgk2, sem="c_bg")
    t_sg = P.dma("sync", sSg[:], dsg, sem="c_sg")
    t_su = P.dma("sync", sSu[:], dsu, sem="c_su")
    t_sel = P.dma("sync", sSel01[:], dsel01, sem="c_s0")
    t_selsc = P.dma("sync", sSelsc[:], dselsc, sem="c_s1")
    t_selbi = P.dma("sync", sSelbi[:], dselbi, sem="c_s2")
    t_zb = V(lambda e_: e_.memset(sZb[:], 0.0))
    t_ones = V(lambda e_: e_.memset(sOneB[:], 1.0))
    t_prev = V(lambda e_: e_.memset(sPrev[:], 0.0))
    t_s0 = V(lambda e_: e_.memset(sS.ap().rearrange("p a b -> p (a b)"), 0.0))
    t_eps = V(lambda e_: e_.memset(sEps[:], EPS))

    st0 = ExitStack()
    e0 = st0.enter_context
    sXst = [e0(nc.sbuf_tensor(f"sXst{i}", [128, 2, D], F32)) for i in range(2)]
    sXbfs = [e0(nc.sbuf_tensor(f"sXbfs{i}", [128, 2, D], BF16)) for i in range(2)]
    sWbig = [e0(nc.sbuf_tensor(f"sWbig{i}", [128, 8, 512], F32)) for i in range(2)]
    sWcol = [e0(nc.sbuf_tensor(f"sWcol{i}", [128, 16, 512], BF16)) for i in range(2)]
    sW8 = [e0(nc.sbuf_tensor(f"sW8_{i}", [128, 2, D], I8)) for i in range(2)]
    sW8b = [e0(nc.sbuf_tensor(f"sW8b{i}", [128, 2, D], BF16)) for i in range(2)]
    sRow = e0(nc.sbuf_tensor("sRow", [1, D], F32))
    sRowB = e0(nc.sbuf_tensor("sRowB", [1, D], BF16))
    sSdnB = e0(nc.sbuf_tensor("sSdnB", [128, D], BF16))
    sGkSt = e0(nc.sbuf_tensor("sGkSt", [128, 16 * LOW], F32))
    sGk2St = e0(nc.sbuf_tensor("sGk2St", [LOW, KD], F32))

    t = P.dma("sync", sGkSt.ap().rearrange("p (k f) -> p k f", k=16),
              dwgk1.rearrange("(k p) f -> p k f", p=128), sem="c_g1")
    t_gk1 = V(lambda e_: e_.tensor_copy(
        out=sGk1.ap().rearrange("p a b -> p (a b)"), in_=sGkSt[:]), [t])
    t = P.dma("sync", sGk2St[:], dwgk2, sem="c_g2")
    t_gk2 = V(lambda e_: e_.tensor_copy(out=sGk2[:], in_=sGk2St[:]), [t])

    t = P.dma("sync", sRow[0:1, 0:DV], dwnorm, sem="c_wn")
    t_wnrow = V(lambda e_: e_.tensor_copy(out=sRowB[0:1, 0:DV],
                                          in_=sRow[0:1, 0:DV]), [t, t_ones])
    t = MM(pb[0][:, 0:DV], [(sOneB[0:1, 0:128], sRowB[0:1, 0:DV])], [t_wnrow])
    t_wnb = V(lambda e_: e_.tensor_copy(out=sWnB[:], in_=pb[0][:, 0:DV]), [t])
    pb_free[0] = t_wnb

    t = P.dma("sync", sRow[:], dsdn, [t_wnrow], sem="c_sd")
    t = V(lambda e_: e_.tensor_copy(out=sRowB[:], in_=sRow[:]), [t, t_wnb])
    t_sdnb = None
    for i in range(4):
        tm = MM(pb[1][:, 0:512],
                [(sOneB[0:1, 0:128], sRowB[0:1, i * 512:(i + 1) * 512])],
                [t, pb_free[1]])
        t_sdnb = V(lambda e_, i=i: e_.tensor_copy(
            out=sSdnB[:, i * 512:(i + 1) * 512], in_=pb[1][:, 0:512]), [tm])
        pb_free[1] = t_sdnb

    dxr = dx.rearrange("(i p) m -> p i m", p=128)
    dxbr = dxbf.rearrange("(i p) m -> p i m", p=128)
    t_xbf2 = []
    prev_st = [None, None]
    for i in range(8):
        b = i % 2
        tl = P.dma("sync", sXst[b].ap(), dxr[:, i * 2:(i + 1) * 2, :],
                   [prev_st[b]], sem=f"sxl{b}")
        tc = SC(lambda e_, b=b: e_.copy(
            out=sXbfs[b].ap().rearrange("p a b -> p (a b)"),
            in_=sXst[b].ap().rearrange("p a b -> p (a b)")), [tl])
        ts_ = P.dma("gpsimd" if b == 0 else "scalar",
                    dxbr[:, i * 2:(i + 1) * 2, :], sXbfs[b].ap(), [tc],
                    sem=f"sxs{b}")
        prev_st[b] = ts_
        t_xbf2.append(ts_)
    t_xbf = [t_xbf2[i // 2] for i in range(16)]

    t_awb = {}
    prev_big = [None, None]
    prev_col = [None, None]
    _blk = [0]

    def cast_big(tag, wsrc, col0, ncol, scale, outs):
        i = _blk[0]
        _blk[0] += 1
        cb = i % 2
        wr = wsrc.rearrange("(k p) m -> p k m", p=128)
        tcs = []
        for hh in range(2):
            tl = P.dma("sync",
                       sWbig[cb].ap()[:, hh * 4:(hh + 1) * 4, 0:ncol]
                       if False else sWbig[cb].ap()[:, 0:8, 0:ncol],
                       wr[:, hh * 8:(hh + 1) * 8, col0:col0 + ncol],
                       [prev_big[cb] if hh == 0 else tcs[-1]], sem=f"swl{cb}")
            eng = "vector"
            tc = P.do(eng, lambda e_, cb=cb, hh=hh, ncol=ncol, sc_=scale:
                      (e_.tensor_copy(
                          out=sWcol[cb].ap()[:, hh * 8:(hh + 1) * 8, 0:ncol],
                          in_=sWbig[cb].ap()[:, 0:8, 0:ncol])
                       if sc_ is None else e_.tensor_scalar(
                          out=sWcol[cb].ap()[:, hh * 8:(hh + 1) * 8, 0:ncol],
                          in0=sWbig[cb].ap()[:, 0:8, 0:ncol],
                          scalar1=sc_, scalar2=None, op0=ALU.mult)),
                      [tl, prev_col[cb]],
                      inc="vv" if eng == "vector" else "gg")
            tcs.append(tc)
            prev_big[cb] = tc
        stoks = []
        for (oap, srcsl) in outs:
            stoks.append(P.dma("gpsimd" if i % 2 == 0 else "scalar", oap,
                               sWcol[cb].ap()[:, :, srcsl[0]:srcsl[1]],
                               tcs + ([stoks[-1]] if stoks else [prev_col[cb]]),
                               sem=f"sws{cb}"))
        prev_col[cb] = stoks[-1]
        t_awb[tag] = stoks
        return stoks

    for h in range(H):
        cast_big(("q", h), dwq, h * 512, 512, SCALE,
                 [(dwqbf[h, d2].rearrange("p (a b) -> p a b", a=16),
                   (d2 * 128, (d2 + 1) * 128)) for d2 in range(4)])
        cast_big(("k", h), dwk, h * 512, 512, None,
                 [(dwkbf[h, d2].rearrange("p (a b) -> p a b", a=16),
                   (d2 * 128, (d2 + 1) * 128)) for d2 in range(4)])
        cast_big(("v", h), dwv, h * 256, 256, None,
                 [(dwvbf[h].rearrange("p (a b) -> p a b", a=16), (0, 256))])
        cast_big(("g", h), dwg, h * 256, 256, None,
                 [(dwgbf[h].rearrange("p (a b) -> p a b", a=16), (0, 256))])
    for og in range(4):
        i = _blk[0]
        _blk[0] += 1
        cb = i % 2
        wr = dwo.rearrange("(k p) m -> p k m", p=128)
        tl = P.dma("sync", sWbig[cb].ap(),
                   wr[:, :, og * 512:(og + 1) * 512],
                   [prev_big[cb]], sem=f"swl{cb}")
        tc = V(lambda e_, cb=cb: e_.tensor_copy(
            out=sWcol[cb].ap()[:, 0:8, :].rearrange("p a b -> p (a b)"),
            in_=sWbig[cb].ap().rearrange("p a b -> p (a b)")),
            [tl, prev_col[cb]])
        prev_big[cb] = tc
        ts_ = P.dma("gpsimd" if i % 2 == 0 else "scalar",
                    dwobf[og].rearrange("p (a b) -> p a b", a=8),
                    sWcol[cb].ap()[:, 0:8, :], [tc], sem=f"sws{cb}")
        prev_col[cb] = ts_
        t_awb[("o", og)] = [ts_]

    t_ffnw = {}
    prev8 = [None, None]
    prev8b = [None, None]
    dgr = dwg8.rearrange("j p m -> p j m")
    dur = dwu8.rearrange("j p m -> p j m")
    dgfr = dwgfb.rearrange("j p m -> p j m")
    dufr = dwufb.rearrange("j p m -> p j m")
    ddr = dwd8.rearrange("k p m -> p k m")
    ddfr = dwdfb.rearrange("k p m -> p k m")
    for jj in range(NJ // 2):
        j = jj * 2
        b = jj % 2
        tl = P.dma("sync", sW8[b].ap(), dgr[:, j:j + 2, :], [prev8[b]],
                   sem=f"s8l{b}")
        tc = V(lambda e_, b=b: e_.tensor_copy(
            out=sW8b[b].ap().rearrange("p a b -> p (a b)"),
            in_=sW8[b].ap().rearrange("p a b -> p (a b)")), [tl, prev8b[b]])
        ts_ = P.dma("gpsimd", dgfr[:, j:j + 2, :], sW8b[b].ap(), [tc],
                    sem=f"s8s{b}")
        tl2 = P.dma("sync", sW8[b].ap(), dur[:, j:j + 2, :], [tc],
                    sem=f"s8l{b}")
        tc2 = SC(lambda e_, b=b: e_.copy(
            out=sW8b[b].ap().rearrange("p a b -> p (a b)"),
            in_=sW8[b].ap().rearrange("p a b -> p (a b)")), [tl2, ts_])
        ts2 = P.dma("scalar", dufr[:, j:j + 2, :], sW8b[b].ap(), [tc2],
                    sem=f"s8t{b}")
        prev8[b] = tc2
        prev8b[b] = ts2
        t_ffnw[("g", j)] = ts_
        t_ffnw[("g", j + 1)] = ts_
        t_ffnw[("u", j)] = ts2
        t_ffnw[("u", j + 1)] = ts2
    prevd = [prev8b[0], prev8b[1]]
    prevdb = [prev8b[0], prev8b[1]]
    for kk2 in range(32):
        k = kk2 * 2
        b = kk2 % 2
        tl = P.dma("sync", sW8[b].ap(), ddr[:, k:k + 2, :], [prevd[b]],
                   sem=f"sdl{b}")
        tc = None
        for k2 in range(2):
            tc = V(lambda e_, b=b, k2=k2: e_.tensor_tensor(
                out=sW8b[b][:, k2, :], in0=sW8[b][:, k2, :], in1=sSdnB[:],
                op=ALU.mult),
                [tl, prevdb[b], t_sdnb, tc])
        ts_ = P.dma("scalar", ddfr[:, k:k + 2, :], sW8b[b].ap(), [tc],
                    sem=f"sdt{b}")
        prevd[b] = tc
        prevdb[b] = ts_
        t_ffnw[("d", k)] = ts_
        t_ffnw[("d", k + 1)] = ts_
    st0.close()
    P.barrier()

    if upto == "p0":
        for sem, q in P.dma_sems.items():
            if P.counts.get(sem, 0) > 0:
                P.do(q, [], waits=[(sem, P.counts[sem], "none", 16)])
        st.close()
        P.emit()
        return nc

    st1 = ExitStack()
    e1 = st1.enter_context
    sXfm = e1(nc.sbuf_tensor("sXfm", [128, NKC, TS], BF16))
    sEp = e1(nc.sbuf_tensor("sEp", [128, 4, TS], BF16))
    sEm = e1(nc.sbuf_tensor("sEm", [128, 4, TS], BF16))
    sQe = e1(nc.sbuf_tensor("sQe", [128, 4, TS], BF16))
    sKd = e1(nc.sbuf_tensor("sKd", [128, 4, TS], BF16))
    sQF = e1(nc.sbuf_tensor("sQF", [128, 16, CH], F32))
    sXf32 = e1(nc.sbuf_tensor("sXf32", [128, NKC, CH], F32))
    sEpF = e1(nc.sbuf_tensor("sEpF", [128, 4, CH], F32))
    sEmF = e1(nc.sbuf_tensor("sEmF", [128, 4, CH], F32))
    sUb = e1(nc.sbuf_tensor("sUb", [128, 4, 512], BF16))
    sKF = e1(nc.sbuf_tensor("sKF", [128, 16, CH], F32))
    sIdF = e1(nc.sbuf_tensor("sIdF", [128, 128], F32))
    sQd = e1(nc.sbuf_tensor("sQd", [128, 4, TS], BF16))
    sV1 = e1(nc.sbuf_tensor("sV1", [128, NCH1, DV], BF16))
    sG1 = e1(nc.sbuf_tensor("sG1", [128, NCH1, DV], BF16))
    sOl = e1(nc.sbuf_tensor("sOl", [128, NCH1, DV], BF16))
    sGb = e1(nc.sbuf_tensor("sGb", [128, TS], F32))
    sBb = e1(nc.sbuf_tensor("sBb", [128, TS], F32))
    sSig = e1(nc.sbuf_tensor("sSig", [128, 512], F32))
    sT1 = e1(nc.sbuf_tensor("sT1", [LOW, TS], BF16))
    sAT = e1(nc.sbuf_tensor("sAT", [128, 128], BF16))
    sKkf = e1(nc.sbuf_tensor("sKkf", [128, 4, 128], BF16))
    sKkT = e1(nc.sbuf_tensor("sKkT", [128, 512], BF16))
    sDiag = e1(nc.sbuf_tensor("sDiag", [128, 4, 128], BF16))
    sWqs = [e1(nc.sbuf_tensor(f"sWqs{i}", [128, 16, 128], BF16)) for i in range(2)]
    sWks = [e1(nc.sbuf_tensor(f"sWks{i}", [128, 16, 128], BF16)) for i in range(2)]
    sWvs = [e1(nc.sbuf_tensor(f"sWvs{i}", [128, 16, 256], BF16)) for i in range(1)]
    sWgs = [e1(nc.sbuf_tensor(f"sWgs{i}", [128, 16, 256], BF16)) for i in range(1)]

    tok_xfm_free = None
    tok_wq_slot = [None, None]
    tok_wk_slot = [None, None]
    tok_wv_slot = [None, None]
    tok_wg_slot = [None, None]
    tok_spill_qd = None
    tok_spill_ol = None
    tok_spill_g = None
    tok_Sevac = [None] * 16
    tok_t1 = None
    tok_prep = None
    prev_vf = None
    prev_sf = None

    _ntau, _nh, _stg = (lim + (99,))[:3] if lim else (NT1, H, 99)
    for tau in range(_ntau):
        t_xfm = []
        for fc in range(NKC):
            w = [t_xbf[tau * 8 + s_] for s_ in range(8)]
            if fc == 0:
                w = w + [tok_xfm_free]
            t_xfm.append(P.dma(
                "sync", sXfm[:, fc, :],
                dxbf[tau * TS:(tau + 1) * TS, fc * 128:(fc + 1) * 128],
                w, transpose=True, sem="sxf"))

        for half in range(2):
            tm = MM(pb[half][0:LOW, 0:512],
                    [(sGk1[:, k, :], sXfm[:, k, half * 512:(half + 1) * 512])
                     for k in range(NKC)],
                    t_xfm + [t_gk1, pb_free[half], tok_t1])
            tok_t1 = V(lambda e_, half=half: e_.tensor_copy(
                out=sT1[:, half * 512:(half + 1) * 512],
                in_=pb[half][0:LOW, 0:512]), [tm])
            pb_free[half] = tok_t1

        if tau == 0:
            tidf = V(lambda e_: e_.tensor_copy(out=sIdF[:], in_=sId[:]), [t_id])
            tok_qkF = [None] * 16
            if True:
                sXtmF = sGb.ap()[:, 0:512]
                sWqkF = sBb.ap().rearrange("p (k m) -> p k m", k=8)
                if True:
                    tprev_e = None
                    txl = None
                    for piece in range(4):
                        txl = P.dma("sync", sXtmF,
                                    dx[0:128, piece * 512:(piece + 1) * 512],
                                    [txl if piece == 0 else ttr], sem="sxfz")
                        for f2 in range(4):
                            fc = piece * 4 + f2
                            ttr = P.do("tensor",
                                       lambda e_, f2=f2: e_.transpose(
                                           pb[0][:, 0:128],
                                           sXtmF[:, f2 * 128:(f2 + 1) * 128],
                                           sIdF[:]),
                                       [txl, tidf, pb_free[0]], inc="pe")
                            tev = V(lambda e_, fc=fc: e_.tensor_copy(
                                out=sXf32[:, fc, :], in_=pb[0][:, 0:128]),
                                [ttr, tprev_e])
                            pb_free[0] = tev
                            tprev_e = tev
                if True:
                    twf = None
                    for (wsrc, dstT, scl) in ((dwq, sQF, SCALE), (dwk, sKF, None)):
                        for hd in range(16):
                            tmf = None
                            for kh in range(2):
                                twf = P.dma(
                                    "sync", sWqkF,
                                    wsrc.rearrange("(k p) m -> p k m", p=128)
                                    [:, kh * 8:(kh + 1) * 8,
                                     hd * 128:(hd + 1) * 128],
                                    [twf], sem="swfz")
                                tmf = P.do("tensor", [
                                    (lambda e_, k=k, kh=kh, s_=(kh == 0 and k == 0),
                                     t_=(kh == 1 and k == 7):
                                     e_.matmul(pb[1][:, 0:CH], sWqkF[:, k, :],
                                               sXf32[:, kh * 8 + k, :],
                                               start=s_, stop=t_))
                                    for k in range(8)],
                                    [twf, tprev_e, pb_free[1] if kh == 0 else None],
                                    inc="pe")
                                twf = tmf
                            if scl is None:
                                tevf = V(lambda e_, hd=hd, dstT=dstT: e_.tensor_copy(
                                    out=dstT[:, hd, :], in_=pb[1][:, 0:CH]),
                                    [tmf, tok_qkF[hd]])
                            else:
                                tevf = V(lambda e_, hd=hd, dstT=dstT, scl=scl:
                                         e_.tensor_scalar(
                                    out=dstT[:, hd, :], in0=pb[1][:, 0:CH],
                                    scalar1=scl, scalar2=None, op0=ALU.mult),
                                    [tmf, tok_qkF[hd]])
                            pb_free[1] = tevf
                            tok_qkF[hd] = tevf
            tok_prep = tok_qkF[15]

        if tau == 0 and "p1" in dump:
            P.dma("gpsimd", dumpten("xf32", [128, NKC, CH]), sXf32.ap(),
                  [tprev_e], sem="sdmp")
            P.dma("gpsimd", dumpten("qFr", [128, 16, CH]), sQF.ap(),
                  [tok_qkF[hd] for hd in range(16)], sem="sdmp")

        for h in range(_nh):
            hd0 = h * 4

            tws_q = [None] * 4
            tws_k = [None] * 4
            for d in (0, 1):
                tws_q[d] = P.dma("sync", sWqs[d % 2].ap(), dwqbf[h, d],
                                 [tok_wq_slot[d % 2]] + t_awb[("q", h)],
                                 sem=f"swq{d % 2}")
                tws_k[d] = P.dma("sync", sWks[d % 2].ap(), dwkbf[h, d],
                                 [tok_wk_slot[d % 2]] + t_awb[("k", h)],
                                 sem=f"swk{d % 2}")
            slvg = 0
            tws_v = P.dma("sync", sWvs[slvg].ap(), dwvbf[h],
                          [tok_wv_slot[slvg]] + t_awb[("v", h)], sem=f"swv{slvg}")
            tws_g = P.dma("sync", sWgs[slvg].ap(), dwgbf[h],
                          [tok_wg_slot[slvg]] + t_awb[("g", h)], sem=f"swg{slvg}")

            if _stg < 1:
                continue
            tok_ub = [None] * 4
            tok_ep = {}
            tok_em = {}
            tok_ebl = {}
            tok_bl = {}
            for d in range(4):
                hd = hd0 + d
                tg = []
                for half in range(2):
                    bank = 2 + half
                    usl = (d * 2 + half) % 4
                    tm = MM(pb[bank][:, 0:512],
                            [(sGk2[:, hd * 128:(hd + 1) * 128],
                              sT1[:, half * 512:(half + 1) * 512])],
                            [tok_t1, t_gk2, pb_free[bank]])
                    tev = V(lambda e_, bank=bank, usl=usl: e_.tensor_copy(
                        out=sUb[:, usl, :], in_=pb[bank][:, 0:512]),
                        [tm, tok_ub[usl], prev_vf, prev_sf])
                    pb_free[bank] = tev
                    tsg = SC(lambda e_, usl=usl, hd=hd: e_.activation(
                        out=sSig[:], in_=sUb[:, usl, :], func=AF.Sigmoid,
                        bias=sBgk2[:, hd:hd + 1], scale=1.0),
                        [tev, t_bgk2, prev_sf])
                    tok_ub[usl] = tsg
                    tln = SC(lambda e_, half=half: e_.activation(
                        out=sGb[:, half * 512:(half + 1) * 512], in_=sSig[:],
                        func=AF.Ln),
                        [tsg, tok_prep if tau == 0 else None])
                    tg.append(tln)
                tsc = []
                for c in range(NCH1):
                    tsc.append(Vs(lambda e_, c=c: e_.tensor_tensor_scan(
                        out=sBb[:, c * 128:(c + 1) * 128],
                        data0=sGb[:, c * 128:(c + 1) * 128],
                        data1=sZb[:], initial=0.0, op0=ALU.add, op1=ALU.add),
                        [tg[c // 4], t_zb] + ([tsc[-1]] if tsc else [])))
                tok_ep[d] = SC(lambda e_, d=d: e_.activation(
                    out=sEp[:, d, :], in_=sBb[:], func=AF.Exp, scale=INV16), tsc)
                tok_em[d] = SC(lambda e_, d=d: e_.activation(
                    out=sEm[:, d, :], in_=sBb[:], func=AF.Exp, scale=-INV16), tsc)
                if tau == 0:
                    tok_epF = SC(lambda e_, d=d: e_.activation(
                        out=sEpF[:, d, :], in_=sBb[:, 0:CH], func=AF.Exp,
                        scale=INV16), tsc)
                    tok_emF = SC(lambda e_, d=d: e_.activation(
                        out=sEmF[:, d, :], in_=sBb[:, 0:CH], func=AF.Exp,
                        scale=-INV16), tsc)
                tbl = Vs(lambda e_, hd=hd: e_.tensor_copy(
                    out=sBl[:, hd, :], in_=sBb[:, 127::128]), tsc)
                tok_ebl[d] = SC(lambda e_, hd=hd: e_.activation(
                    out=sEbl[:, hd, :], in_=sBl[:, hd, :], func=AF.Exp,
                    scale=INV16), [tbl])
                tok_bl[d] = tbl

            blsl = sBl.ap()[:, hd0:hd0 + 4, :]
            w0 = [tok_bl[d] for d in range(4)] + [t_prev]
            tI = V(lambda e_, blsl=blsl: e_.tensor_copy(
                out=sI1.ap()[:, :, 0:1], in_=blsl[:, :, 0:1]), w0)
            tI = V(lambda e_, blsl=blsl: e_.tensor_add(
                out=sI1.ap()[:, :, 1:8], in0=blsl[:, :, 1:8],
                in1=blsl[:, :, 0:7]), w0 + [tI])
            tI = V(lambda e_: e_.tensor_copy(
                out=sI2.ap()[:, :, 0:2], in_=sI1.ap()[:, :, 0:2]), [tI])
            tI = V(lambda e_: e_.tensor_add(
                out=sI2.ap()[:, :, 2:8], in0=sI1.ap()[:, :, 2:8],
                in1=sI1.ap()[:, :, 0:6]), [tI])
            tI = V(lambda e_: e_.tensor_copy(
                out=sI1.ap()[:, :, 0:4], in_=sI2.ap()[:, :, 0:4]), [tI])
            tI = V(lambda e_: e_.tensor_add(
                out=sI1.ap()[:, :, 4:8], in0=sI2.ap()[:, :, 4:8],
                in1=sI2.ap()[:, :, 0:4]), [tI])
            tI = V(lambda e_, blsl=blsl: e_.tensor_sub(
                out=sI2.ap().rearrange("p a b -> p (a b)"),
                in0=sI1.ap().rearrange("p a b -> p (a b)"),
                in1=blsl.rearrange("p a b -> p (a b)")), [tI])
            tprevn = []
            for dd in range(4):
                hd = hd0 + dd
                tI2 = V(lambda e_, dd=dd, hd=hd: e_.tensor_scalar(
                    out=sI2.ap()[:, dd, :], in0=sI2.ap()[:, dd, :],
                    scalar1=sPrev[:, hd:hd + 1], scalar2=None, op0=ALU.add),
                    [tI])
                tprevn.append((hd, tI2))
            tok_pexp_h = SC(lambda e_, hd0=hd0: e_.activation(
                out=sPexp.ap()[:, hd0:hd0 + 4, :],
                in_=sI2.ap().rearrange("p a b -> p (a b)"),
                func=AF.Exp, scale=INV16), [t for _, t in tprevn])
            for dd in range(4):
                hd = hd0 + dd
                t_prev = V(lambda e_, dd=dd, hd=hd: e_.tensor_add(
                    out=sPrev[:, hd:hd + 1], in0=sI2.ap()[:, dd, 7:8],
                    in1=sBl[:, hd, 7:8]),
                    [tprevn[dd][1], tok_pexp_h, t_prev])

            if tau == 0:
                for d in range(4):
                    hd = hd0 + d
                    tok_qkF[hd] = V(lambda e_, hd=hd, d=d: e_.tensor_tensor(
                        out=sQF[:, hd, :], in0=sQF[:, hd, :],
                        in1=sEpF[:, d, :], op=ALU.mult),
                        [tok_qkF[hd], tok_epF])
                    tok_qkF[hd] = V(lambda e_, hd=hd, d=d: e_.tensor_tensor(
                        out=sKF[:, hd, :], in0=sKF[:, hd, :],
                        in1=sEmF[:, d, :], op=ALU.mult),
                        [tok_qkF[hd], tok_emF])

            if _stg < 2:
                continue
            tok_qe = {}
            tok_kd = {}
            last_qd = []
            for d in range(4):
                sl = d % 2
                for half in range(2):
                    bank = (d * 2 + half) % 2
                    tm = MM(pb[bank][:, 0:512],
                            [(sWqs[sl][:, k, :],
                              sXfm[:, k, half * 512:(half + 1) * 512])
                             for k in range(NKC)],
                            [tws_q[d], pb_free[bank]])
                    te = V(lambda e_, bank=bank, d=d, half=half: e_.tensor_tensor(
                        out=sQe[:, d, half * 512:(half + 1) * 512],
                        in0=pb[bank][:, 0:512],
                        in1=sEp[:, d, half * 512:(half + 1) * 512], op=ALU.mult),
                        [tm, tok_ep[d]])
                    tok_qe[(d, half)] = te
                    pb_free[bank] = te
                    for cc_ in range(4):
                        c = half * 4 + cc_
                        last_qd.append(V(lambda e_, d=d, c=c, hd=hd0 + d:
                                         e_.tensor_scalar(
                            out=sQd[:, d, c * 128:(c + 1) * 128],
                            in0=sQe[:, d, c * 128:(c + 1) * 128],
                            scalar1=sPexp[:, hd, c:c + 1],
                            scalar2=None, op0=ALU.mult),
                            [te, tok_pexp_h, tok_spill_qd]))
                    bank2 = 2 + (d * 2 + half) % 2
                    tm2 = MM(pb[bank2][:, 0:512],
                             [(sWks[sl][:, k, :],
                               sXfm[:, k, half * 512:(half + 1) * 512])
                              for k in range(NKC)],
                             [tws_k[d], pb_free[bank2]])
                    te2 = V(lambda e_, bank2=bank2, d=d, half=half: e_.tensor_tensor(
                        out=sKd[:, d, half * 512:(half + 1) * 512],
                        in0=pb[bank2][:, 0:512],
                        in1=sEm[:, d, half * 512:(half + 1) * 512], op=ALU.mult),
                        [tm2, tok_em[d]])
                    tok_kd[(d, half)] = te2
                    pb_free[bank2] = te2
                if d + 2 < 4:
                    tws_q[d + 2] = P.dma("sync", sWqs[sl].ap(), dwqbf[h, d + 2],
                                         [tm] + t_awb[("q", h)], sem=f"swq{sl}")
                    tws_k[d + 2] = P.dma("sync", sWks[sl].ap(), dwkbf[h, d + 2],
                                         [tm2] + t_awb[("k", h)], sem=f"swk{sl}")
            tok_wq_slot[0] = tok_qe[(2, 1)]
            tok_wq_slot[1] = tok_qe[(3, 1)]
            tok_wk_slot[0] = tok_kd[(2, 1)]
            tok_wk_slot[1] = tok_kd[(3, 1)]

            tok_spill_qd = P.dma("gpsimd", dqd[tau, h],
                                 sQd.ap().rearrange("p a b -> p (a b)"), last_qd,
                                 sem="sqd")

            if _stg < 3:
                continue
            tok_v = {}
            tok_gp = {}
            for c in range(NCH1):
                bank = c % 2
                tm = MM(pb[bank][:, 0:256],
                        [(sXfm[:, k, c * 128:(c + 1) * 128], sWvs[slvg][:, k, :])
                         for k in range(NKC)],
                        [tws_v, pb_free[bank]])
                tok_v[c] = SC(lambda e_, bank=bank, c=c: e_.copy(
                    out=sV1[:, c, :], in_=pb[bank][:, 0:256]), [tm])
                pb_free[bank] = tok_v[c]
                bank2 = 2 + c % 2
                tm2 = MM(pb[bank2][:, 0:256],
                         [(sXfm[:, k, c * 128:(c + 1) * 128], sWgs[slvg][:, k, :])
                          for k in range(NKC)],
                         [tws_g, pb_free[bank2]])
                tok_gp[c] = SC(lambda e_, bank2=bank2, c=c: e_.copy(
                    out=sG1[:, c, :], in_=pb[bank2][:, 0:256]),
                    [tm2, tok_spill_g])
                pb_free[bank2] = tok_gp[c]
                if c == NCH1 - 1:
                    tok_wv_slot[slvg] = tok_v[c]
                    tok_wg_slot[slvg] = tok_gp[c]
                    last_vg_mm = tm2
            tok_spill_g = P.dma("gpsimd", dgs[tau, h],
                                sG1.ap().rearrange("p a b -> p (a b)"),
                                [tok_gp[c] for c in range(NCH1)], sem="sgl")

            if _stg < 4:
                P.fence("vector"); P.fence("scalar")
                continue
            tok_olc = []
            tok_kkT_evac = None
            tok_Ard = None
            for c in range(NCH1):
                if tau == 0 and c == 0:
                    tA = MM(pb[2][:, 0:128],
                            [(sKF[:, hd0 + d, :], sQF[:, hd0 + d, :])
                             for d in range(4)],
                            [tok_qkF[hd0 + d] for d in range(4)] + [pb_free[2]])
                else:
                    tA = MM(pb[2][:, 0:128],
                            [(sKd[:, d, c * 128:(c + 1) * 128],
                              sQe[:, d, c * 128:(c + 1) * 128]) for d in range(4)],
                            [tok_kd[(d, c // 4)] for d in range(4)] +
                            [tok_qe[(d, c // 4)] for d in range(4)] + [pb_free[2]])
                tAm = V(lambda e_: e_.tensor_tensor(
                    out=sAT[:], in0=pb[2][:, 0:128], in1=sMask[:], op=ALU.mult),
                    [tA, t_mask, tok_Ard])
                pb_free[2] = tAm
                if _stg < 5:
                    continue
                tO = MM(pb[3][:, 0:256],
                        [(sAT[:], sV1[:, c, :])] +
                        [(sQe[:, d, c * 128:(c + 1) * 128], sS[:, hd0 + d, :])
                         for d in range(4)],
                        [tAm, tok_v[c], pb_free[3], t_s0] +
                        [tok_Sevac[hd0 + d] for d in range(4)])
                tOe = SC(lambda e_, c=c: e_.copy(
                    out=sOl[:, c, :], in_=pb[3][:, 0:256]),
                    [tO, tok_spill_ol])
                pb_free[3] = tOe
                tok_olc.append(tOe)
                tok_Ard = tO
                if _stg < 6:
                    continue
                tkk = []
                tdg = []
                for d in range(4):
                    tkk.append(V(lambda e_, d=d, c=c, hd=hd0 + d: e_.tensor_scalar(
                        out=sKkf[:, d, :], in0=sKd[:, d, c * 128:(c + 1) * 128],
                        scalar1=sEbl[:, hd, c:c + 1], scalar2=None, op0=ALU.mult),
                        [tok_kd[(d, c // 4)], tok_ebl[d], tok_kkT_evac]))
                    tdg.append(V(lambda e_, d=d, c=c, hd=hd0 + d: e_.tensor_scalar(
                        out=sDiag[:, d, :], in0=sId[:],
                        scalar1=sEbl[:, hd, c:c + 1], scalar2=None, op0=ALU.mult),
                        [t_id, tok_ebl[d]]))
                pbkk = pb[6].ap().bitcast(BF16)
                ttr = []
                for d in range(4):
                    ttr.append(TR(pbkk[:, d * 128:(d + 1) * 128], sKkf[:, d, :],
                                  [tkk[d], pb_free[6]]))
                tok_kkT_evac = V(lambda e_: e_.tensor_copy(
                    out=sKkT[:], in_=pbkk[:, 0:512]), [ttr[-1]])
                pb_free[6] = tok_kkT_evac
                if _stg < 7:
                    continue
                smm = []
                for d in range(4):
                    bank = 4 + d // 2
                    off = (d % 2) * 256
                    smm.append(MM(
                        pb[bank][:, off:off + 256],
                        [(sDiag[:, d, :], sS[:, hd0 + d, :]),
                         (sKkT[:, d * 128:(d + 1) * 128], sV1[:, c, :])],
                        [tdg[d], tok_kkT_evac, tO, tok_Sevac[hd0 + d],
                         pb_free[bank]]))
                for d in range(4):
                    bank = 4 + d // 2
                    off = (d % 2) * 256
                    if d < 2:
                        tok_Sevac[hd0 + d] = V(
                            lambda e_, bank=bank, off=off, hd=hd0 + d:
                            e_.tensor_copy(out=sS[:, hd, :],
                                           in_=pb[bank][:, off:off + 256]),
                            [smm[0], smm[1]])
                    else:
                        tok_Sevac[hd0 + d] = SC(
                            lambda e_, bank=bank, off=off, hd=hd0 + d:
                            e_.copy(out=sS[:, hd, :],
                                    in_=pb[bank][:, off:off + 256]),
                            [smm[2], smm[3]])
                pb_free[4] = tok_Sevac[hd0 + 1]
                pb_free[5] = tok_Sevac[hd0 + 3]

            tok_spill_ol = P.dma("gpsimd", dol[tau, h],
                                 sOl.ap().rearrange("p a b -> p (a b)"), tok_olc,
                                 sem="sol")

            prev_vf = P.fence("vector")
            prev_sf = P.fence("scalar")
            if h == H - 1:
                tok_xfm_free = last_vg_mm

        if "p1" in dump and tau == 0:
            P.dma("gpsimd", dumpten("qF", [128, 16, CH]), sQF.ap(),
                  [tok_qkF[hd] for hd in range(16)], sem="sdmp")
            P.dma("gpsimd", dumpten("kF", [128, 16, CH]), sKF.ap(),
                  [tok_qkF[hd] for hd in range(16)], sem="sdmp")
            wfin = [prev_vf, prev_sf, tok_spill_qd, tok_spill_ol, tok_spill_g]
            P.dma("gpsimd", dumpten("qe", [128, 4, TS], BF16), sQe.ap(), wfin, sem="sdmp")
            P.dma("gpsimd", dumpten("kd", [128, 4, TS], BF16), sKd.ap(), wfin, sem="sdmp")
            P.dma("gpsimd", dumpten("v1", [128, NCH1, DV], BF16), sV1.ap(), wfin, sem="sdmp")
            P.dma("gpsimd", dumpten("g1", [128, NCH1, DV], BF16), sG1.ap(), wfin, sem="sdmp")
            P.dma("gpsimd", dumpten("ol", [128, NCH1, DV], BF16), sOl.ap(), wfin, sem="sdmp")
            P.dma("gpsimd", dumpten("qd", [128, 4, TS], BF16), sQd.ap(), wfin, sem="sdmp")
            P.dma("gpsimd", dumpten("S1", [128, 16, DV], BF16), sS.ap(), wfin, sem="sdmp")
            P.dma("gpsimd", dumpten("prev", [128, 16]), sPrev[:], wfin, sem="sdmp")
            P.dma("gpsimd", dumpten("ebl", [128, 16, NCH1]), sEbl.ap(), wfin, sem="sdmp")
            P.dma("gpsimd", dumpten("pexp", [128, 16, NCH1]), sPexp.ap(), wfin, sem="sdmp")
            P.dma("gpsimd", dumpten("bl", [128, 16, NCH1]), sBl.ap(), wfin, sem="sdmp")
            P.dma("gpsimd", dumpten("bb", [128, TS]), sBb[:], wfin, sem="sdmp")

    last_S = list(tok_Sevac)
    t_cca = P.dma("gpsimd", dccinA, sPrev[:], [t_prev], sem="scc")
    t_ccz = P.dma("gpsimd", dccinZ,
                  sS.ap().rearrange("p a b -> p (a b)"), last_S + [t_cca],
                  sem="scc")

    if "z" in dump:
        P.dma("gpsimd", dumpten("A", [128, 16]), sPrev[:], [t_prev], sem="sdmp")
        P.dma("gpsimd", dumpten("Z", [128, 16, DV], BF16), sS.ap(), last_S, sem="sdmp")

    if upto == "p1":
        for sem, q in P.dma_sems.items():
            if P.counts.get(sem, 0) > 0:
                P.do(q, [], waits=[(sem, P.counts[sem], "none", 16)])
        st1.close()
        st.close()
        P.emit()
        return nc

    t_cc = P.do("gpsimd",
                lambda e_: e_.collective_compute(
                    "AllGather", ALU.bypass,
                    replica_groups=[[0, 1, 2, 3], [4, 5, 6, 7]],
                    ins=[dccinA.opt()], outs=[dccoutA.opt()]),
                [t_cca], inc="cc")
    t_cc = P.do("gpsimd",
                lambda e_: e_.collective_compute(
                    "AllGather", ALU.bypass,
                    replica_groups=[[0, 1, 2, 3], [4, 5, 6, 7]],
                    ins=[dccinZ.opt()], outs=[dccoutZ.opt()]),
                [t_ccz, t_cc], inc="cc")

    st1.close()
    P.barrier()

    stset = ExitStack()
    es = stset.enter_context
    sAga = es(nc.sbuf_tensor("sAga", [128, 4, 16], F32))
    sC = es(nc.sbuf_tensor("sC", [128, 3, 16], F32))
    sD = es(nc.sbuf_tensor("sD", [128, 16], F32))
    sDmc = es(nc.sbuf_tensor("sDmc", [128, 16], F32))
    sCoef = es(nc.sbuf_tensor("sCoef", [128, 3, 16], F32))
    sZld = [es(nc.sbuf_tensor(f"sZld{i}", [128, 4096], BF16)) for i in range(2)]

    t_aga = P.dma("gpsimd", sAga[:],
                  dccoutA.rearrange("(r p) m -> p r m", p=128), [t_cc], sem="sag")
    t_c = V(lambda e_: e_.tensor_copy(out=sC[:, 0, :], in_=sAga[:, 0, :]), [t_aga])
    t_c = V(lambda e_: e_.tensor_add(out=sC[:, 1, :], in0=sC[:, 0, :],
                                     in1=sAga[:, 1, :]), [t_c])
    t_c = V(lambda e_: e_.tensor_add(out=sC[:, 2, :], in0=sC[:, 1, :],
                                     in1=sAga[:, 2, :]), [t_c])
    t_d = V(lambda e_: e_.tensor_scalar(
        out=sD[:], in0=sAga[:, 0, :], scalar1=sSel01[:, 0:1], scalar2=None,
        op0=ALU.mult), [t_aga, t_sel])
    for i in (1, 2):
        t_d = V(lambda e_, i=i: e_.scalar_tensor_tensor(
            out=sD[:], in0=sAga[:, i, :], scalar=sSel01[:, i:i + 1], in1=sD[:],
            op0=ALU.mult, op1=ALU.add), [t_d])
    t_coef = []
    for i in range(3):
        tdm = V(lambda e_, i=i: e_.tensor_sub(
            out=sDmc[:], in0=sD[:], in1=sC[:, i, :]),
            [t_d, t_c, t_coef[-1] if t_coef else None])
        tcp = SC(lambda e_, i=i: e_.activation(
            out=sCoef[:, i, :], in_=sDmc[:], func=AF.Exp,
            scale=sSelsc[:, i:i + 1], bias=sSelbi[:, i:i + 1]),
            [tdm, t_selsc, t_selbi])
        t_coef.append(V(lambda e_, i=i: e_.tensor_scalar(
            out=sCoef[:, i, :], in0=sCoef[:, i, :],
            scalar1=sSel01[:, i:i + 1], scalar2=None, op0=ALU.mult), [tcp]))

    t_sacc = V(lambda e_: e_.memset(sS.ap().rearrange("p a b -> p (a b)"), 0.0),
               [t_cc])
    t_zld = [None, None]
    for i in range(3):
        b = i % 2
        t_zld[b] = P.dma("gpsimd", sZld[b][:],
                         dccoutZ[i * 128:(i + 1) * 128, :],
                         [t_cc, t_zld[b], t_sacc if i < 2 else None],
                         sem=f"sz{b}")
        for hd in range(16):
            t_sacc = V(lambda e_, i=i, hd=hd, b=b: e_.scalar_tensor_tensor(
                out=sS[:, hd, :], in0=sZld[b][:, hd * 256:(hd + 1) * 256],
                scalar=sCoef[:, i, hd:hd + 1], in1=sS[:, hd, :],
                op0=ALU.mult, op1=ALU.add),
                [t_zld[b], t_coef[i], t_sacc])
    tok_Sin = t_sacc

    if "sin" in dump:
        P.dma("gpsimd", dumpten("Sin", [128, 16, DV], BF16), sS.ap(), [tok_Sin], sem="sdmp")
    if upto == "cc":
        for sem, q in P.dma_sems.items():
            if P.counts.get(sem, 0) > 0:
                P.do(q, [], waits=[(sem, P.counts[sem], "none", 16)])
        stset.close()
        st.close()
        P.emit()
        return nc

    stset.close()
    P.barrier()

    st2 = ExitStack()
    e2 = st2.enter_context
    sX1tm = e2(nc.sbuf_tensor("sX1tm", [128, 4, D], BF16))
    sX1fm = e2(nc.sbuf_tensor("sX1fm", [128, 16, TF], BF16))
    sR2 = e2(nc.sbuf_tensor("sR2", [128, 4, D], BF16))
    sSqD = e2(nc.sbuf_tensor("sSqD", [128, D], BF16))
    sSsq = e2(nc.sbuf_tensor("sSsq", [128, 16], F32))
    sRstd = e2(nc.sbuf_tensor("sRstd", [128, 16], F32))
    sVar = e2(nc.sbuf_tensor("sVar", [128, 16], F32))
    sSd = e2(nc.sbuf_tensor("sSd", [128, 16], F32))
    sLNa = e2(nc.sbuf_tensor("sLNa", [128, 4, 4], F32))
    sLNs = e2(nc.sbuf_tensor("sLNs", [128, 4], F32))
    sLNq = e2(nc.sbuf_tensor("sLNq", [128, 4], F32))
    sLNm = e2(nc.sbuf_tensor("sLNm", [128, 4], F32))
    sLNr = e2(nc.sbuf_tensor("sLNr", [128, 4], F32))
    sLNt = e2(nc.sbuf_tensor("sLNt", [128, 4], F32))
    sLNv = e2(nc.sbuf_tensor("sLNv", [128, 4], F32))
    sWgj = [e2(nc.sbuf_tensor(f"sWgj{i}", [128, 16, 128], BF16)) for i in range(2)]
    sWuj = [e2(nc.sbuf_tensor(f"sWuj{i}", [128, 16, 128], BF16)) for i in range(2)]
    sWdk = [e2(nc.sbuf_tensor(f"sWdk{i}", [128, 1024], BF16)) for i in range(2)]
    sOut = [e2(nc.sbuf_tensor(f"sOut{i}", [128, D], F32)) for i in range(2)]

    slot_wgj = [None, None]
    slot_wuj = [None, None]
    slot_wdk = [None, None]
    slot_out = [None, None]
    tok_prevB = []

    for t2 in range(NT2):
        tau = t2 // 2
        lc0 = (t2 % 2) * 4
        tk0 = t2 * TF

        stA = ExitStack()
        eA = stA.enter_context
        sQds = [eA(nc.sbuf_tensor(f"sQds{i}_{t2}", [128, 4, TF], BF16)) for i in range(2)]
        sOls = [eA(nc.sbuf_tensor(f"sOls{i}_{t2}", [128, 4, DV], BF16)) for i in range(2)]
        sGs = [eA(nc.sbuf_tensor(f"sGs{i}_{t2}", [128, 4, DV], BF16)) for i in range(2)]
        sOfull = eA(nc.sbuf_tensor(f"sOfull_{t2}", [128, 16, DV], BF16))
        sSgw = eA(nc.sbuf_tensor(f"sSgw_{t2}", [128, 16, DV], BF16))
        sTmp = eA(nc.sbuf_tensor(f"sTmp_{t2}", [128, DV], BF16))
        sOtm = eA(nc.sbuf_tensor(f"sOtm_{t2}", [128, 4, VD], BF16))
        sOfm = eA(nc.sbuf_tensor(f"sOfm_{t2}", [128, 8, TF], BF16))
        sXtm2 = eA(nc.sbuf_tensor(f"sXtm2_{t2}", [128, 4, D], BF16))
        sWos = [eA(nc.sbuf_tensor(f"sWos{i}_{t2}", [128, 8, 512], BF16)) for i in range(2)]

        slot_qd2 = [None, None]
        slot_ol2 = [None, None]
        slot_g2 = [None, None]
        t_sq = []
        t_ofull = {}
        t_sgw = {}
        for h in range(H):
            b = h % 2
            tqd = P.dma("gpsimd", sQds[b].ap(),
                        dqd[tau, h][:, :, (t2 % 2) * TF:(t2 % 2) * TF + TF],
                        [slot_qd2[b], tok_spill_qd] + tok_prevB, sem=f"rq{b}")
            tol = P.dma("gpsimd", sOls[b].ap(), dol[tau, h][:, lc0:lc0 + 4, :],
                        [slot_ol2[b], tok_spill_ol] + tok_prevB, sem=f"ro{b}")
            tgl = P.dma("gpsimd", sGs[b].ap(), dgs[tau, h][:, lc0:lc0 + 4, :],
                        [slot_g2[b], tok_spill_g] + tok_prevB, sem=f"rg{b}")
            last_corr = None
            for tc in range(4):
                i = h * 4 + tc
                bank = tc % 2
                tm = MM(pb[bank][:, 0:256],
                        [(sQds[b][:, d, tc * 128:(tc + 1) * 128],
                          sS[:, h * 4 + d, :]) for d in range(4)],
                        [tqd, tok_Sin, pb_free[bank]])
                last_corr = tm
                tof = V(lambda e_, bank=bank, i=i, b=b, tc=tc:
                        e_.scalar_tensor_tensor(
                    out=sOfull[:, i, :], in0=pb[bank][:, 0:256], scalar=0.0,
                    in1=sOls[b][:, tc, :], op0=ALU.bypass, op1=ALU.add),
                    [tm, tol])
                pb_free[bank] = tof
                t_ofull[i] = tof
                t_sq.append(SC(lambda e_, i=i: e_.activation(
                    out=sSqD[:, 0:DV], in_=sOfull[:, i, :], func=AF.Square,
                    accum_out=sSsq[:, i:i + 1]), [tof]))
                tsg = SC(lambda e_, i=i, b=b, tc=tc: e_.activation(
                    out=sSgw[:, i, :], in_=sGs[b][:, tc, :], func=AF.Silu),
                    [tgl])
                t_sgw[i] = V(lambda e_, i=i: e_.tensor_tensor(
                    out=sSgw[:, i, :], in0=sSgw[:, i, :], in1=sWnB[:],
                    op=ALU.mult), [tsg, t_wnb])
            slot_qd2[b] = last_corr
            slot_ol2[b] = t_ofull[h * 4 + 3]
            slot_g2[b] = t_sgw[h * 4 + 3]

        tv_ = V(lambda e_: e_.tensor_scalar(
            out=sVar[:], in0=sSsq[:], scalar1=1.0 / DV, scalar2=None,
            op0=ALU.mult), t_sq)
        ts_ = SC(lambda e_: e_.activation(out=sSd[:], in_=sVar[:],
                                          func=AF.Ln, bias=sEps[:, 0:1], scale=1.0),
                 [tv_, t_eps])
        t_rstd = SC(lambda e_: e_.activation(out=sRstd[:], in_=sSd[:],
                                             func=AF.Exp, scale=-0.5), [ts_])

        t_otm = {}
        prev_otm = None
        for h in range(H):
            for tc in range(4):
                i = h * 4 + tc
                tt = V(lambda e_, i=i: e_.tensor_scalar(
                    out=sTmp[:], in0=sOfull[:, i, :], scalar1=sRstd[:, i:i + 1],
                    scalar2=None, op0=ALU.mult), [t_rstd, t_ofull[i], prev_otm])
                t_otm[(tc, h)] = V(lambda e_, i=i, tc=tc, h=h: e_.tensor_tensor(
                    out=sOtm[:, tc, h * DV:(h + 1) * DV], in0=sTmp[:],
                    in1=sSgw[:, i, :], op=ALU.mult), [tt, t_sgw[i]])
                prev_otm = t_otm[(tc, h)]

        t_ofm = {}
        for tc in range(4):
            pbk = pb[6 + tc % 2].ap().bitcast(BF16)
            ttr = []
            for v in range(8):
                ttr.append(TR(pbk[:, v * 128:(v + 1) * 128],
                              sOtm[:, tc, v * 128:(v + 1) * 128],
                              [t_otm[(tc, hh)] for hh in range(H)] +
                              [pb_free[6 + tc % 2]]))
            t_ofm[tc] = V(lambda e_, tc=tc, pbk=pbk: e_.tensor_copy(
                out=sOfm.ap()[:, 0:8, tc * 128:(tc + 1) * 128],
                in_=pbk[:, 0:1024].rearrange("p (a b) -> p a b", a=8)),
                [ttr[-1]])
            pb_free[6 + tc % 2] = t_ofm[tc]

        t_xtm = []
        for tc in range(4):
            t_xtm.append(P.dma(
                "sync", sXtm2[:, tc, :],
                dxbf[tk0 + tc * 128: tk0 + (tc + 1) * 128, :],
                [t_xbf[(tk0 + tc * 128) // 128]] + tok_prevB, sem="rxt"))

        t_x1 = {}
        slot_wo = [None, None]
        for og in range(4):
            b = og % 2
            two = P.dma("sync", sWos[b].ap(), dwobf[og],
                        [slot_wo[b]] + t_awb[("o", og)] + tok_prevB,
                        sem=f"rwo{b}")
            last_womm = None
            for tc in range(4):
                bank = (og * 4 + tc) % 4
                tm = MM(pb[bank][:, 0:512],
                        [(sOfm[:, k, tc * 128:(tc + 1) * 128], sWos[b][:, k, :])
                         for k in range(8)],
                        [t_ofm[tc], two, pb_free[bank]])
                last_womm = tm
                te = V(lambda e_, bank=bank, tc=tc, og=og: e_.scalar_tensor_tensor(
                    out=sX1tm[:, tc, og * 512:(og + 1) * 512],
                    in0=pb[bank][:, 0:512], scalar=0.0,
                    in1=sXtm2[:, tc, og * 512:(og + 1) * 512],
                    op0=ALU.bypass, op1=ALU.add,
                    accum_out=sLNa[:, tc, og:og + 1]),
                    [tm] + t_xtm)
                pb_free[bank] = te
                t_x1[(og, tc)] = te
            slot_wo[b] = last_womm

        t_ln1 = []
        for tc in range(4):
            tsum = V(lambda e_, tc=tc: e_.reduce_sum(
                out=sLNs[:, tc:tc + 1], in_=sLNa[:, tc, :], axis=AXX),
                [t_x1[(og2, tc)] for og2 in range(4)])
            tssq = SC(lambda e_, tc=tc: e_.activation(
                out=sSqD[:], in_=sX1tm[:, tc, :], func=AF.Square,
                accum_out=sLNq[:, tc:tc + 1]),
                [t_x1[(og2, tc)] for og2 in range(4)] + [t_sq[-1]])
            tm_ = V(lambda e_, tc=tc: e_.tensor_scalar(
                out=sLNm[:, tc:tc + 1], in0=sLNs[:, tc:tc + 1],
                scalar1=1.0 / D, scalar2=None, op0=ALU.mult), [tsum])
            tms = V(lambda e_, tc=tc: e_.tensor_mul(
                out=sLNt[:, tc:tc + 1], in0=sLNm[:, tc:tc + 1],
                in1=sLNm[:, tc:tc + 1]), [tm_])
            tvq = V(lambda e_, tc=tc: e_.tensor_scalar(
                out=sLNv[:, tc:tc + 1], in0=sLNq[:, tc:tc + 1],
                scalar1=1.0 / D, scalar2=None, op0=ALU.mult), [tssq])
            tvar = V(lambda e_, tc=tc: e_.tensor_sub(
                out=sLNr[:, tc:tc + 1], in0=sLNv[:, tc:tc + 1],
                in1=sLNt[:, tc:tc + 1]), [tvq, tms])
            tsq2 = SC(lambda e_, tc=tc: e_.activation(
                out=sLNt[:, tc:tc + 1], in_=sLNr[:, tc:tc + 1], func=AF.Ln,
                bias=sEps[:, 0:1], scale=1.0), [tvar, t_eps])
            trc = SC(lambda e_, tc=tc: e_.activation(
                out=sLNr[:, tc:tc + 1], in_=sLNt[:, tc:tc + 1], func=AF.Exp,
                scale=-0.5), [tsq2])
            t_ln1.append(V(lambda e_, tc=tc: e_.tensor_scalar(
                out=sX1tm[:, tc, :], in0=sX1tm[:, tc, :],
                scalar1=sLNm[:, tc:tc + 1], scalar2=sLNr[:, tc:tc + 1],
                op0=ALU.subtract, op1=ALU.mult), [trc]))

        if "x1" in dump and t2 == 0:
            P.dma("gpsimd", dumpten("wnb", [128, DV], BF16), sWnB[:], [t_wnb], sem="sdmp")
            P.dma("gpsimd", dumpten("ols1", [128, 4, DV], BF16), sOls[1].ap(),
                  list(t_ofull.values()), sem="sdmp")
            P.dma("gpsimd", dumpten("gs0", [128, 4, DV], BF16), sGs[0].ap(),
                  list(t_sgw.values()), sem="sdmp")
            P.dma("gpsimd", dumpten("rstd", [128, 16]), sRstd[:],
                  [t_rstd] + list(t_otm.values()), sem="sdmp")
            P.dma("gpsimd", dumpten("sgw", [128, 16, DV], BF16), sSgw.ap(),
                  list(t_sgw.values()) + list(t_otm.values()), sem="sdmp")
            P.dma("gpsimd", dumpten("ssq", [128, 16]), sSsq[:], t_sq + [t_rstd], sem="sdmp")
            P.dma("gpsimd", dumpten("x1tm", [128, 4, D], BF16), sX1tm.ap(), t_ln1, sem="sdmp")
            P.dma("gpsimd", dumpten("ofull", [128, 16, DV], BF16), sOfull.ap(),
                  list(t_otm.values()), sem="sdmp")
            P.dma("gpsimd", dumpten("otm", [128, 4, VD], BF16), sOtm.ap(),
                  list(t_otm.values()) + [t_ofm[3]], sem="sdmp")

        t_x1fm = []
        for tc in range(4):
            for half in range(2):
                pbk = pb[6 + half].ap().bitcast(BF16)
                ttr = []
                for f in range(8):
                    fc = half * 8 + f
                    ttr.append(TR(pbk[:, f * 128:(f + 1) * 128],
                                  sX1tm[:, tc, fc * 128:(fc + 1) * 128],
                                  [t_ln1[tc], pb_free[6 + half]]))
                tev = V(lambda e_, tc=tc, half=half, pbk=pbk: e_.tensor_copy(
                    out=sX1fm.ap()[:, half * 8:(half + 1) * 8,
                                   tc * 128:(tc + 1) * 128],
                    in_=pbk[:, 0:1024].rearrange("p (a b) -> p a b", a=8)),
                    [ttr[-1]])
                pb_free[6 + half] = tev
                t_x1fm.append(tev)

        stA.close()

        stB = ExitStack()
        eB = stB.enter_context
        sFuse = eB(nc.sbuf_tensor(f"sFuse_{t2}", [128, NJ, TF], BF16))
        sSilu = eB(nc.sbuf_tensor(f"sSilu_{t2}", [128, TF], BF16))

        t_fuse = {}
        prev_tfu = None
        for j in range(NJ):
            b = j % 2
            twg = P.dma("sync", sWgj[b].ap(), dwgfb[j],
                        [slot_wgj[b], t_ffnw[("g", j)]], sem=f"rwg{b}")
            twu = P.dma("sync", sWuj[b].ap(), dwufb[j],
                        [slot_wuj[b], t_ffnw[("u", j)]], sem=f"rwu{b}")
            bank = (j % 2) * 2
            bank2 = bank + 1
            tmg = MM(pb[bank][:, 0:512],
                     [(sWgj[b][:, k, :], sX1fm[:, k, :]) for k in range(NKC)],
                     t_x1fm + [twg, pb_free[bank]])
            tmu = MM(pb[bank2][:, 0:512],
                     [(sWuj[b][:, k, :], sX1fm[:, k, :]) for k in range(NKC)],
                     [twu, pb_free[bank2]])
            slot_wgj[b] = tmg
            slot_wuj[b] = tmu
            tsl = SC(lambda e_, bank=bank, j=j: e_.activation(
                out=sSilu[:], in_=pb[bank][:, 0:512], func=AF.Silu,
                scale=sSg[:, j:j + 1]), [tmg, t_sg, prev_tfu])
            pb_free[bank] = tsl
            tfu = V(lambda e_, bank2=bank2, j=j: e_.scalar_tensor_tensor(
                out=sFuse[:, j, :], in0=pb[bank2][:, 0:512],
                scalar=sSu[:, j:j + 1], in1=sSilu[:],
                op0=ALU.mult, op1=ALU.mult), [tmu, tsl, t_su])
            pb_free[bank2] = tfu
            t_fuse[j] = tfu
            prev_tfu = tfu
        all_fuse = [t_fuse[j] for j in range(NJ)]

        t_r2 = {}
        for ps in range(2):
            groups = [(tc, ps * 2 + og) for og in range(2) for tc in range(4)]
            smm = [None] * 8
            for k in range(64):
                b = k % 2
                twd = P.dma("sync", sWdk[b][:],
                            dwdfb[k][:, ps * 1024:(ps + 1) * 1024],
                            [slot_wdk[b], t_ffnw[("d", k)]], sem=f"rwd{b}")
                for gi, (tc, og) in enumerate(groups):
                    w = [twd]
                    if k == 0:
                        w = w + all_fuse + [pb_free[gi]]
                    l = sFuse[:, k, tc * 128:(tc + 1) * 128]
                    r = sWdk[b].ap()[:, (og - ps * 2) * 512:(og - ps * 2 + 1) * 512]
                    smm[gi] = P.do(
                        "tensor",
                        lambda e_, l=l, r=r, o=pb[gi], s=(k == 0), tt=(k == 63):
                        e_.matmul(o[:, 0:512], l, r, start=s, stop=tt),
                        w, inc="pe")
                slot_wdk[b] = smm[7]
            for gi, (tc, og) in enumerate(groups):
                te = V(lambda e_, gi=gi, tc=tc, og=og: e_.scalar_tensor_tensor(
                    out=sR2[:, tc, og * 512:(og + 1) * 512], in0=pb[gi][:, 0:512],
                    scalar=0.0, in1=sX1tm[:, tc, og * 512:(og + 1) * 512],
                    op0=ALU.bypass, op1=ALU.add,
                    accum_out=sLNa[:, tc, og:og + 1]),
                    [smm[gi], t_ln1[tc]])
                pb_free[gi] = te
                t_r2[(og, tc)] = te

        t_store = []
        for tc in range(4):
            tsum = V(lambda e_, tc=tc: e_.reduce_sum(
                out=sLNs[:, tc:tc + 1], in_=sLNa[:, tc, :], axis=AXX),
                [t_r2[(og2, tc)] for og2 in range(4)])
            tssq = SC(lambda e_, tc=tc: e_.activation(
                out=sSqD[:], in_=sR2[:, tc, :], func=AF.Square,
                accum_out=sLNq[:, tc:tc + 1]),
                [t_r2[(og2, tc)] for og2 in range(4)])
            tm_ = V(lambda e_, tc=tc: e_.tensor_scalar(
                out=sLNm[:, tc:tc + 1], in0=sLNs[:, tc:tc + 1],
                scalar1=1.0 / D, scalar2=None, op0=ALU.mult), [tsum])
            tms = V(lambda e_, tc=tc: e_.tensor_mul(
                out=sLNt[:, tc:tc + 1], in0=sLNm[:, tc:tc + 1],
                in1=sLNm[:, tc:tc + 1]), [tm_])
            tvq = V(lambda e_, tc=tc: e_.tensor_scalar(
                out=sLNv[:, tc:tc + 1], in0=sLNq[:, tc:tc + 1],
                scalar1=1.0 / D, scalar2=None, op0=ALU.mult), [tssq])
            tvar = V(lambda e_, tc=tc: e_.tensor_sub(
                out=sLNr[:, tc:tc + 1], in0=sLNv[:, tc:tc + 1],
                in1=sLNt[:, tc:tc + 1]), [tvq, tms])
            tsq2 = SC(lambda e_, tc=tc: e_.activation(
                out=sLNt[:, tc:tc + 1], in_=sLNr[:, tc:tc + 1], func=AF.Ln,
                bias=sEps[:, 0:1], scale=1.0), [tvar, t_eps])
            trc = SC(lambda e_, tc=tc: e_.activation(
                out=sLNr[:, tc:tc + 1], in_=sLNt[:, tc:tc + 1], func=AF.Exp,
                scale=-0.5), [tsq2])
            b = tc % 2
            tap = V(lambda e_, tc=tc, b=b: e_.tensor_scalar(
                out=sOut[b][:], in0=sR2[:, tc, :],
                scalar1=sLNm[:, tc:tc + 1], scalar2=sLNr[:, tc:tc + 1],
                op0=ALU.subtract, op1=ALU.mult), [trc, slot_out[b]])
            tst = P.dma("sync", dy[tk0 + tc * 128: tk0 + (tc + 1) * 128, :],
                        sOut[b][:], [tap], sem=f"ry{b}")
            slot_out[b] = tst
            t_store.append(tst)

        if "ln2" in dump and t2 == 0:
            P.dma("gpsimd", dumpten("r2", [128, 4, D], BF16), sR2.ap(),
                  list(t_r2.values()) + t_store, sem="sdmp")
            P.dma("gpsimd", dumpten("ln2a", [128, 4, 4]), sLNa.ap(),
                  list(t_r2.values()) + t_store, sem="sdmp")
            P.dma("gpsimd", dumpten("ln2s", [128, 4]), sLNs[:], t_store, sem="sdmp")
            P.dma("gpsimd", dumpten("ln2q", [128, 4]), sLNq[:], t_store, sem="sdmp")
            P.dma("gpsimd", dumpten("ln2m", [128, 4]), sLNm[:], t_store, sem="sdmp")
            P.dma("gpsimd", dumpten("ln2r", [128, 4]), sLNr[:], t_store, sem="sdmp")
            P.dma("gpsimd", dumpten("fuse0", [128, 8, TF], BF16),
                  sFuse.ap()[:, 0:8, :], list(t_fuse.values()), sem="sdmp")
        stB.close()
        tok_prevB = [smm[7], t_fuse[NJ - 1]]

    for sem, q in P.dma_sems.items():
        if P.counts.get(sem, 0) > 0:
            P.do(q, [], waits=[(sem, P.counts[sem], "none", 16)])
    P.do("sync", [], waits=t_store)

    st2.close()
    st.close()
    P.emit()
    return nc


_BUILD_CACHE = {}


def _get_nc():
    if "nc" not in _BUILD_CACHE:
        _BUILD_CACHE["nc"] = build("full")
    return _BUILD_CACHE["nc"]


def prep_shared(i):
    d = {}
    d["wq"] = np.ascontiguousarray(np.asarray(i["L0_attn_Wq"], dtype=np.float32))
    d["wk"] = np.ascontiguousarray(np.asarray(i["L0_attn_Wk"], dtype=np.float32))
    d["wv"] = np.ascontiguousarray(np.asarray(i["L0_attn_Wv"], dtype=np.float32))
    d["wg"] = np.ascontiguousarray(np.asarray(i["L0_attn_Wg"], dtype=np.float32))
    d["wgk1"] = np.ascontiguousarray(np.asarray(i["L0_attn_Wgk1"], dtype=np.float32))
    d["wgk2"] = np.ascontiguousarray(np.asarray(i["L0_attn_Wgk2"], dtype=np.float32))
    d["wo"] = np.ascontiguousarray(np.asarray(i["L0_attn_Wo"], dtype=np.float32))
    d["bgk2pf"] = np.ascontiguousarray(
        np.asarray(i["L0_attn_bgk2"], dtype=np.float32).reshape(16, 128).T)
    d["wnormr"] = np.ascontiguousarray(
        np.asarray(i["L0_attn_Wnorm"], dtype=np.float32).reshape(1, DV))
    d["sgpf"] = np.ascontiguousarray(
        np.asarray(i["L0_ffn_s_gate"], dtype=np.float32).reshape(NJ, 128).T)
    d["supf"] = np.ascontiguousarray(
        np.asarray(i["L0_ffn_s_up"], dtype=np.float32).reshape(NJ, 128).T)
    d["sdnr"] = np.ascontiguousarray(
        np.asarray(i["L0_ffn_s_down"], dtype=np.float32).reshape(1, D))
    wgt = np.asarray(i["L0_ffn_w_gate"])
    d["wg8t"] = np.ascontiguousarray(
        wgt.reshape(16, 128, NJ, 128).transpose(2, 1, 0, 3)
        .reshape(NJ, 128, 16 * 128))
    wut = np.asarray(i["L0_ffn_w_up"])
    d["wu8t"] = np.ascontiguousarray(
        wut.reshape(16, 128, NJ, 128).transpose(2, 1, 0, 3)
        .reshape(NJ, 128, 16 * 128))
    d["wd8"] = np.ascontiguousarray(
        np.asarray(i["L0_ffn_w_down"]).reshape(64, 128, D))
    d["identb"] = np.eye(128, dtype=NPBF)
    s_ = np.arange(128)[:, None]
    t_ = np.arange(128)[None, :]
    d["maskut"] = (s_ <= t_).astype(NPBF)
    return d


def make_in_maps(ins):
    shared = prep_shared(ins)
    x = np.asarray(ins["L0_attn_x"], dtype=np.float32)
    in_maps = []
    for c in range(8):
        r = c % 4
        m = dict(shared)
        m["xs"] = np.ascontiguousarray(x[c // 4, r * TPC:(r + 1) * TPC, :])
        sel = (np.arange(3) < r).astype(np.float32)
        m["sel01"] = np.ascontiguousarray(
            np.broadcast_to(sel, (128, 3)).astype(np.float32))
        m["selsc"] = np.ascontiguousarray(
            np.broadcast_to(sel * INV16, (128, 3)).astype(np.float32))
        m["selbi"] = np.ascontiguousarray(
            np.broadcast_to(-(1.0 - sel) * 100.0, (128, 3)).astype(np.float32))
        in_maps.append(m)
    return in_maps


def kernel(L0_attn_x, L0_attn_Wq, L0_attn_Wk, L0_attn_Wv, L0_attn_Wgk1,
           L0_attn_Wgk2, L0_attn_bgk2, L0_attn_Wg, L0_attn_Wnorm, L0_attn_Wo,
           L0_ffn_w_gate, L0_ffn_w_up, L0_ffn_w_down, L0_ffn_s_gate,
           L0_ffn_s_up, L0_ffn_s_down):
    ins = dict(L0_attn_x=L0_attn_x, L0_attn_Wq=L0_attn_Wq, L0_attn_Wk=L0_attn_Wk,
               L0_attn_Wv=L0_attn_Wv, L0_attn_Wgk1=L0_attn_Wgk1,
               L0_attn_Wgk2=L0_attn_Wgk2, L0_attn_bgk2=L0_attn_bgk2,
               L0_attn_Wg=L0_attn_Wg, L0_attn_Wnorm=L0_attn_Wnorm,
               L0_attn_Wo=L0_attn_Wo, L0_ffn_w_gate=L0_ffn_w_gate,
               L0_ffn_w_up=L0_ffn_w_up, L0_ffn_w_down=L0_ffn_w_down,
               L0_ffn_s_gate=L0_ffn_s_gate, L0_ffn_s_up=L0_ffn_s_up,
               L0_ffn_s_down=L0_ffn_s_down)
    in_maps = make_in_maps(ins)
    nc = _get_nc()
    res = run_bass_kernel_spmd(nc, in_maps, core_ids=list(range(8)))
    out = np.empty((2, 8192, D), dtype=np.float32)
    for c in range(8):
        out[c // 4, (c % 4) * TPC:(c % 4 + 1) * TPC, :] = res.results[c]["y"]
    return out


# version 11
# speedup vs baseline: 1.1360x; 1.0360x over previous
import numpy as np
import ml_dtypes
from contextlib import ExitStack

import concourse.bass as bass
import concourse.mybir as mybir
from concourse.bass_utils import run_bass_kernel_spmd

F32 = mybir.dt.float32
BF16 = mybir.dt.bfloat16
I8 = mybir.dt.int8
AF = mybir.ActivationFunctionType
ALU = mybir.AluOpType
NPBF = ml_dtypes.bfloat16

ENG = ("sync", "tensor", "vector", "scalar", "gpsimd")

TPC = 2048
D = 2048
H = 4
DK = 512
DV = 256
KD = 2048
VD = 1024
LOW = 16
INTER = 8192
CH = 128
TS = 1024
NT1 = TPC // TS
NCH1 = TS // CH
TF = 512
NT2 = TPC // TF
NKC = D // 128
NJ = INTER // 128
SCALE = 1.0 / float(np.sqrt(512.0))
EPS = 1e-5
INV16 = 1.0 / 16.0
AXX = mybir.AxisListType.X


class Prog:

    def __init__(self, nc):
        self.nc = nc
        self.ops = []
        self.counts = {}
        self.sem_order = []
        self.dma_sems = {}

    def _sem(self, name):
        if name not in self.counts:
            self.counts[name] = 0
            self.sem_order.append(name)
        return name

    def do(self, engine, fns, waits=(), inc=None, inc_n=1, noskip=False):
        if fns is None:
            fns = []
        if not isinstance(fns, (list, tuple)):
            fns = [fns]
        ws = [w for w in waits if w is not None]
        tok = None
        if inc is not None:
            self._sem(inc)
            self.counts[inc] += inc_n
            tok = (inc, self.counts[inc], engine, inc_n)
        self.ops.append((engine, list(fns), ws, (inc, inc_n) if inc else None,
                         noskip))
        return tok

    def dma(self, queue, out, in_, waits=(), sem=None, **kw):
        sem = sem or ("q_" + queue)
        self.dma_sems.setdefault(sem, queue)
        return self.do(
            queue,
            lambda e, out=out, in_=in_, kw=kw: e.dma_start(out=out, in_=in_, **kw),
            waits, inc=sem, inc_n=16,
        )

    def fence(self, engine):
        return self.do(engine, [], inc={"vector": "vv", "scalar": "sc",
                                        "tensor": "pe", "gpsimd": "gg",
                                        "sync": "sy"}[engine])

    def barrier(self):
        for sem, q in self.dma_sems.items():
            if self.counts.get(sem, 0) > 0:
                self.do(q, [], waits=[(sem, self.counts[sem], "none", 16)])
        toks = [self.do(e, [], inc="bar") for e in ENG]
        for e in ENG:
            self.do(e, [], waits=toks)

    def emit(self):
        nc = self.nc
        with ExitStack() as stx:
            sems = {n: stx.enter_context(nc.semaphore(n)) for n in self.sem_order}
            waited = {e: {} for e in ENG}
            with nc.Block() as block:
                def mk(engname):
                    def body(eng):
                        for (e, fns, ws, inc, noskip) in self.ops:
                            if e != engname:
                                continue
                            need = {}
                            for (sem, val, prod, n) in ws:
                                if waited[engname].get(sem, 0) >= val:
                                    continue
                                if need.get(sem, 0) < val:
                                    need[sem] = val
                            for sem, val in need.items():
                                eng.wait_ge(sems[sem], val)
                                waited[engname][sem] = val
                            inst = None
                            for fn in fns:
                                inst = fn(eng)
                            if inc is not None:
                                if inst is None:
                                    inst = eng.nop()
                                inst.then_inc(sems[inc[0]], inc[1])
                    return body
                block.sync(mk("sync"))
                block.tensor(mk("tensor"))
                block.vector(mk("vector"))
                block.scalar(mk("scalar"))
                block.gpsimd(mk("gpsimd"))


def build(upto="full", dump=(), lim=None):
    nc = bass.Bass("TRN2", target_bir_lowering=False)
    P = Prog(nc)
    dump = set(dump)

    dx = nc.dram_tensor("xs", [TPC, D], F32, kind="ExternalInput").ap()
    dwq = nc.dram_tensor("wq", [D, KD], F32, kind="ExternalInput").ap()
    dwk = nc.dram_tensor("wk", [D, KD], F32, kind="ExternalInput").ap()
    dwv = nc.dram_tensor("wv", [D, VD], F32, kind="ExternalInput").ap()
    dwg = nc.dram_tensor("wg", [D, VD], F32, kind="ExternalInput").ap()
    dwgk1 = nc.dram_tensor("wgk1", [D, LOW], F32, kind="ExternalInput").ap()
    dwgk2 = nc.dram_tensor("wgk2", [LOW, KD], F32, kind="ExternalInput").ap()
    dwo = nc.dram_tensor("wo", [VD, D], F32, kind="ExternalInput").ap()
    dbgk2 = nc.dram_tensor("bgk2pf", [128, 16], F32, kind="ExternalInput").ap()
    dwnorm = nc.dram_tensor("wnormr", [1, DV], F32, kind="ExternalInput").ap()
    dsg = nc.dram_tensor("sgpf", [128, NJ], F32, kind="ExternalInput").ap()
    dsu = nc.dram_tensor("supf", [128, NJ], F32, kind="ExternalInput").ap()
    dsdn = nc.dram_tensor("sdnr", [1, D], F32, kind="ExternalInput").ap()
    dwg8 = nc.dram_tensor("wg8t", [NJ, 128, 16 * 128], I8, kind="ExternalInput").ap()
    dwu8 = nc.dram_tensor("wu8t", [NJ, 128, 16 * 128], I8, kind="ExternalInput").ap()
    dwd8 = nc.dram_tensor("wd8", [64, 128, D], I8, kind="ExternalInput").ap()
    dident = nc.dram_tensor("identb", [128, 128], BF16, kind="ExternalInput").ap()
    dmask = nc.dram_tensor("maskut", [128, 128], BF16, kind="ExternalInput").ap()
    dsel01 = nc.dram_tensor("sel01", [128, 3], F32, kind="ExternalInput").ap()
    dselsc = nc.dram_tensor("selsc", [128, 3], F32, kind="ExternalInput").ap()
    dselbi = nc.dram_tensor("selbi", [128, 3], F32, kind="ExternalInput").ap()

    dy = nc.dram_tensor("y", [TPC, D], F32, kind="ExternalOutput").ap()

    dxbf = nc.dram_tensor("xbf", [TPC, D], BF16).ap()
    dwqbf = nc.dram_tensor("wqbf", [H, 4, 128, 16 * 128], BF16).ap()
    dwkbf = nc.dram_tensor("wkbf", [H, 4, 128, 16 * 128], BF16).ap()
    dwvbf = nc.dram_tensor("wvbf", [H, 128, 16 * 256], BF16).ap()
    dwgbf = nc.dram_tensor("wgbf", [H, 128, 16 * 256], BF16).ap()
    dwobf = nc.dram_tensor("wobf", [4, 128, 8 * 512], BF16).ap()
    dwgfb = nc.dram_tensor("wgfb", [NJ, 128, 16 * 128], BF16).ap()
    dwufb = nc.dram_tensor("wufb", [NJ, 128, 16 * 128], BF16).ap()
    dwdfb = nc.dram_tensor("wdfb", [64, 128, D], BF16).ap()
    dqd = nc.dram_tensor("qd_scr", [NT1, H, 128, 4, TS], BF16).ap()
    dol = nc.dram_tensor("ol_scr", [NT1, H, 128, NCH1, DV], BF16).ap()
    dgs = nc.dram_tensor("g_scr", [NT1, H, 128, NCH1, DV], BF16).ap()
    dccinA = nc.dram_tensor("ccinA", [128, 16], F32).ap()
    dccoutA = nc.dram_tensor("ccoutA", [4 * 128, 16], F32).ap()
    dccinZ = nc.dram_tensor("ccinZ", [128, 4096], BF16).ap()
    dccoutZ = nc.dram_tensor("ccoutZ", [4 * 128, 4096], BF16).ap()

    ddump = {}
    U16 = mybir.dt.uint16

    def dumpten(name, shape, dtype=F32):
        if name not in ddump:
            if dtype == BF16:
                t = nc.dram_tensor("dbg_" + name, shape, U16,
                                   kind="ExternalOutput").ap()
                ddump[name] = t.bitcast(BF16)
            else:
                ddump[name] = nc.dram_tensor(
                    "dbg_" + name, shape, dtype, kind="ExternalOutput").ap()
        return ddump[name]

    st = ExitStack()
    e = st.enter_context

    sS = e(nc.sbuf_tensor("sS", [128, 16, DV], BF16))
    sId = e(nc.sbuf_tensor("sId", [128, 128], BF16))
    sMask = e(nc.sbuf_tensor("sMask", [128, 128], BF16))
    sWnB = e(nc.sbuf_tensor("sWnB", [128, DV], BF16))
    sGk1 = e(nc.sbuf_tensor("sGk1", [128, 16, LOW], BF16))
    sGk2 = e(nc.sbuf_tensor("sGk2", [LOW, KD], BF16))
    sBgk2 = e(nc.sbuf_tensor("sBgk2", [128, 16], F32))
    sSg = e(nc.sbuf_tensor("sSg", [128, NJ], F32))
    sSu = e(nc.sbuf_tensor("sSu", [128, NJ], F32))
    sSel01 = e(nc.sbuf_tensor("sSel01", [128, 3], F32))
    sSelsc = e(nc.sbuf_tensor("sSelsc", [128, 3], F32))
    sSelbi = e(nc.sbuf_tensor("sSelbi", [128, 3], F32))
    sPrev = e(nc.sbuf_tensor("sPrev", [128, 16], F32))
    sBl = e(nc.sbuf_tensor("sBl", [128, 16, NCH1], F32))
    sEbl = e(nc.sbuf_tensor("sEbl", [128, 16, NCH1], F32))
    sPexp = e(nc.sbuf_tensor("sPexp", [128, 16, NCH1], F32))
    sZb = e(nc.sbuf_tensor("sZb", [128, 128], F32))
    sOneB = e(nc.sbuf_tensor("sOneB", [1, 512], BF16))
    sEight = e(nc.sbuf_tensor("sEight", [128, NCH1], F32))
    sI1 = e(nc.sbuf_tensor("sI1", [128, 4, NCH1], F32))
    sI2 = e(nc.sbuf_tensor("sI2", [128, 4, NCH1], F32))
    sEps = e(nc.sbuf_tensor("sEps", [128, 1], F32))

    pb = [e(nc.psum_tensor(f"pb{i}", [128, 512], F32)) for i in range(8)]
    pb_free = [None] * 8

    def V(fn, waits=()):
        return P.do("vector", fn, waits, inc="vv")

    def Vs(fn, waits=()):
        return P.do("vector", fn, waits, inc="vv", noskip=True)

    def SC(fn, waits=()):
        return P.do("scalar", fn, waits, inc="sc")

    def GP(fn, waits=()):
        return P.do("gpsimd", fn, waits, inc="gg")

    def MM(out_ap, pairs, waits=()):
        n = len(pairs)
        fns = []
        for i, (l, r) in enumerate(pairs):
            fns.append(lambda eng, l=l, r=r, s=(i == 0), t=(i == n - 1), o=out_ap:
                       eng.matmul(o, l, r, start=s, stop=t))
        return P.do("tensor", fns, waits, inc="pe")

    def TR(out_ap, in_ap, waits=()):
        return P.do("tensor",
                    lambda eng, o=out_ap, i=in_ap: eng.transpose(o, i, sId[:]),
                    waits, inc="pe")

    t_id = P.dma("sync", sId[:], dident, sem="c_id")
    t_mask = P.dma("sync", sMask[:], dmask, sem="c_mk")
    t_bgk2 = P.dma("sync", sBgk2[:], dbgk2, sem="c_bg")
    t_sg = P.dma("sync", sSg[:], dsg, sem="c_sg")
    t_su = P.dma("sync", sSu[:], dsu, sem="c_su")
    t_sel = P.dma("sync", sSel01[:], dsel01, sem="c_s0")
    t_selsc = P.dma("sync", sSelsc[:], dselsc, sem="c_s1")
    t_selbi = P.dma("sync", sSelbi[:], dselbi, sem="c_s2")
    t_zb = V(lambda e_: e_.memset(sZb[:], 0.0))
    t_ones = V(lambda e_: e_.memset(sOneB[:], 1.0))
    t_prev = V(lambda e_: e_.memset(sPrev[:], 0.0))
    t_s0 = V(lambda e_: e_.memset(sS.ap().rearrange("p a b -> p (a b)"), 0.0))
    t_eps = V(lambda e_: e_.memset(sEps[:], EPS))

    st0 = ExitStack()
    e0 = st0.enter_context
    sXst = [e0(nc.sbuf_tensor(f"sXst{i}", [128, 2, D], F32)) for i in range(2)]
    sXbfs = [e0(nc.sbuf_tensor(f"sXbfs{i}", [128, 2, D], BF16)) for i in range(2)]
    sWbig = [e0(nc.sbuf_tensor(f"sWbig{i}", [128, 8, 512], F32)) for i in range(2)]
    sWcol = [e0(nc.sbuf_tensor(f"sWcol{i}", [128, 16, 512], BF16)) for i in range(2)]
    sW8 = [e0(nc.sbuf_tensor(f"sW8_{i}", [128, 2, D], I8)) for i in range(2)]
    sW8b = [e0(nc.sbuf_tensor(f"sW8b{i}", [128, 2, D], BF16)) for i in range(2)]
    sRow = e0(nc.sbuf_tensor("sRow", [1, D], F32))
    sRowB = e0(nc.sbuf_tensor("sRowB", [1, D], BF16))
    sSdnB = e0(nc.sbuf_tensor("sSdnB", [128, D], BF16))
    sGkSt = e0(nc.sbuf_tensor("sGkSt", [128, 16 * LOW], F32))
    sGk2St = e0(nc.sbuf_tensor("sGk2St", [LOW, KD], F32))

    t = P.dma("sync", sGkSt.ap().rearrange("p (k f) -> p k f", k=16),
              dwgk1.rearrange("(k p) f -> p k f", p=128), sem="c_g1")
    t_gk1 = V(lambda e_: e_.tensor_copy(
        out=sGk1.ap().rearrange("p a b -> p (a b)"), in_=sGkSt[:]), [t])
    t = P.dma("sync", sGk2St[:], dwgk2, sem="c_g2")
    t_gk2 = V(lambda e_: e_.tensor_copy(out=sGk2[:], in_=sGk2St[:]), [t])

    t = P.dma("sync", sRow[0:1, 0:DV], dwnorm, sem="c_wn")
    t_wnrow = V(lambda e_: e_.tensor_copy(out=sRowB[0:1, 0:DV],
                                          in_=sRow[0:1, 0:DV]), [t, t_ones])
    t = MM(pb[0][:, 0:DV], [(sOneB[0:1, 0:128], sRowB[0:1, 0:DV])], [t_wnrow])
    t_wnb = V(lambda e_: e_.tensor_copy(out=sWnB[:], in_=pb[0][:, 0:DV]), [t])
    pb_free[0] = t_wnb

    t = P.dma("sync", sRow[:], dsdn, [t_wnrow], sem="c_sd")
    t = V(lambda e_: e_.tensor_copy(out=sRowB[:], in_=sRow[:]), [t, t_wnb])
    t_sdnb = None
    for i in range(4):
        tm = MM(pb[1][:, 0:512],
                [(sOneB[0:1, 0:128], sRowB[0:1, i * 512:(i + 1) * 512])],
                [t, pb_free[1]])
        t_sdnb = V(lambda e_, i=i: e_.tensor_copy(
            out=sSdnB[:, i * 512:(i + 1) * 512], in_=pb[1][:, 0:512]), [tm])
        pb_free[1] = t_sdnb

    dxr = dx.rearrange("(i p) m -> p i m", p=128)
    dxbr = dxbf.rearrange("(i p) m -> p i m", p=128)
    t_xbf2 = []
    prev_st = [None, None]
    for i in range(8):
        b = i % 2
        tl = P.dma("sync", sXst[b].ap(), dxr[:, i * 2:(i + 1) * 2, :],
                   [prev_st[b]], sem=f"sxl{b}")
        tc = SC(lambda e_, b=b: e_.copy(
            out=sXbfs[b].ap().rearrange("p a b -> p (a b)"),
            in_=sXst[b].ap().rearrange("p a b -> p (a b)")), [tl])
        ts_ = P.dma("gpsimd" if b == 0 else "scalar",
                    dxbr[:, i * 2:(i + 1) * 2, :], sXbfs[b].ap(), [tc],
                    sem=f"sxs{b}")
        prev_st[b] = ts_
        t_xbf2.append(ts_)
    t_xbf = [t_xbf2[i // 2] for i in range(16)]

    t_awb = {}
    prev_big = [None, None]
    prev_col = [None, None]
    _blk = [0]

    def cast_big(tag, wsrc, col0, ncol, scale, outs):
        i = _blk[0]
        _blk[0] += 1
        cb = i % 2
        wr = wsrc.rearrange("(k p) m -> p k m", p=128)
        tcs = []
        for hh in range(2):
            tl = P.dma("sync",
                       sWbig[cb].ap()[:, hh * 4:(hh + 1) * 4, 0:ncol]
                       if False else sWbig[cb].ap()[:, 0:8, 0:ncol],
                       wr[:, hh * 8:(hh + 1) * 8, col0:col0 + ncol],
                       [prev_big[cb] if hh == 0 else tcs[-1]], sem=f"swl{cb}")
            eng = "vector"
            tc = P.do(eng, lambda e_, cb=cb, hh=hh, ncol=ncol, sc_=scale:
                      (e_.tensor_copy(
                          out=sWcol[cb].ap()[:, hh * 8:(hh + 1) * 8, 0:ncol],
                          in_=sWbig[cb].ap()[:, 0:8, 0:ncol])
                       if sc_ is None else e_.tensor_scalar(
                          out=sWcol[cb].ap()[:, hh * 8:(hh + 1) * 8, 0:ncol],
                          in0=sWbig[cb].ap()[:, 0:8, 0:ncol],
                          scalar1=sc_, scalar2=None, op0=ALU.mult)),
                      [tl, prev_col[cb]],
                      inc="vv" if eng == "vector" else "gg")
            tcs.append(tc)
            prev_big[cb] = tc
        stoks = []
        for (oap, srcsl) in outs:
            stoks.append(P.dma("gpsimd" if i % 2 == 0 else "scalar", oap,
                               sWcol[cb].ap()[:, :, srcsl[0]:srcsl[1]],
                               tcs + ([stoks[-1]] if stoks else [prev_col[cb]]),
                               sem=f"sws{cb}"))
        prev_col[cb] = stoks[-1]
        t_awb[tag] = stoks
        return stoks

    for h in range(H):
        cast_big(("q", h), dwq, h * 512, 512, SCALE,
                 [(dwqbf[h, d2].rearrange("p (a b) -> p a b", a=16),
                   (d2 * 128, (d2 + 1) * 128)) for d2 in range(4)])
        cast_big(("k", h), dwk, h * 512, 512, None,
                 [(dwkbf[h, d2].rearrange("p (a b) -> p a b", a=16),
                   (d2 * 128, (d2 + 1) * 128)) for d2 in range(4)])
        cast_big(("v", h), dwv, h * 256, 256, None,
                 [(dwvbf[h].rearrange("p (a b) -> p a b", a=16), (0, 256))])
        cast_big(("g", h), dwg, h * 256, 256, None,
                 [(dwgbf[h].rearrange("p (a b) -> p a b", a=16), (0, 256))])
    for og in range(4):
        i = _blk[0]
        _blk[0] += 1
        cb = i % 2
        wr = dwo.rearrange("(k p) m -> p k m", p=128)
        tl = P.dma("sync", sWbig[cb].ap(),
                   wr[:, :, og * 512:(og + 1) * 512],
                   [prev_big[cb]], sem=f"swl{cb}")
        tc = V(lambda e_, cb=cb: e_.tensor_copy(
            out=sWcol[cb].ap()[:, 0:8, :].rearrange("p a b -> p (a b)"),
            in_=sWbig[cb].ap().rearrange("p a b -> p (a b)")),
            [tl, prev_col[cb]])
        prev_big[cb] = tc
        ts_ = P.dma("gpsimd" if i % 2 == 0 else "scalar",
                    dwobf[og].rearrange("p (a b) -> p a b", a=8),
                    sWcol[cb].ap()[:, 0:8, :], [tc], sem=f"sws{cb}")
        prev_col[cb] = ts_
        t_awb[("o", og)] = [ts_]

    t_ffnw = {}
    prev8 = [None, None]
    prev8b = [None, None]
    dgr = dwg8.rearrange("j p m -> p j m")
    dur = dwu8.rearrange("j p m -> p j m")
    dgfr = dwgfb.rearrange("j p m -> p j m")
    dufr = dwufb.rearrange("j p m -> p j m")
    ddr = dwd8.rearrange("k p m -> p k m")
    ddfr = dwdfb.rearrange("k p m -> p k m")
    for jj in range(NJ // 2):
        j = jj * 2
        b = jj % 2
        tl = P.dma("sync", sW8[b].ap(), dgr[:, j:j + 2, :], [prev8[b]],
                   sem=f"s8l{b}")
        tc = V(lambda e_, b=b: e_.tensor_copy(
            out=sW8b[b].ap().rearrange("p a b -> p (a b)"),
            in_=sW8[b].ap().rearrange("p a b -> p (a b)")), [tl, prev8b[b]])
        ts_ = P.dma("gpsimd", dgfr[:, j:j + 2, :], sW8b[b].ap(), [tc],
                    sem=f"s8s{b}")
        tl2 = P.dma("sync", sW8[b].ap(), dur[:, j:j + 2, :], [tc],
                    sem=f"s8l{b}")
        tc2 = SC(lambda e_, b=b: e_.copy(
            out=sW8b[b].ap().rearrange("p a b -> p (a b)"),
            in_=sW8[b].ap().rearrange("p a b -> p (a b)")), [tl2, ts_])
        ts2 = P.dma("scalar", dufr[:, j:j + 2, :], sW8b[b].ap(), [tc2],
                    sem=f"s8t{b}")
        prev8[b] = tc2
        prev8b[b] = ts2
        t_ffnw[("g", j)] = ts_
        t_ffnw[("g", j + 1)] = ts_
        t_ffnw[("u", j)] = ts2
        t_ffnw[("u", j + 1)] = ts2
    prevd = [prev8b[0], prev8b[1]]
    prevdb = [prev8b[0], prev8b[1]]
    for kk2 in range(32):
        k = kk2 * 2
        b = kk2 % 2
        tl = P.dma("sync", sW8[b].ap(), ddr[:, k:k + 2, :], [prevd[b]],
                   sem=f"sdl{b}")
        tc = None
        for k2 in range(2):
            tc = V(lambda e_, b=b, k2=k2: e_.tensor_tensor(
                out=sW8b[b][:, k2, :], in0=sW8[b][:, k2, :], in1=sSdnB[:],
                op=ALU.mult),
                [tl, prevdb[b], t_sdnb, tc])
        ts_ = P.dma("scalar", ddfr[:, k:k + 2, :], sW8b[b].ap(), [tc],
                    sem=f"sdt{b}")
        prevd[b] = tc
        prevdb[b] = ts_
        t_ffnw[("d", k)] = ts_
        t_ffnw[("d", k + 1)] = ts_
    st0.close()
    P.barrier()

    if upto == "p0":
        for sem, q in P.dma_sems.items():
            if P.counts.get(sem, 0) > 0:
                P.do(q, [], waits=[(sem, P.counts[sem], "none", 16)])
        st.close()
        P.emit()
        return nc

    st1 = ExitStack()
    e1 = st1.enter_context
    sXfm = e1(nc.sbuf_tensor("sXfm", [128, NKC, TS], BF16))
    sEp = e1(nc.sbuf_tensor("sEp", [128, 4, TS], BF16))
    sEm = e1(nc.sbuf_tensor("sEm", [128, 4, TS], BF16))
    sQe = e1(nc.sbuf_tensor("sQe", [128, 4, TS], BF16))
    sKd = e1(nc.sbuf_tensor("sKd", [128, 4, TS], BF16))
    sQF = e1(nc.sbuf_tensor("sQF", [128, 16, CH], F32))
    sXf32 = e1(nc.sbuf_tensor("sXf32", [128, NKC, CH], F32))
    sEpF = e1(nc.sbuf_tensor("sEpF", [128, 4, CH], F32))
    sEmF = e1(nc.sbuf_tensor("sEmF", [128, 4, CH], F32))
    sUb = e1(nc.sbuf_tensor("sUb", [128, 4, 512], BF16))
    sKF = e1(nc.sbuf_tensor("sKF", [128, 16, CH], F32))
    sIdF = e1(nc.sbuf_tensor("sIdF", [128, 128], F32))
    sQd = e1(nc.sbuf_tensor("sQd", [128, 4, TS], BF16))
    sV1 = e1(nc.sbuf_tensor("sV1", [128, NCH1, DV], BF16))
    sG1 = e1(nc.sbuf_tensor("sG1", [128, NCH1, DV], BF16))
    sOl = e1(nc.sbuf_tensor("sOl", [128, NCH1, DV], BF16))
    sGb = e1(nc.sbuf_tensor("sGb", [128, TS], F32))
    sBb = e1(nc.sbuf_tensor("sBb", [128, TS], F32))
    sSig = e1(nc.sbuf_tensor("sSig", [128, 512], F32))
    sT1 = e1(nc.sbuf_tensor("sT1", [LOW, TS], BF16))
    sAT = e1(nc.sbuf_tensor("sAT", [128, 128], BF16))
    sKkf = e1(nc.sbuf_tensor("sKkf", [128, 4, 128], BF16))
    sKkT = e1(nc.sbuf_tensor("sKkT", [128, 512], BF16))
    sDiag = e1(nc.sbuf_tensor("sDiag", [128, 4, 128], BF16))
    sWqs = [e1(nc.sbuf_tensor(f"sWqs{i}", [128, 16, 128], BF16)) for i in range(2)]
    sWks = [e1(nc.sbuf_tensor(f"sWks{i}", [128, 16, 128], BF16)) for i in range(2)]
    sWvs = [e1(nc.sbuf_tensor(f"sWvs{i}", [128, 16, 256], BF16)) for i in range(1)]
    sWgs = [e1(nc.sbuf_tensor(f"sWgs{i}", [128, 16, 256], BF16)) for i in range(1)]

    tok_xfm_free = None
    tok_wq_slot = [None, None]
    tok_wk_slot = [None, None]
    tok_wv_slot = [None, None]
    tok_wg_slot = [None, None]
    tok_spill_qd = None
    tok_spill_ol = None
    tok_spill_g = None
    tok_Sevac = [None] * 16
    tok_t1 = None
    tok_prep = None
    prev_vf = None
    prev_sf = None

    _ntau, _nh, _stg = (lim + (99,))[:3] if lim else (NT1, H, 99)
    for tau in range(_ntau):
        t_xfm = []
        for fc in range(NKC):
            w = [t_xbf[tau * 8 + s_] for s_ in range(8)]
            if fc == 0:
                w = w + [tok_xfm_free]
            t_xfm.append(P.dma(
                "sync", sXfm[:, fc, :],
                dxbf[tau * TS:(tau + 1) * TS, fc * 128:(fc + 1) * 128],
                w, transpose=True, sem="sxf"))

        for half in range(2):
            tm = MM(pb[half][0:LOW, 0:512],
                    [(sGk1[:, k, :], sXfm[:, k, half * 512:(half + 1) * 512])
                     for k in range(NKC)],
                    t_xfm + [t_gk1, pb_free[half], tok_t1])
            tok_t1 = V(lambda e_, half=half: e_.tensor_copy(
                out=sT1[:, half * 512:(half + 1) * 512],
                in_=pb[half][0:LOW, 0:512]), [tm])
            pb_free[half] = tok_t1

        if tau == 0:
            tidf = V(lambda e_: e_.tensor_copy(out=sIdF[:], in_=sId[:]), [t_id])
            tok_qkF = [None] * 16
            if True:
                sXtmF = sGb.ap()[:, 0:512]
                sWqkF = sBb.ap().rearrange("p (k m) -> p k m", k=8)
                if True:
                    tprev_e = None
                    txl = None
                    for piece in range(4):
                        txl = P.dma("sync", sXtmF,
                                    dx[0:128, piece * 512:(piece + 1) * 512],
                                    [txl if piece == 0 else ttr], sem="sxfz")
                        for f2 in range(4):
                            fc = piece * 4 + f2
                            ttr = P.do("tensor",
                                       lambda e_, f2=f2: e_.transpose(
                                           pb[0][:, 0:128],
                                           sXtmF[:, f2 * 128:(f2 + 1) * 128],
                                           sIdF[:]),
                                       [txl, tidf, pb_free[0]], inc="pe")
                            tev = V(lambda e_, fc=fc: e_.tensor_copy(
                                out=sXf32[:, fc, :], in_=pb[0][:, 0:128]),
                                [ttr, tprev_e])
                            pb_free[0] = tev
                            tprev_e = tev
                if True:
                    twf = None
                    for (wsrc, dstT, scl) in ((dwq, sQF, SCALE), (dwk, sKF, None)):
                        for hd in range(16):
                            tmf = None
                            for kh in range(2):
                                twf = P.dma(
                                    "sync", sWqkF,
                                    wsrc.rearrange("(k p) m -> p k m", p=128)
                                    [:, kh * 8:(kh + 1) * 8,
                                     hd * 128:(hd + 1) * 128],
                                    [twf], sem="swfz")
                                tmf = P.do("tensor", [
                                    (lambda e_, k=k, kh=kh, s_=(kh == 0 and k == 0),
                                     t_=(kh == 1 and k == 7):
                                     e_.matmul(pb[1][:, 0:CH], sWqkF[:, k, :],
                                               sXf32[:, kh * 8 + k, :],
                                               start=s_, stop=t_))
                                    for k in range(8)],
                                    [twf, tprev_e, pb_free[1] if kh == 0 else None],
                                    inc="pe")
                                twf = tmf
                            if scl is None:
                                tevf = V(lambda e_, hd=hd, dstT=dstT: e_.tensor_copy(
                                    out=dstT[:, hd, :], in_=pb[1][:, 0:CH]),
                                    [tmf, tok_qkF[hd]])
                            else:
                                tevf = V(lambda e_, hd=hd, dstT=dstT, scl=scl:
                                         e_.tensor_scalar(
                                    out=dstT[:, hd, :], in0=pb[1][:, 0:CH],
                                    scalar1=scl, scalar2=None, op0=ALU.mult),
                                    [tmf, tok_qkF[hd]])
                            pb_free[1] = tevf
                            tok_qkF[hd] = tevf
            tok_prep = tok_qkF[15]

        if tau == 0 and "p1" in dump:
            P.dma("gpsimd", dumpten("xf32", [128, NKC, CH]), sXf32.ap(),
                  [tprev_e], sem="sdmp")
            P.dma("gpsimd", dumpten("qFr", [128, 16, CH]), sQF.ap(),
                  [tok_qkF[hd] for hd in range(16)], sem="sdmp")

        for h in range(_nh):
            hd0 = h * 4

            tws_q = [None] * 4
            tws_k = [None] * 4
            for d in (0, 1):
                tws_q[d] = P.dma("sync", sWqs[d % 2].ap(), dwqbf[h, d],
                                 [tok_wq_slot[d % 2]] + t_awb[("q", h)],
                                 sem=f"swq{d % 2}")
                tws_k[d] = P.dma("sync", sWks[d % 2].ap(), dwkbf[h, d],
                                 [tok_wk_slot[d % 2]] + t_awb[("k", h)],
                                 sem=f"swk{d % 2}")
            slvg = 0
            tws_v = P.dma("sync", sWvs[slvg].ap(), dwvbf[h],
                          [tok_wv_slot[slvg]] + t_awb[("v", h)], sem=f"swv{slvg}")
            tws_g = P.dma("sync", sWgs[slvg].ap(), dwgbf[h],
                          [tok_wg_slot[slvg]] + t_awb[("g", h)], sem=f"swg{slvg}")

            if _stg < 1:
                continue
            tok_ub = [None] * 4
            tok_ep = {}
            tok_em = {}
            tok_ebl = {}
            tok_bl = {}
            for d in range(4):
                hd = hd0 + d
                tg = []
                for half in range(2):
                    bank = 2 + half
                    usl = (d * 2 + half) % 4
                    tm = MM(pb[bank][:, 0:512],
                            [(sGk2[:, hd * 128:(hd + 1) * 128],
                              sT1[:, half * 512:(half + 1) * 512])],
                            [tok_t1, t_gk2, pb_free[bank]])
                    tev = V(lambda e_, bank=bank, usl=usl: e_.tensor_copy(
                        out=sUb[:, usl, :], in_=pb[bank][:, 0:512]),
                        [tm, tok_ub[usl], prev_vf, prev_sf])
                    pb_free[bank] = tev
                    tsg = SC(lambda e_, usl=usl, hd=hd: e_.activation(
                        out=sSig[:], in_=sUb[:, usl, :], func=AF.Sigmoid,
                        bias=sBgk2[:, hd:hd + 1], scale=1.0),
                        [tev, t_bgk2, prev_sf])
                    tok_ub[usl] = tsg
                    tln = SC(lambda e_, half=half: e_.activation(
                        out=sGb[:, half * 512:(half + 1) * 512], in_=sSig[:],
                        func=AF.Ln),
                        [tsg, tok_prep if tau == 0 else None])
                    tg.append(tln)
                tsc = []
                for c in range(NCH1):
                    tsc.append(Vs(lambda e_, c=c: e_.tensor_tensor_scan(
                        out=sBb[:, c * 128:(c + 1) * 128],
                        data0=sGb[:, c * 128:(c + 1) * 128],
                        data1=sZb[:], initial=0.0, op0=ALU.add, op1=ALU.add),
                        [tg[c // 4], t_zb] + ([tsc[-1]] if tsc else [])))
                tok_ep[d] = SC(lambda e_, d=d: e_.activation(
                    out=sEp[:, d, :], in_=sBb[:], func=AF.Exp, scale=INV16), tsc)
                tok_em[d] = SC(lambda e_, d=d: e_.activation(
                    out=sEm[:, d, :], in_=sBb[:], func=AF.Exp, scale=-INV16), tsc)
                if tau == 0:
                    tok_epF = SC(lambda e_, d=d: e_.activation(
                        out=sEpF[:, d, :], in_=sBb[:, 0:CH], func=AF.Exp,
                        scale=INV16), tsc)
                    tok_emF = SC(lambda e_, d=d: e_.activation(
                        out=sEmF[:, d, :], in_=sBb[:, 0:CH], func=AF.Exp,
                        scale=-INV16), tsc)
                tbl = Vs(lambda e_, hd=hd: e_.tensor_copy(
                    out=sBl[:, hd, :], in_=sBb[:, 127::128]), tsc)
                tok_ebl[d] = SC(lambda e_, hd=hd: e_.activation(
                    out=sEbl[:, hd, :], in_=sBl[:, hd, :], func=AF.Exp,
                    scale=INV16), [tbl])
                tok_bl[d] = tbl

            blsl = sBl.ap()[:, hd0:hd0 + 4, :]
            w0 = [tok_bl[d] for d in range(4)] + [t_prev]
            tI = V(lambda e_, blsl=blsl: e_.tensor_copy(
                out=sI1.ap()[:, :, 0:1], in_=blsl[:, :, 0:1]), w0)
            tI = V(lambda e_, blsl=blsl: e_.tensor_add(
                out=sI1.ap()[:, :, 1:8], in0=blsl[:, :, 1:8],
                in1=blsl[:, :, 0:7]), w0 + [tI])
            tI = V(lambda e_: e_.tensor_copy(
                out=sI2.ap()[:, :, 0:2], in_=sI1.ap()[:, :, 0:2]), [tI])
            tI = V(lambda e_: e_.tensor_add(
                out=sI2.ap()[:, :, 2:8], in0=sI1.ap()[:, :, 2:8],
                in1=sI1.ap()[:, :, 0:6]), [tI])
            tI = V(lambda e_: e_.tensor_copy(
                out=sI1.ap()[:, :, 0:4], in_=sI2.ap()[:, :, 0:4]), [tI])
            tI = V(lambda e_: e_.tensor_add(
                out=sI1.ap()[:, :, 4:8], in0=sI2.ap()[:, :, 4:8],
                in1=sI2.ap()[:, :, 0:4]), [tI])
            tI = V(lambda e_, blsl=blsl: e_.tensor_sub(
                out=sI2.ap().rearrange("p a b -> p (a b)"),
                in0=sI1.ap().rearrange("p a b -> p (a b)"),
                in1=blsl.rearrange("p a b -> p (a b)")), [tI])
            tprevn = []
            for dd in range(4):
                hd = hd0 + dd
                tI2 = V(lambda e_, dd=dd, hd=hd: e_.tensor_scalar(
                    out=sI2.ap()[:, dd, :], in0=sI2.ap()[:, dd, :],
                    scalar1=sPrev[:, hd:hd + 1], scalar2=None, op0=ALU.add),
                    [tI])
                tprevn.append((hd, tI2))
            tok_pexp_h = SC(lambda e_, hd0=hd0: e_.activation(
                out=sPexp.ap()[:, hd0:hd0 + 4, :],
                in_=sI2.ap().rearrange("p a b -> p (a b)"),
                func=AF.Exp, scale=INV16), [t for _, t in tprevn])
            for dd in range(4):
                hd = hd0 + dd
                t_prev = V(lambda e_, dd=dd, hd=hd: e_.tensor_add(
                    out=sPrev[:, hd:hd + 1], in0=sI2.ap()[:, dd, 7:8],
                    in1=sBl[:, hd, 7:8]),
                    [tprevn[dd][1], tok_pexp_h, t_prev])

            if tau == 0:
                for d in range(4):
                    hd = hd0 + d
                    tok_qkF[hd] = V(lambda e_, hd=hd, d=d: e_.tensor_tensor(
                        out=sQF[:, hd, :], in0=sQF[:, hd, :],
                        in1=sEpF[:, d, :], op=ALU.mult),
                        [tok_qkF[hd], tok_epF])
                    tok_qkF[hd] = V(lambda e_, hd=hd, d=d: e_.tensor_tensor(
                        out=sKF[:, hd, :], in0=sKF[:, hd, :],
                        in1=sEmF[:, d, :], op=ALU.mult),
                        [tok_qkF[hd], tok_emF])

            if _stg < 2:
                continue
            tok_qe = {}
            tok_kd = {}
            last_qd = []
            for d in range(4):
                sl = d % 2
                for half in range(2):
                    bank = (d * 2 + half) % 2
                    tm = MM(pb[bank][:, 0:512],
                            [(sWqs[sl][:, k, :],
                              sXfm[:, k, half * 512:(half + 1) * 512])
                             for k in range(NKC)],
                            [tws_q[d], pb_free[bank]])
                    te = V(lambda e_, bank=bank, d=d, half=half: e_.tensor_tensor(
                        out=sQe[:, d, half * 512:(half + 1) * 512],
                        in0=pb[bank][:, 0:512],
                        in1=sEp[:, d, half * 512:(half + 1) * 512], op=ALU.mult),
                        [tm, tok_ep[d]])
                    tok_qe[(d, half)] = te
                    pb_free[bank] = te
                    for cc_ in range(4):
                        c = half * 4 + cc_
                        last_qd.append(V(lambda e_, d=d, c=c, hd=hd0 + d:
                                         e_.tensor_scalar(
                            out=sQd[:, d, c * 128:(c + 1) * 128],
                            in0=sQe[:, d, c * 128:(c + 1) * 128],
                            scalar1=sPexp[:, hd, c:c + 1],
                            scalar2=None, op0=ALU.mult),
                            [te, tok_pexp_h, tok_spill_qd]))
                    bank2 = 2 + (d * 2 + half) % 2
                    tm2 = MM(pb[bank2][:, 0:512],
                             [(sWks[sl][:, k, :],
                               sXfm[:, k, half * 512:(half + 1) * 512])
                              for k in range(NKC)],
                             [tws_k[d], pb_free[bank2]])
                    te2 = V(lambda e_, bank2=bank2, d=d, half=half: e_.tensor_tensor(
                        out=sKd[:, d, half * 512:(half + 1) * 512],
                        in0=pb[bank2][:, 0:512],
                        in1=sEm[:, d, half * 512:(half + 1) * 512], op=ALU.mult),
                        [tm2, tok_em[d]])
                    tok_kd[(d, half)] = te2
                    pb_free[bank2] = te2
                if d + 2 < 4:
                    tws_q[d + 2] = P.dma("sync", sWqs[sl].ap(), dwqbf[h, d + 2],
                                         [tm] + t_awb[("q", h)], sem=f"swq{sl}")
                    tws_k[d + 2] = P.dma("sync", sWks[sl].ap(), dwkbf[h, d + 2],
                                         [tm2] + t_awb[("k", h)], sem=f"swk{sl}")
            tok_wq_slot[0] = tok_qe[(2, 1)]
            tok_wq_slot[1] = tok_qe[(3, 1)]
            tok_wk_slot[0] = tok_kd[(2, 1)]
            tok_wk_slot[1] = tok_kd[(3, 1)]

            tok_spill_qd = P.dma("gpsimd", dqd[tau, h],
                                 sQd.ap().rearrange("p a b -> p (a b)"), last_qd,
                                 sem="sqd")

            if _stg < 3:
                continue
            tok_v = {}
            tok_gp = {}
            for c in range(NCH1):
                bank = c % 2
                tm = MM(pb[bank][:, 0:256],
                        [(sXfm[:, k, c * 128:(c + 1) * 128], sWvs[slvg][:, k, :])
                         for k in range(NKC)],
                        [tws_v, pb_free[bank]])
                tok_v[c] = SC(lambda e_, bank=bank, c=c: e_.copy(
                    out=sV1[:, c, :], in_=pb[bank][:, 0:256]), [tm])
                pb_free[bank] = tok_v[c]
                bank2 = 2 + c % 2
                tm2 = MM(pb[bank2][:, 0:256],
                         [(sXfm[:, k, c * 128:(c + 1) * 128], sWgs[slvg][:, k, :])
                          for k in range(NKC)],
                         [tws_g, pb_free[bank2]])
                tok_gp[c] = SC(lambda e_, bank2=bank2, c=c: e_.copy(
                    out=sG1[:, c, :], in_=pb[bank2][:, 0:256]),
                    [tm2, tok_spill_g])
                pb_free[bank2] = tok_gp[c]
                if c == NCH1 - 1:
                    tok_wv_slot[slvg] = tok_v[c]
                    tok_wg_slot[slvg] = tok_gp[c]
                    last_vg_mm = tm2
            tok_spill_g = P.dma("gpsimd", dgs[tau, h],
                                sG1.ap().rearrange("p a b -> p (a b)"),
                                [tok_gp[c] for c in range(NCH1)], sem="sgl")

            if _stg < 4:
                P.fence("vector"); P.fence("scalar")
                continue
            tok_olc = []
            tok_kkT_evac = None
            tok_Ard = None
            for c in range(NCH1):
                if tau == 0 and c == 0:
                    tA = MM(pb[2][:, 0:128],
                            [(sKF[:, hd0 + d, :], sQF[:, hd0 + d, :])
                             for d in range(4)],
                            [tok_qkF[hd0 + d] for d in range(4)] + [pb_free[2]])
                else:
                    tA = MM(pb[2][:, 0:128],
                            [(sKd[:, d, c * 128:(c + 1) * 128],
                              sQe[:, d, c * 128:(c + 1) * 128]) for d in range(4)],
                            [tok_kd[(d, c // 4)] for d in range(4)] +
                            [tok_qe[(d, c // 4)] for d in range(4)] + [pb_free[2]])
                tAm = V(lambda e_: e_.tensor_tensor(
                    out=sAT[:], in0=pb[2][:, 0:128], in1=sMask[:], op=ALU.mult),
                    [tA, t_mask, tok_Ard])
                pb_free[2] = tAm
                if _stg < 5:
                    continue
                tO = MM(pb[3][:, 0:256],
                        [(sAT[:], sV1[:, c, :])] +
                        [(sQe[:, d, c * 128:(c + 1) * 128], sS[:, hd0 + d, :])
                         for d in range(4)],
                        [tAm, tok_v[c], pb_free[3], t_s0] +
                        [tok_Sevac[hd0 + d] for d in range(4)])
                tOe = SC(lambda e_, c=c: e_.copy(
                    out=sOl[:, c, :], in_=pb[3][:, 0:256]),
                    [tO, tok_spill_ol])
                pb_free[3] = tOe
                tok_olc.append(tOe)
                tok_Ard = tO
                if _stg < 6:
                    continue
                tkk = []
                tdg = []
                for d in range(4):
                    tkk.append(V(lambda e_, d=d, c=c, hd=hd0 + d: e_.tensor_scalar(
                        out=sKkf[:, d, :], in0=sKd[:, d, c * 128:(c + 1) * 128],
                        scalar1=sEbl[:, hd, c:c + 1], scalar2=None, op0=ALU.mult),
                        [tok_kd[(d, c // 4)], tok_ebl[d], tok_kkT_evac]))
                    tdg.append(V(lambda e_, d=d, c=c, hd=hd0 + d: e_.tensor_scalar(
                        out=sDiag[:, d, :], in0=sId[:],
                        scalar1=sEbl[:, hd, c:c + 1], scalar2=None, op0=ALU.mult),
                        [t_id, tok_ebl[d]]))
                pbkk = pb[6].ap().bitcast(BF16)
                ttr = []
                for d in range(4):
                    ttr.append(TR(pbkk[:, d * 128:(d + 1) * 128], sKkf[:, d, :],
                                  [tkk[d], pb_free[6]]))
                tok_kkT_evac = V(lambda e_: e_.tensor_copy(
                    out=sKkT[:], in_=pbkk[:, 0:512]), [ttr[-1]])
                pb_free[6] = tok_kkT_evac
                if _stg < 7:
                    continue
                smm = []
                for d in range(4):
                    bank = 4 + d // 2
                    off = (d % 2) * 256
                    smm.append(MM(
                        pb[bank][:, off:off + 256],
                        [(sDiag[:, d, :], sS[:, hd0 + d, :]),
                         (sKkT[:, d * 128:(d + 1) * 128], sV1[:, c, :])],
                        [tdg[d], tok_kkT_evac, tO, tok_Sevac[hd0 + d],
                         pb_free[bank]]))
                for d in range(4):
                    bank = 4 + d // 2
                    off = (d % 2) * 256
                    if d < 2:
                        tok_Sevac[hd0 + d] = V(
                            lambda e_, bank=bank, off=off, hd=hd0 + d:
                            e_.tensor_copy(out=sS[:, hd, :],
                                           in_=pb[bank][:, off:off + 256]),
                            [smm[0], smm[1]])
                    else:
                        tok_Sevac[hd0 + d] = SC(
                            lambda e_, bank=bank, off=off, hd=hd0 + d:
                            e_.copy(out=sS[:, hd, :],
                                    in_=pb[bank][:, off:off + 256]),
                            [smm[2], smm[3]])
                pb_free[4] = tok_Sevac[hd0 + 1]
                pb_free[5] = tok_Sevac[hd0 + 3]

            tok_spill_ol = P.dma("gpsimd", dol[tau, h],
                                 sOl.ap().rearrange("p a b -> p (a b)"), tok_olc,
                                 sem="sol")

            prev_vf = P.fence("vector")
            prev_sf = P.fence("scalar")
            if h == H - 1:
                tok_xfm_free = last_vg_mm

        if "p1" in dump and tau == 0:
            P.dma("gpsimd", dumpten("qF", [128, 16, CH]), sQF.ap(),
                  [tok_qkF[hd] for hd in range(16)], sem="sdmp")
            P.dma("gpsimd", dumpten("kF", [128, 16, CH]), sKF.ap(),
                  [tok_qkF[hd] for hd in range(16)], sem="sdmp")
            wfin = [prev_vf, prev_sf, tok_spill_qd, tok_spill_ol, tok_spill_g]
            P.dma("gpsimd", dumpten("qe", [128, 4, TS], BF16), sQe.ap(), wfin, sem="sdmp")
            P.dma("gpsimd", dumpten("kd", [128, 4, TS], BF16), sKd.ap(), wfin, sem="sdmp")
            P.dma("gpsimd", dumpten("v1", [128, NCH1, DV], BF16), sV1.ap(), wfin, sem="sdmp")
            P.dma("gpsimd", dumpten("g1", [128, NCH1, DV], BF16), sG1.ap(), wfin, sem="sdmp")
            P.dma("gpsimd", dumpten("ol", [128, NCH1, DV], BF16), sOl.ap(), wfin, sem="sdmp")
            P.dma("gpsimd", dumpten("qd", [128, 4, TS], BF16), sQd.ap(), wfin, sem="sdmp")
            P.dma("gpsimd", dumpten("S1", [128, 16, DV], BF16), sS.ap(), wfin, sem="sdmp")
            P.dma("gpsimd", dumpten("prev", [128, 16]), sPrev[:], wfin, sem="sdmp")
            P.dma("gpsimd", dumpten("ebl", [128, 16, NCH1]), sEbl.ap(), wfin, sem="sdmp")
            P.dma("gpsimd", dumpten("pexp", [128, 16, NCH1]), sPexp.ap(), wfin, sem="sdmp")
            P.dma("gpsimd", dumpten("bl", [128, 16, NCH1]), sBl.ap(), wfin, sem="sdmp")
            P.dma("gpsimd", dumpten("bb", [128, TS]), sBb[:], wfin, sem="sdmp")

    last_S = list(tok_Sevac)
    t_cca = P.dma("gpsimd", dccinA, sPrev[:], [t_prev], sem="scc")
    t_ccz = P.dma("gpsimd", dccinZ,
                  sS.ap().rearrange("p a b -> p (a b)"), last_S + [t_cca],
                  sem="scc")

    if "z" in dump:
        P.dma("gpsimd", dumpten("A", [128, 16]), sPrev[:], [t_prev], sem="sdmp")
        P.dma("gpsimd", dumpten("Z", [128, 16, DV], BF16), sS.ap(), last_S, sem="sdmp")

    if upto == "p1":
        for sem, q in P.dma_sems.items():
            if P.counts.get(sem, 0) > 0:
                P.do(q, [], waits=[(sem, P.counts[sem], "none", 16)])
        st1.close()
        st.close()
        P.emit()
        return nc

    t_cc = P.do("gpsimd",
                lambda e_: e_.collective_compute(
                    "AllGather", ALU.bypass,
                    replica_groups=[[0, 1, 2, 3], [4, 5, 6, 7]],
                    ins=[dccinA.opt()], outs=[dccoutA.opt()]),
                [t_cca], inc="cc")
    t_cc = P.do("gpsimd",
                lambda e_: e_.collective_compute(
                    "AllGather", ALU.bypass,
                    replica_groups=[[0, 1, 2, 3], [4, 5, 6, 7]],
                    ins=[dccinZ.opt()], outs=[dccoutZ.opt()]),
                [t_ccz, t_cc], inc="cc")

    st1.close()
    P.barrier()

    stset = ExitStack()
    es = stset.enter_context
    sAga = es(nc.sbuf_tensor("sAga", [128, 4, 16], F32))
    sC = es(nc.sbuf_tensor("sC", [128, 3, 16], F32))
    sD = es(nc.sbuf_tensor("sD", [128, 16], F32))
    sDmc = es(nc.sbuf_tensor("sDmc", [128, 16], F32))
    sCoef = es(nc.sbuf_tensor("sCoef", [128, 3, 16], F32))
    sZld = [es(nc.sbuf_tensor(f"sZld{i}", [128, 4096], BF16)) for i in range(2)]

    t_aga = P.dma("gpsimd", sAga[:],
                  dccoutA.rearrange("(r p) m -> p r m", p=128), [t_cc], sem="sag")
    t_c = V(lambda e_: e_.tensor_copy(out=sC[:, 0, :], in_=sAga[:, 0, :]), [t_aga])
    t_c = V(lambda e_: e_.tensor_add(out=sC[:, 1, :], in0=sC[:, 0, :],
                                     in1=sAga[:, 1, :]), [t_c])
    t_c = V(lambda e_: e_.tensor_add(out=sC[:, 2, :], in0=sC[:, 1, :],
                                     in1=sAga[:, 2, :]), [t_c])
    t_d = V(lambda e_: e_.tensor_scalar(
        out=sD[:], in0=sAga[:, 0, :], scalar1=sSel01[:, 0:1], scalar2=None,
        op0=ALU.mult), [t_aga, t_sel])
    for i in (1, 2):
        t_d = V(lambda e_, i=i: e_.scalar_tensor_tensor(
            out=sD[:], in0=sAga[:, i, :], scalar=sSel01[:, i:i + 1], in1=sD[:],
            op0=ALU.mult, op1=ALU.add), [t_d])
    t_coef = []
    for i in range(3):
        tdm = V(lambda e_, i=i: e_.tensor_sub(
            out=sDmc[:], in0=sD[:], in1=sC[:, i, :]),
            [t_d, t_c, t_coef[-1] if t_coef else None])
        tcp = SC(lambda e_, i=i: e_.activation(
            out=sCoef[:, i, :], in_=sDmc[:], func=AF.Exp,
            scale=sSelsc[:, i:i + 1], bias=sSelbi[:, i:i + 1]),
            [tdm, t_selsc, t_selbi])
        t_coef.append(V(lambda e_, i=i: e_.tensor_scalar(
            out=sCoef[:, i, :], in0=sCoef[:, i, :],
            scalar1=sSel01[:, i:i + 1], scalar2=None, op0=ALU.mult), [tcp]))

    t_sacc = V(lambda e_: e_.memset(sS.ap().rearrange("p a b -> p (a b)"), 0.0),
               [t_cc])
    t_zld = [None, None]
    for i in range(3):
        b = i % 2
        t_zld[b] = P.dma("gpsimd", sZld[b][:],
                         dccoutZ[i * 128:(i + 1) * 128, :],
                         [t_cc, t_zld[b], t_sacc if i < 2 else None],
                         sem=f"sz{b}")
        for hd in range(16):
            t_sacc = V(lambda e_, i=i, hd=hd, b=b: e_.scalar_tensor_tensor(
                out=sS[:, hd, :], in0=sZld[b][:, hd * 256:(hd + 1) * 256],
                scalar=sCoef[:, i, hd:hd + 1], in1=sS[:, hd, :],
                op0=ALU.mult, op1=ALU.add),
                [t_zld[b], t_coef[i], t_sacc])
    tok_Sin = t_sacc

    if "sin" in dump:
        P.dma("gpsimd", dumpten("Sin", [128, 16, DV], BF16), sS.ap(), [tok_Sin], sem="sdmp")
    if upto == "cc":
        for sem, q in P.dma_sems.items():
            if P.counts.get(sem, 0) > 0:
                P.do(q, [], waits=[(sem, P.counts[sem], "none", 16)])
        stset.close()
        st.close()
        P.emit()
        return nc

    stset.close()
    P.barrier()

    st2 = ExitStack()
    e2 = st2.enter_context
    sX1tm = e2(nc.sbuf_tensor("sX1tm", [128, 4, D], BF16))
    sX1fm = e2(nc.sbuf_tensor("sX1fm", [128, 16, TF], BF16))
    sSqD = e2(nc.sbuf_tensor("sSqD", [128, D], BF16))
    sSsq = e2(nc.sbuf_tensor("sSsq", [128, 16], F32))
    sRstd = e2(nc.sbuf_tensor("sRstd", [128, 16], F32))
    sVar = e2(nc.sbuf_tensor("sVar", [128, 16], F32))
    sSd = e2(nc.sbuf_tensor("sSd", [128, 16], F32))
    sLNa = e2(nc.sbuf_tensor("sLNa", [128, 4, 4], F32))
    sLNs = e2(nc.sbuf_tensor("sLNs", [128, 4], F32))
    sLNq = e2(nc.sbuf_tensor("sLNq", [128, 4], F32))
    sLNm = e2(nc.sbuf_tensor("sLNm", [128, 4], F32))
    sLNr = e2(nc.sbuf_tensor("sLNr", [128, 4], F32))
    sLNt = e2(nc.sbuf_tensor("sLNt", [128, 4], F32))
    sLNv = e2(nc.sbuf_tensor("sLNv", [128, 4], F32))
    sWgj = [e2(nc.sbuf_tensor(f"sWgj{i}", [128, 16, 128], BF16)) for i in range(2)]
    sWuj = [e2(nc.sbuf_tensor(f"sWuj{i}", [128, 16, 128], BF16)) for i in range(2)]
    sWdk = [e2(nc.sbuf_tensor(f"sWdk{i}", [128, 1024], BF16)) for i in range(2)]
    sOut = [e2(nc.sbuf_tensor(f"sOut{i}", [128, D], F32)) for i in range(2)]
    sQds = [e2(nc.sbuf_tensor(f"sQds{i}", [128, 4, TF], BF16)) for i in range(2)]
    sOls = [e2(nc.sbuf_tensor(f"sOls{i}", [128, 4, DV], BF16)) for i in range(2)]
    sGs = [e2(nc.sbuf_tensor(f"sGs{i}", [128, 4, DV], BF16)) for i in range(2)]
    sXtm2 = e2(nc.sbuf_tensor("sXtm2", [128, 4, D], BF16))
    sWos = [e2(nc.sbuf_tensor(f"sWos{i}", [128, 8, 512], BF16)) for i in range(2)]

    slot_wgj = [None, None]
    slot_wuj = [None, None]
    slot_wdk = [None, None]
    slot_out = [None, None]
    tok_prevB = []
    slot_wo = [None, None]
    slot_xtm = None
    slot_qd2 = [None, None]
    slot_ol2 = [None, None]
    slot_g2 = [None, None]
    prev_x1_rd = []

    for t2 in range(NT2):
        tau = t2 // 2
        lc0 = (t2 % 2) * 4
        tk0 = t2 * TF

        stA = ExitStack()
        eA = stA.enter_context
        sOfull = eA(nc.sbuf_tensor(f"sOfull_{t2}", [128, 16, DV], BF16))
        sSgw = eA(nc.sbuf_tensor(f"sSgw_{t2}", [128, 16, DV], BF16))
        sTmp = eA(nc.sbuf_tensor(f"sTmp_{t2}", [128, DV], BF16))
        sOtm = eA(nc.sbuf_tensor(f"sOtm_{t2}", [128, 4, VD], BF16))
        sOfm = eA(nc.sbuf_tensor(f"sOfm_{t2}", [128, 8, TF], BF16))


        t_sq = []
        t_ofull = {}
        t_sgw = {}
        for h in range(H):
            b = h % 2
            tqd = P.dma("gpsimd", sQds[b].ap(),
                        dqd[tau, h][:, :, (t2 % 2) * TF:(t2 % 2) * TF + TF],
                        [slot_qd2[b], tok_spill_qd], sem=f"rq{b}")
            tol = P.dma("gpsimd", sOls[b].ap(), dol[tau, h][:, lc0:lc0 + 4, :],
                        [slot_ol2[b], tok_spill_ol], sem=f"ro{b}")
            tgl = P.dma("gpsimd", sGs[b].ap(), dgs[tau, h][:, lc0:lc0 + 4, :],
                        [slot_g2[b], tok_spill_g], sem=f"rg{b}")
            last_corr = None
            for tc in range(4):
                i = h * 4 + tc
                bank = tc % 2
                tm = MM(pb[bank][:, 0:256],
                        [(sQds[b][:, d, tc * 128:(tc + 1) * 128],
                          sS[:, h * 4 + d, :]) for d in range(4)],
                        [tqd, tok_Sin, pb_free[bank]])
                last_corr = tm
                tof = V(lambda e_, bank=bank, i=i, b=b, tc=tc:
                        e_.scalar_tensor_tensor(
                    out=sOfull[:, i, :], in0=pb[bank][:, 0:256], scalar=0.0,
                    in1=sOls[b][:, tc, :], op0=ALU.bypass, op1=ALU.add),
                    [tm, tol])
                pb_free[bank] = tof
                t_ofull[i] = tof
                t_sq.append(SC(lambda e_, i=i: e_.activation(
                    out=sSqD[:, 0:DV], in_=sOfull[:, i, :], func=AF.Square,
                    accum_out=sSsq[:, i:i + 1]), [tof]))
                tsg = SC(lambda e_, i=i, b=b, tc=tc: e_.activation(
                    out=sSgw[:, i, :], in_=sGs[b][:, tc, :], func=AF.Silu),
                    [tgl])
                t_sgw[i] = V(lambda e_, i=i: e_.tensor_tensor(
                    out=sSgw[:, i, :], in0=sSgw[:, i, :], in1=sWnB[:],
                    op=ALU.mult), [tsg, t_wnb])
            slot_qd2[b] = last_corr
            slot_ol2[b] = t_ofull[h * 4 + 3]
            slot_g2[b] = t_sgw[h * 4 + 3]

        tv_ = V(lambda e_: e_.tensor_scalar(
            out=sVar[:], in0=sSsq[:], scalar1=1.0 / DV, scalar2=None,
            op0=ALU.mult), t_sq)
        ts_ = SC(lambda e_: e_.activation(out=sSd[:], in_=sVar[:],
                                          func=AF.Ln, bias=sEps[:, 0:1], scale=1.0),
                 [tv_, t_eps])
        t_rstd = SC(lambda e_: e_.activation(out=sRstd[:], in_=sSd[:],
                                             func=AF.Exp, scale=-0.5), [ts_])

        t_otm = {}
        prev_otm = None
        for h in range(H):
            for tc in range(4):
                i = h * 4 + tc
                tt = V(lambda e_, i=i: e_.tensor_scalar(
                    out=sTmp[:], in0=sOfull[:, i, :], scalar1=sRstd[:, i:i + 1],
                    scalar2=None, op0=ALU.mult), [t_rstd, t_ofull[i], prev_otm])
                t_otm[(tc, h)] = V(lambda e_, i=i, tc=tc, h=h: e_.tensor_tensor(
                    out=sOtm[:, tc, h * DV:(h + 1) * DV], in0=sTmp[:],
                    in1=sSgw[:, i, :], op=ALU.mult), [tt, t_sgw[i]])
                prev_otm = t_otm[(tc, h)]

        t_ofm = {}
        for tc in range(4):
            pbk = pb[6 + tc % 2].ap().bitcast(BF16)
            ttr = []
            for v in range(8):
                ttr.append(TR(pbk[:, v * 128:(v + 1) * 128],
                              sOtm[:, tc, v * 128:(v + 1) * 128],
                              [t_otm[(tc, hh)] for hh in range(H)] +
                              [pb_free[6 + tc % 2]]))
            t_ofm[tc] = V(lambda e_, tc=tc, pbk=pbk: e_.tensor_copy(
                out=sOfm.ap()[:, 0:8, tc * 128:(tc + 1) * 128],
                in_=pbk[:, 0:1024].rearrange("p (a b) -> p a b", a=8)),
                [ttr[-1]])
            pb_free[6 + tc % 2] = t_ofm[tc]

        t_xtm = []
        for tc in range(4):
            t_xtm.append(P.dma(
                "sync", sXtm2[:, tc, :],
                dxbf[tk0 + tc * 128: tk0 + (tc + 1) * 128, :],
                [t_xbf[(tk0 + tc * 128) // 128], slot_xtm], sem="rxt"))

        t_x1 = {}
        for og in range(4):
            b = og % 2
            two = P.dma("sync", sWos[b].ap(), dwobf[og],
                        [slot_wo[b]] + t_awb[("o", og)], sem=f"rwo{b}")
            last_womm = None
            for tc in range(4):
                bank = (og * 4 + tc) % 4
                tm = MM(pb[bank][:, 0:512],
                        [(sOfm[:, k, tc * 128:(tc + 1) * 128], sWos[b][:, k, :])
                         for k in range(8)],
                        [t_ofm[tc], two, pb_free[bank]])
                last_womm = tm
                te = V(lambda e_, bank=bank, tc=tc, og=og: e_.scalar_tensor_tensor(
                    out=sX1tm[:, tc, og * 512:(og + 1) * 512],
                    in0=pb[bank][:, 0:512], scalar=0.0,
                    in1=sXtm2[:, tc, og * 512:(og + 1) * 512],
                    op0=ALU.bypass, op1=ALU.add,
                    accum_out=sLNa[:, tc, og:og + 1]),
                    [tm] + t_xtm + prev_x1_rd)
                pb_free[bank] = te
                t_x1[(og, tc)] = te
            slot_wo[b] = last_womm
            slot_xtm = last_womm

        t_ln1 = []
        for tc in range(4):
            tsum = V(lambda e_, tc=tc: e_.reduce_sum(
                out=sLNs[:, tc:tc + 1], in_=sLNa[:, tc, :], axis=AXX),
                [t_x1[(og2, tc)] for og2 in range(4)])
            tssq = SC(lambda e_, tc=tc: e_.activation(
                out=sSqD[:], in_=sX1tm[:, tc, :], func=AF.Square,
                accum_out=sLNq[:, tc:tc + 1]),
                [t_x1[(og2, tc)] for og2 in range(4)] + [t_sq[-1]])
            tm_ = V(lambda e_, tc=tc: e_.tensor_scalar(
                out=sLNm[:, tc:tc + 1], in0=sLNs[:, tc:tc + 1],
                scalar1=1.0 / D, scalar2=None, op0=ALU.mult), [tsum])
            tms = V(lambda e_, tc=tc: e_.tensor_mul(
                out=sLNt[:, tc:tc + 1], in0=sLNm[:, tc:tc + 1],
                in1=sLNm[:, tc:tc + 1]), [tm_])
            tvq = V(lambda e_, tc=tc: e_.tensor_scalar(
                out=sLNv[:, tc:tc + 1], in0=sLNq[:, tc:tc + 1],
                scalar1=1.0 / D, scalar2=None, op0=ALU.mult), [tssq])
            tvar = V(lambda e_, tc=tc: e_.tensor_sub(
                out=sLNr[:, tc:tc + 1], in0=sLNv[:, tc:tc + 1],
                in1=sLNt[:, tc:tc + 1]), [tvq, tms])
            tsq2 = SC(lambda e_, tc=tc: e_.activation(
                out=sLNt[:, tc:tc + 1], in_=sLNr[:, tc:tc + 1], func=AF.Ln,
                bias=sEps[:, 0:1], scale=1.0), [tvar, t_eps])
            trc = SC(lambda e_, tc=tc: e_.activation(
                out=sLNr[:, tc:tc + 1], in_=sLNt[:, tc:tc + 1], func=AF.Exp,
                scale=-0.5), [tsq2])
            t_ln1.append(V(lambda e_, tc=tc: e_.tensor_scalar(
                out=sX1tm[:, tc, :], in0=sX1tm[:, tc, :],
                scalar1=sLNm[:, tc:tc + 1], scalar2=sLNr[:, tc:tc + 1],
                op0=ALU.subtract, op1=ALU.mult), [trc]))

        if "x1" in dump and t2 == 0:
            P.dma("gpsimd", dumpten("wnb", [128, DV], BF16), sWnB[:], [t_wnb], sem="sdmp")
            P.dma("gpsimd", dumpten("ols1", [128, 4, DV], BF16), sOls[1].ap(),
                  list(t_ofull.values()), sem="sdmp")
            P.dma("gpsimd", dumpten("gs0", [128, 4, DV], BF16), sGs[0].ap(),
                  list(t_sgw.values()), sem="sdmp")
            P.dma("gpsimd", dumpten("rstd", [128, 16]), sRstd[:],
                  [t_rstd] + list(t_otm.values()), sem="sdmp")
            P.dma("gpsimd", dumpten("sgw", [128, 16, DV], BF16), sSgw.ap(),
                  list(t_sgw.values()) + list(t_otm.values()), sem="sdmp")
            P.dma("gpsimd", dumpten("ssq", [128, 16]), sSsq[:], t_sq + [t_rstd], sem="sdmp")
            P.dma("gpsimd", dumpten("x1tm", [128, 4, D], BF16), sX1tm.ap(), t_ln1, sem="sdmp")
            P.dma("gpsimd", dumpten("ofull", [128, 16, DV], BF16), sOfull.ap(),
                  list(t_otm.values()), sem="sdmp")
            P.dma("gpsimd", dumpten("otm", [128, 4, VD], BF16), sOtm.ap(),
                  list(t_otm.values()) + [t_ofm[3]], sem="sdmp")

        t_x1fm = []
        for tc in range(4):
            for half in range(2):
                pbk = pb[6 + half].ap().bitcast(BF16)
                ttr = []
                for f in range(8):
                    fc = half * 8 + f
                    ttr.append(TR(pbk[:, f * 128:(f + 1) * 128],
                                  sX1tm[:, tc, fc * 128:(fc + 1) * 128],
                                  [t_ln1[tc], pb_free[6 + half]]))
                tev = V(lambda e_, tc=tc, half=half, pbk=pbk: e_.tensor_copy(
                    out=sX1fm.ap()[:, half * 8:(half + 1) * 8,
                                   tc * 128:(tc + 1) * 128],
                    in_=pbk[:, 0:1024].rearrange("p (a b) -> p a b", a=8)),
                    [ttr[-1]])
                pb_free[6 + half] = tev
                t_x1fm.append(tev)

        stA.close()

        stB = ExitStack()
        eB = stB.enter_context
        sFuse = eB(nc.sbuf_tensor(f"sFuse_{t2}", [128, NJ, TF], BF16))
        sSilu = eB(nc.sbuf_tensor(f"sSilu_{t2}", [128, TF], BF16))

        t_fuse = {}
        prev_tfu = None
        for j in range(NJ):
            b = j % 2
            twg = P.dma("sync", sWgj[b].ap(), dwgfb[j],
                        [slot_wgj[b], t_ffnw[("g", j)]], sem=f"rwg{b}")
            twu = P.dma("sync", sWuj[b].ap(), dwufb[j],
                        [slot_wuj[b], t_ffnw[("u", j)]], sem=f"rwu{b}")
            bank = (j % 2) * 2
            bank2 = bank + 1
            tmg = MM(pb[bank][:, 0:512],
                     [(sWgj[b][:, k, :], sX1fm[:, k, :]) for k in range(NKC)],
                     t_x1fm + [twg, pb_free[bank]])
            tmu = MM(pb[bank2][:, 0:512],
                     [(sWuj[b][:, k, :], sX1fm[:, k, :]) for k in range(NKC)],
                     [twu, pb_free[bank2]])
            slot_wgj[b] = tmg
            slot_wuj[b] = tmu
            tsl = SC(lambda e_, bank=bank, j=j: e_.activation(
                out=sSilu[:], in_=pb[bank][:, 0:512], func=AF.Silu,
                scale=sSg[:, j:j + 1]), [tmg, t_sg, prev_tfu])
            pb_free[bank] = tsl
            tfu = V(lambda e_, bank2=bank2, j=j: e_.scalar_tensor_tensor(
                out=sFuse[:, j, :], in0=pb[bank2][:, 0:512],
                scalar=sSu[:, j:j + 1], in1=sSilu[:],
                op0=ALU.mult, op1=ALU.mult), [tmu, tsl, t_su])
            pb_free[bank2] = tfu
            t_fuse[j] = tfu
            prev_tfu = tfu
        all_fuse = [t_fuse[j] for j in range(NJ)]

        t_r2 = {}
        for ps in range(2):
            groups = [(tc, ps * 2 + og) for og in range(2) for tc in range(4)]
            smm = [None] * 8
            for k in range(64):
                b = k % 2
                twd = P.dma("sync", sWdk[b][:],
                            dwdfb[k][:, ps * 1024:(ps + 1) * 1024],
                            [slot_wdk[b], t_ffnw[("d", k)]], sem=f"rwd{b}")
                for gi, (tc, og) in enumerate(groups):
                    w = [twd]
                    if k == 0:
                        w = w + all_fuse + [pb_free[gi]]
                    l = sFuse[:, k, tc * 128:(tc + 1) * 128]
                    r = sWdk[b].ap()[:, (og - ps * 2) * 512:(og - ps * 2 + 1) * 512]
                    smm[gi] = P.do(
                        "tensor",
                        lambda e_, l=l, r=r, o=pb[gi], s=(k == 0), tt=(k == 63):
                        e_.matmul(o[:, 0:512], l, r, start=s, stop=tt),
                        w, inc="pe")
                slot_wdk[b] = smm[7]
            for gi, (tc, og) in enumerate(groups):
                te = V(lambda e_, gi=gi, tc=tc, og=og: e_.scalar_tensor_tensor(
                    out=sX1tm[:, tc, og * 512:(og + 1) * 512],
                    in0=pb[gi][:, 0:512], scalar=0.0,
                    in1=sX1tm[:, tc, og * 512:(og + 1) * 512],
                    op0=ALU.bypass, op1=ALU.add,
                    accum_out=sLNa[:, tc, og:og + 1]),
                    [smm[gi], t_ln1[tc]])
                pb_free[gi] = te
                t_r2[(og, tc)] = te

        t_store = []
        for tc in range(4):
            tsum = V(lambda e_, tc=tc: e_.reduce_sum(
                out=sLNs[:, tc:tc + 1], in_=sLNa[:, tc, :], axis=AXX),
                [t_r2[(og2, tc)] for og2 in range(4)])
            tssq = SC(lambda e_, tc=tc: e_.activation(
                out=sSqD[:], in_=sX1tm[:, tc, :], func=AF.Square,
                accum_out=sLNq[:, tc:tc + 1]),
                [t_r2[(og2, tc)] for og2 in range(4)])
            tm_ = V(lambda e_, tc=tc: e_.tensor_scalar(
                out=sLNm[:, tc:tc + 1], in0=sLNs[:, tc:tc + 1],
                scalar1=1.0 / D, scalar2=None, op0=ALU.mult), [tsum])
            tms = V(lambda e_, tc=tc: e_.tensor_mul(
                out=sLNt[:, tc:tc + 1], in0=sLNm[:, tc:tc + 1],
                in1=sLNm[:, tc:tc + 1]), [tm_])
            tvq = V(lambda e_, tc=tc: e_.tensor_scalar(
                out=sLNv[:, tc:tc + 1], in0=sLNq[:, tc:tc + 1],
                scalar1=1.0 / D, scalar2=None, op0=ALU.mult), [tssq])
            tvar = V(lambda e_, tc=tc: e_.tensor_sub(
                out=sLNr[:, tc:tc + 1], in0=sLNv[:, tc:tc + 1],
                in1=sLNt[:, tc:tc + 1]), [tvq, tms])
            tsq2 = SC(lambda e_, tc=tc: e_.activation(
                out=sLNt[:, tc:tc + 1], in_=sLNr[:, tc:tc + 1], func=AF.Ln,
                bias=sEps[:, 0:1], scale=1.0), [tvar, t_eps])
            trc = SC(lambda e_, tc=tc: e_.activation(
                out=sLNr[:, tc:tc + 1], in_=sLNt[:, tc:tc + 1], func=AF.Exp,
                scale=-0.5), [tsq2])
            b = tc % 2
            tap = V(lambda e_, tc=tc, b=b: e_.tensor_scalar(
                out=sOut[b][:], in0=sX1tm[:, tc, :],
                scalar1=sLNm[:, tc:tc + 1], scalar2=sLNr[:, tc:tc + 1],
                op0=ALU.subtract, op1=ALU.mult), [trc, slot_out[b]])
            tst = P.dma("sync", dy[tk0 + tc * 128: tk0 + (tc + 1) * 128, :],
                        sOut[b][:], [tap], sem=f"ry{b}")
            slot_out[b] = tst
            t_store.append(tst)

        if "ln2" in dump and t2 == 0:
            P.dma("gpsimd", dumpten("r2", [128, 4, D], BF16), sX1tm.ap(),
                  list(t_r2.values()) + t_store, sem="sdmp")
            P.dma("gpsimd", dumpten("ln2a", [128, 4, 4]), sLNa.ap(),
                  list(t_r2.values()) + t_store, sem="sdmp")
            P.dma("gpsimd", dumpten("ln2s", [128, 4]), sLNs[:], t_store, sem="sdmp")
            P.dma("gpsimd", dumpten("ln2q", [128, 4]), sLNq[:], t_store, sem="sdmp")
            P.dma("gpsimd", dumpten("ln2m", [128, 4]), sLNm[:], t_store, sem="sdmp")
            P.dma("gpsimd", dumpten("ln2r", [128, 4]), sLNr[:], t_store, sem="sdmp")
            P.dma("gpsimd", dumpten("fuse0", [128, 8, TF], BF16),
                  sFuse.ap()[:, 0:8, :], list(t_fuse.values()), sem="sdmp")
        stB.close()
        tok_prevB = [smm[7], t_fuse[NJ - 1]]
        prev_x1_rd = [tssq]

    for sem, q in P.dma_sems.items():
        if P.counts.get(sem, 0) > 0:
            P.do(q, [], waits=[(sem, P.counts[sem], "none", 16)])
    P.do("sync", [], waits=t_store)

    st2.close()
    st.close()
    P.emit()
    return nc


_BUILD_CACHE = {}


def _get_nc():
    if "nc" not in _BUILD_CACHE:
        _BUILD_CACHE["nc"] = build("full")
    return _BUILD_CACHE["nc"]


def prep_shared(i):
    d = {}
    d["wq"] = np.ascontiguousarray(np.asarray(i["L0_attn_Wq"], dtype=np.float32))
    d["wk"] = np.ascontiguousarray(np.asarray(i["L0_attn_Wk"], dtype=np.float32))
    d["wv"] = np.ascontiguousarray(np.asarray(i["L0_attn_Wv"], dtype=np.float32))
    d["wg"] = np.ascontiguousarray(np.asarray(i["L0_attn_Wg"], dtype=np.float32))
    d["wgk1"] = np.ascontiguousarray(np.asarray(i["L0_attn_Wgk1"], dtype=np.float32))
    d["wgk2"] = np.ascontiguousarray(np.asarray(i["L0_attn_Wgk2"], dtype=np.float32))
    d["wo"] = np.ascontiguousarray(np.asarray(i["L0_attn_Wo"], dtype=np.float32))
    d["bgk2pf"] = np.ascontiguousarray(
        np.asarray(i["L0_attn_bgk2"], dtype=np.float32).reshape(16, 128).T)
    d["wnormr"] = np.ascontiguousarray(
        np.asarray(i["L0_attn_Wnorm"], dtype=np.float32).reshape(1, DV))
    d["sgpf"] = np.ascontiguousarray(
        np.asarray(i["L0_ffn_s_gate"], dtype=np.float32).reshape(NJ, 128).T)
    d["supf"] = np.ascontiguousarray(
        np.asarray(i["L0_ffn_s_up"], dtype=np.float32).reshape(NJ, 128).T)
    d["sdnr"] = np.ascontiguousarray(
        np.asarray(i["L0_ffn_s_down"], dtype=np.float32).reshape(1, D))
    wgt = np.asarray(i["L0_ffn_w_gate"])
    d["wg8t"] = np.ascontiguousarray(
        wgt.reshape(16, 128, NJ, 128).transpose(2, 1, 0, 3)
        .reshape(NJ, 128, 16 * 128))
    wut = np.asarray(i["L0_ffn_w_up"])
    d["wu8t"] = np.ascontiguousarray(
        wut.reshape(16, 128, NJ, 128).transpose(2, 1, 0, 3)
        .reshape(NJ, 128, 16 * 128))
    d["wd8"] = np.ascontiguousarray(
        np.asarray(i["L0_ffn_w_down"]).reshape(64, 128, D))
    d["identb"] = np.eye(128, dtype=NPBF)
    s_ = np.arange(128)[:, None]
    t_ = np.arange(128)[None, :]
    d["maskut"] = (s_ <= t_).astype(NPBF)
    return d


def make_in_maps(ins):
    shared = prep_shared(ins)
    x = np.asarray(ins["L0_attn_x"], dtype=np.float32)
    in_maps = []
    for c in range(8):
        r = c % 4
        m = dict(shared)
        m["xs"] = np.ascontiguousarray(x[c // 4, r * TPC:(r + 1) * TPC, :])
        sel = (np.arange(3) < r).astype(np.float32)
        m["sel01"] = np.ascontiguousarray(
            np.broadcast_to(sel, (128, 3)).astype(np.float32))
        m["selsc"] = np.ascontiguousarray(
            np.broadcast_to(sel * INV16, (128, 3)).astype(np.float32))
        m["selbi"] = np.ascontiguousarray(
            np.broadcast_to(-(1.0 - sel) * 100.0, (128, 3)).astype(np.float32))
        in_maps.append(m)
    return in_maps


def kernel(L0_attn_x, L0_attn_Wq, L0_attn_Wk, L0_attn_Wv, L0_attn_Wgk1,
           L0_attn_Wgk2, L0_attn_bgk2, L0_attn_Wg, L0_attn_Wnorm, L0_attn_Wo,
           L0_ffn_w_gate, L0_ffn_w_up, L0_ffn_w_down, L0_ffn_s_gate,
           L0_ffn_s_up, L0_ffn_s_down):
    ins = dict(L0_attn_x=L0_attn_x, L0_attn_Wq=L0_attn_Wq, L0_attn_Wk=L0_attn_Wk,
               L0_attn_Wv=L0_attn_Wv, L0_attn_Wgk1=L0_attn_Wgk1,
               L0_attn_Wgk2=L0_attn_Wgk2, L0_attn_bgk2=L0_attn_bgk2,
               L0_attn_Wg=L0_attn_Wg, L0_attn_Wnorm=L0_attn_Wnorm,
               L0_attn_Wo=L0_attn_Wo, L0_ffn_w_gate=L0_ffn_w_gate,
               L0_ffn_w_up=L0_ffn_w_up, L0_ffn_w_down=L0_ffn_w_down,
               L0_ffn_s_gate=L0_ffn_s_gate, L0_ffn_s_up=L0_ffn_s_up,
               L0_ffn_s_down=L0_ffn_s_down)
    in_maps = make_in_maps(ins)
    nc = _get_nc()
    res = run_bass_kernel_spmd(nc, in_maps, core_ids=list(range(8)))
    out = np.empty((2, 8192, D), dtype=np.float32)
    for c in range(8):
        out[c // 4, (c % 4) * TPC:(c % 4 + 1) * TPC, :] = res.results[c]["y"]
    return out


# version 13 (final)
# speedup vs baseline: 1.1683x; 1.0284x over previous
import numpy as np
import ml_dtypes
from contextlib import ExitStack

import concourse.bass as bass
import concourse.mybir as mybir
from concourse.bass_utils import run_bass_kernel_spmd

F32 = mybir.dt.float32
BF16 = mybir.dt.bfloat16
I8 = mybir.dt.int8
AF = mybir.ActivationFunctionType
ALU = mybir.AluOpType
NPBF = ml_dtypes.bfloat16

ENG = ("sync", "tensor", "vector", "scalar", "gpsimd")

TPC = 2048
D = 2048
H = 4
DK = 512
DV = 256
KD = 2048
VD = 1024
LOW = 16
INTER = 8192
CH = 128
TS = 1024
NT1 = TPC // TS
NCH1 = TS // CH
TF = 512
NT2 = TPC // TF
NKC = D // 128
NJ = INTER // 128
SCALE = 1.0 / float(np.sqrt(512.0))
EPS = 1e-5
INV16 = 1.0 / 16.0
AXX = mybir.AxisListType.X


class Prog:

    def __init__(self, nc):
        self.nc = nc
        self.ops = []
        self.counts = {}
        self.sem_order = []
        self.dma_sems = {}

    def _sem(self, name):
        if name not in self.counts:
            self.counts[name] = 0
            self.sem_order.append(name)
        return name

    def do(self, engine, fns, waits=(), inc=None, inc_n=1, noskip=False):
        if fns is None:
            fns = []
        if not isinstance(fns, (list, tuple)):
            fns = [fns]
        ws = [w for w in waits if w is not None]
        tok = None
        if inc is not None:
            self._sem(inc)
            self.counts[inc] += inc_n
            tok = (inc, self.counts[inc], engine, inc_n)
        self.ops.append((engine, list(fns), ws, (inc, inc_n) if inc else None,
                         noskip))
        return tok

    def dma(self, queue, out, in_, waits=(), sem=None, **kw):
        sem = sem or ("q_" + queue)
        self.dma_sems.setdefault(sem, queue)
        return self.do(
            queue,
            lambda e, out=out, in_=in_, kw=kw: e.dma_start(out=out, in_=in_, **kw),
            waits, inc=sem, inc_n=16,
        )

    def fence(self, engine):
        return self.do(engine, [], inc={"vector": "vv", "scalar": "sc",
                                        "tensor": "pe", "gpsimd": "gg",
                                        "sync": "sy"}[engine])

    def barrier(self):
        for sem, q in self.dma_sems.items():
            if self.counts.get(sem, 0) > 0:
                self.do(q, [], waits=[(sem, self.counts[sem], "none", 16)])
        toks = [self.do(e, [], inc="bar") for e in ENG]
        for e in ENG:
            self.do(e, [], waits=toks)

    def emit(self):
        nc = self.nc
        with ExitStack() as stx:
            sems = {n: stx.enter_context(nc.semaphore(n)) for n in self.sem_order}
            waited = {e: {} for e in ENG}
            with nc.Block() as block:
                def mk(engname):
                    def body(eng):
                        for (e, fns, ws, inc, noskip) in self.ops:
                            if e != engname:
                                continue
                            need = {}
                            for (sem, val, prod, n) in ws:
                                if waited[engname].get(sem, 0) >= val:
                                    continue
                                if need.get(sem, 0) < val:
                                    need[sem] = val
                            for sem, val in need.items():
                                eng.wait_ge(sems[sem], val)
                                waited[engname][sem] = val
                            inst = None
                            for fn in fns:
                                inst = fn(eng)
                            if inc is not None:
                                if inst is None:
                                    inst = eng.nop()
                                inst.then_inc(sems[inc[0]], inc[1])
                    return body
                block.sync(mk("sync"))
                block.tensor(mk("tensor"))
                block.vector(mk("vector"))
                block.scalar(mk("scalar"))
                block.gpsimd(mk("gpsimd"))


def build(upto="full", dump=(), lim=None):
    nc = bass.Bass("TRN2", target_bir_lowering=False)
    P = Prog(nc)
    dump = set(dump)

    dx = nc.dram_tensor("xs", [TPC, D], F32, kind="ExternalInput").ap()
    dwq = nc.dram_tensor("wq", [D, KD], F32, kind="ExternalInput").ap()
    dwk = nc.dram_tensor("wk", [D, KD], F32, kind="ExternalInput").ap()
    dwv = nc.dram_tensor("wv", [D, VD], F32, kind="ExternalInput").ap()
    dwg = nc.dram_tensor("wg", [D, VD], F32, kind="ExternalInput").ap()
    dwgk1 = nc.dram_tensor("wgk1", [D, LOW], F32, kind="ExternalInput").ap()
    dwgk2 = nc.dram_tensor("wgk2", [LOW, KD], F32, kind="ExternalInput").ap()
    dwo = nc.dram_tensor("wo", [VD, D], F32, kind="ExternalInput").ap()
    dbgk2 = nc.dram_tensor("bgk2pf", [128, 16], F32, kind="ExternalInput").ap()
    dwnorm = nc.dram_tensor("wnormr", [1, DV], F32, kind="ExternalInput").ap()
    dsg = nc.dram_tensor("sgpf", [128, NJ], F32, kind="ExternalInput").ap()
    dsu = nc.dram_tensor("supf", [128, NJ], F32, kind="ExternalInput").ap()
    dsdn = nc.dram_tensor("sdnr", [1, D], F32, kind="ExternalInput").ap()
    dwg8 = nc.dram_tensor("wg8t", [NJ, 128, 16 * 128], I8, kind="ExternalInput").ap()
    dwu8 = nc.dram_tensor("wu8t", [NJ, 128, 16 * 128], I8, kind="ExternalInput").ap()
    dwd8 = nc.dram_tensor("wd8", [64, 128, D], I8, kind="ExternalInput").ap()
    dident = nc.dram_tensor("identb", [128, 128], BF16, kind="ExternalInput").ap()
    dmask = nc.dram_tensor("maskut", [128, 128], BF16, kind="ExternalInput").ap()
    dsel01 = nc.dram_tensor("sel01", [128, 3], F32, kind="ExternalInput").ap()
    dselsc = nc.dram_tensor("selsc", [128, 3], F32, kind="ExternalInput").ap()
    dselbi = nc.dram_tensor("selbi", [128, 3], F32, kind="ExternalInput").ap()

    dy = nc.dram_tensor("y", [TPC, D], F32, kind="ExternalOutput").ap()

    dxbf = nc.dram_tensor("xbf", [TPC, D], BF16).ap()
    dwqbf = nc.dram_tensor("wqbf", [H, 4, 128, 16 * 128], BF16).ap()
    dwkbf = nc.dram_tensor("wkbf", [H, 4, 128, 16 * 128], BF16).ap()
    dwvbf = nc.dram_tensor("wvbf", [H, 128, 16 * 256], BF16).ap()
    dwgbf = nc.dram_tensor("wgbf", [H, 128, 16 * 256], BF16).ap()
    dwobf = nc.dram_tensor("wobf", [4, 128, 8 * 512], BF16).ap()
    dwgfb = nc.dram_tensor("wgfb", [NJ, 128, 16 * 128], BF16).ap()
    dwufb = nc.dram_tensor("wufb", [NJ, 128, 16 * 128], BF16).ap()
    dwdfb = nc.dram_tensor("wdfb", [64, 128, D], BF16).ap()
    dqd = nc.dram_tensor("qd_scr", [NT1, H, 128, 4, TS], BF16).ap()
    dol = nc.dram_tensor("ol_scr", [NT1, H, 128, NCH1, DV], BF16).ap()
    dgs = nc.dram_tensor("g_scr", [NT1, H, 128, NCH1, DV], BF16).ap()
    dccinA = nc.dram_tensor("ccinA", [128, 16], F32).ap()
    dccoutA = nc.dram_tensor("ccoutA", [4 * 128, 16], F32).ap()
    dccinZ = nc.dram_tensor("ccinZ", [128, 4096], BF16).ap()
    dccoutZ = nc.dram_tensor("ccoutZ", [4 * 128, 4096], BF16).ap()

    ddump = {}
    U16 = mybir.dt.uint16

    def dumpten(name, shape, dtype=F32):
        if name not in ddump:
            if dtype == BF16:
                t = nc.dram_tensor("dbg_" + name, shape, U16,
                                   kind="ExternalOutput").ap()
                ddump[name] = t.bitcast(BF16)
            else:
                ddump[name] = nc.dram_tensor(
                    "dbg_" + name, shape, dtype, kind="ExternalOutput").ap()
        return ddump[name]

    st = ExitStack()
    e = st.enter_context

    sS = e(nc.sbuf_tensor("sS", [128, 16, DV], BF16))
    sId = e(nc.sbuf_tensor("sId", [128, 128], BF16))
    sMask = e(nc.sbuf_tensor("sMask", [128, 128], BF16))
    sWnB = e(nc.sbuf_tensor("sWnB", [128, DV], BF16))
    sGk1 = e(nc.sbuf_tensor("sGk1", [128, 16, LOW], BF16))
    sGk2 = e(nc.sbuf_tensor("sGk2", [LOW, KD], BF16))
    sBgk2 = e(nc.sbuf_tensor("sBgk2", [128, 16], F32))
    sSg = e(nc.sbuf_tensor("sSg", [128, NJ], F32))
    sSu = e(nc.sbuf_tensor("sSu", [128, NJ], F32))
    sSel01 = e(nc.sbuf_tensor("sSel01", [128, 3], F32))
    sSelsc = e(nc.sbuf_tensor("sSelsc", [128, 3], F32))
    sSelbi = e(nc.sbuf_tensor("sSelbi", [128, 3], F32))
    sPrev = e(nc.sbuf_tensor("sPrev", [128, 16], F32))
    sBl = e(nc.sbuf_tensor("sBl", [128, 16, NCH1], F32))
    sEbl = e(nc.sbuf_tensor("sEbl", [128, 16, NCH1], F32))
    sPexp = e(nc.sbuf_tensor("sPexp", [128, 16, NCH1], F32))
    sZb = e(nc.sbuf_tensor("sZb", [128, 128], F32))
    sOneB = e(nc.sbuf_tensor("sOneB", [1, 512], BF16))
    sEight = e(nc.sbuf_tensor("sEight", [128, NCH1], F32))
    sI1 = e(nc.sbuf_tensor("sI1", [128, 4, NCH1], F32))
    sI2 = e(nc.sbuf_tensor("sI2", [128, 4, NCH1], F32))
    sEps = e(nc.sbuf_tensor("sEps", [128, 1], F32))

    pb = [e(nc.psum_tensor(f"pb{i}", [128, 512], F32)) for i in range(8)]
    pb_free = [None] * 8

    def V(fn, waits=()):
        return P.do("vector", fn, waits, inc="vv")

    def Vs(fn, waits=()):
        return P.do("vector", fn, waits, inc="vv", noskip=True)

    def SC(fn, waits=()):
        return P.do("scalar", fn, waits, inc="sc")

    def GP(fn, waits=()):
        return P.do("gpsimd", fn, waits, inc="gg")

    def MM(out_ap, pairs, waits=()):
        n = len(pairs)
        fns = []
        for i, (l, r) in enumerate(pairs):
            fns.append(lambda eng, l=l, r=r, s=(i == 0), t=(i == n - 1), o=out_ap:
                       eng.matmul(o, l, r, start=s, stop=t))
        return P.do("tensor", fns, waits, inc="pe")

    def TR(out_ap, in_ap, waits=()):
        return P.do("tensor",
                    lambda eng, o=out_ap, i=in_ap: eng.transpose(o, i, sId[:]),
                    waits, inc="pe")

    t_id = P.dma("sync", sId[:], dident, sem="c_id")
    t_mask = P.dma("sync", sMask[:], dmask, sem="c_mk")
    t_bgk2 = P.dma("sync", sBgk2[:], dbgk2, sem="c_bg")
    t_sg = P.dma("sync", sSg[:], dsg, sem="c_sg")
    t_su = P.dma("sync", sSu[:], dsu, sem="c_su")
    t_sel = P.dma("sync", sSel01[:], dsel01, sem="c_s0")
    t_selsc = P.dma("sync", sSelsc[:], dselsc, sem="c_s1")
    t_selbi = P.dma("sync", sSelbi[:], dselbi, sem="c_s2")
    t_zb = V(lambda e_: e_.memset(sZb[:], 0.0))
    t_ones = V(lambda e_: e_.memset(sOneB[:], 1.0))
    t_prev = V(lambda e_: e_.memset(sPrev[:], 0.0))
    t_s0 = V(lambda e_: e_.memset(sS.ap().rearrange("p a b -> p (a b)"), 0.0))
    t_eps = V(lambda e_: e_.memset(sEps[:], EPS))

    st0 = ExitStack()
    e0 = st0.enter_context
    sXst = [e0(nc.sbuf_tensor(f"sXst{i}", [128, 2, D], F32)) for i in range(2)]
    sXbfs = [e0(nc.sbuf_tensor(f"sXbfs{i}", [128, 2, D], BF16)) for i in range(2)]
    sWbig = [e0(nc.sbuf_tensor(f"sWbig{i}", [128, 8, 512], F32)) for i in range(2)]
    sWcol = [e0(nc.sbuf_tensor(f"sWcol{i}", [128, 16, 512], BF16)) for i in range(2)]
    sW8 = [e0(nc.sbuf_tensor(f"sW8_{i}", [128, 2, D], I8)) for i in range(2)]
    sW8b = [e0(nc.sbuf_tensor(f"sW8b{i}", [128, 2, D], BF16)) for i in range(2)]
    sRow = e0(nc.sbuf_tensor("sRow", [1, D], F32))
    sRowB = e0(nc.sbuf_tensor("sRowB", [1, D], BF16))
    sSdnB = e0(nc.sbuf_tensor("sSdnB", [128, D], BF16))
    sGkSt = e0(nc.sbuf_tensor("sGkSt", [128, 16 * LOW], F32))
    sGk2St = e0(nc.sbuf_tensor("sGk2St", [LOW, KD], F32))

    t = P.dma("sync", sGkSt.ap().rearrange("p (k f) -> p k f", k=16),
              dwgk1.rearrange("(k p) f -> p k f", p=128), sem="c_g1")
    t_gk1 = V(lambda e_: e_.tensor_copy(
        out=sGk1.ap().rearrange("p a b -> p (a b)"), in_=sGkSt[:]), [t])
    t = P.dma("sync", sGk2St[:], dwgk2, sem="c_g2")
    t_gk2 = V(lambda e_: e_.tensor_copy(out=sGk2[:], in_=sGk2St[:]), [t])

    t = P.dma("sync", sRow[0:1, 0:DV], dwnorm, sem="c_wn")
    t_wnrow = V(lambda e_: e_.tensor_copy(out=sRowB[0:1, 0:DV],
                                          in_=sRow[0:1, 0:DV]), [t, t_ones])
    t = MM(pb[0][:, 0:DV], [(sOneB[0:1, 0:128], sRowB[0:1, 0:DV])], [t_wnrow])
    t_wnb = V(lambda e_: e_.tensor_copy(out=sWnB[:], in_=pb[0][:, 0:DV]), [t])
    pb_free[0] = t_wnb

    t = P.dma("sync", sRow[:], dsdn, [t_wnrow], sem="c_sd")
    t = V(lambda e_: e_.tensor_copy(out=sRowB[:], in_=sRow[:]), [t, t_wnb])
    t_sdnb = None
    for i in range(4):
        tm = MM(pb[1][:, 0:512],
                [(sOneB[0:1, 0:128], sRowB[0:1, i * 512:(i + 1) * 512])],
                [t, pb_free[1]])
        t_sdnb = V(lambda e_, i=i: e_.tensor_copy(
            out=sSdnB[:, i * 512:(i + 1) * 512], in_=pb[1][:, 0:512]), [tm])
        pb_free[1] = t_sdnb

    dxr = dx.rearrange("(i p) m -> p i m", p=128)
    dxbr = dxbf.rearrange("(i p) m -> p i m", p=128)
    t_xbf2 = []
    prev_st = [None, None]
    for i in range(8):
        b = i % 2
        tl = P.dma("sync", sXst[b].ap(), dxr[:, i * 2:(i + 1) * 2, :],
                   [prev_st[b]], sem=f"sxl{b}")
        tc = SC(lambda e_, b=b: e_.copy(
            out=sXbfs[b].ap().rearrange("p a b -> p (a b)"),
            in_=sXst[b].ap().rearrange("p a b -> p (a b)")), [tl])
        ts_ = P.dma("gpsimd" if b == 0 else "scalar",
                    dxbr[:, i * 2:(i + 1) * 2, :], sXbfs[b].ap(), [tc],
                    sem=f"sxs{b}")
        prev_st[b] = ts_
        t_xbf2.append(ts_)
    t_xbf = [t_xbf2[i // 2] for i in range(16)]

    t_awb = {}
    prev_big = [None, None]
    prev_col = [None, None]
    _blk = [0]

    def cast_big(tag, wsrc, col0, ncol, scale, outs):
        i = _blk[0]
        _blk[0] += 1
        cb = i % 2
        wr = wsrc.rearrange("(k p) m -> p k m", p=128)
        tcs = []
        for hh in range(2):
            tl = P.dma("sync",
                       sWbig[cb].ap()[:, hh * 4:(hh + 1) * 4, 0:ncol]
                       if False else sWbig[cb].ap()[:, 0:8, 0:ncol],
                       wr[:, hh * 8:(hh + 1) * 8, col0:col0 + ncol],
                       [prev_big[cb] if hh == 0 else tcs[-1]], sem=f"swl{cb}")
            eng = "vector"
            tc = P.do(eng, lambda e_, cb=cb, hh=hh, ncol=ncol, sc_=scale:
                      (e_.tensor_copy(
                          out=sWcol[cb].ap()[:, hh * 8:(hh + 1) * 8, 0:ncol],
                          in_=sWbig[cb].ap()[:, 0:8, 0:ncol])
                       if sc_ is None else e_.tensor_scalar(
                          out=sWcol[cb].ap()[:, hh * 8:(hh + 1) * 8, 0:ncol],
                          in0=sWbig[cb].ap()[:, 0:8, 0:ncol],
                          scalar1=sc_, scalar2=None, op0=ALU.mult)),
                      [tl, prev_col[cb]],
                      inc="vv" if eng == "vector" else "gg")
            tcs.append(tc)
            prev_big[cb] = tc
        stoks = []
        for (oap, srcsl) in outs:
            stoks.append(P.dma("gpsimd" if i % 2 == 0 else "scalar", oap,
                               sWcol[cb].ap()[:, :, srcsl[0]:srcsl[1]],
                               tcs + ([stoks[-1]] if stoks else [prev_col[cb]]),
                               sem=f"sws{cb}"))
        prev_col[cb] = stoks[-1]
        t_awb[tag] = stoks
        return stoks

    for h in range(H):
        cast_big(("q", h), dwq, h * 512, 512, SCALE,
                 [(dwqbf[h, d2].rearrange("p (a b) -> p a b", a=16),
                   (d2 * 128, (d2 + 1) * 128)) for d2 in range(4)])
        cast_big(("k", h), dwk, h * 512, 512, None,
                 [(dwkbf[h, d2].rearrange("p (a b) -> p a b", a=16),
                   (d2 * 128, (d2 + 1) * 128)) for d2 in range(4)])
        cast_big(("v", h), dwv, h * 256, 256, None,
                 [(dwvbf[h].rearrange("p (a b) -> p a b", a=16), (0, 256))])
        cast_big(("g", h), dwg, h * 256, 256, None,
                 [(dwgbf[h].rearrange("p (a b) -> p a b", a=16), (0, 256))])
    for og in range(4):
        i = _blk[0]
        _blk[0] += 1
        cb = i % 2
        wr = dwo.rearrange("(k p) m -> p k m", p=128)
        tl = P.dma("sync", sWbig[cb].ap(),
                   wr[:, :, og * 512:(og + 1) * 512],
                   [prev_big[cb]], sem=f"swl{cb}")
        tc = V(lambda e_, cb=cb: e_.tensor_copy(
            out=sWcol[cb].ap()[:, 0:8, :].rearrange("p a b -> p (a b)"),
            in_=sWbig[cb].ap().rearrange("p a b -> p (a b)")),
            [tl, prev_col[cb]])
        prev_big[cb] = tc
        ts_ = P.dma("gpsimd" if i % 2 == 0 else "scalar",
                    dwobf[og].rearrange("p (a b) -> p a b", a=8),
                    sWcol[cb].ap()[:, 0:8, :], [tc], sem=f"sws{cb}")
        prev_col[cb] = ts_
        t_awb[("o", og)] = [ts_]

    t_ffnw = {}
    prev8 = [None, None]
    prev8b = [None, None]
    dgr = dwg8.rearrange("j p m -> p j m")
    dur = dwu8.rearrange("j p m -> p j m")
    dgfr = dwgfb.rearrange("j p m -> p j m")
    dufr = dwufb.rearrange("j p m -> p j m")
    ddr = dwd8.rearrange("k p m -> p k m")
    ddfr = dwdfb.rearrange("k p m -> p k m")
    for jj in range(NJ // 2):
        j = jj * 2
        b = jj % 2
        tl = P.dma("sync", sW8[b].ap(), dgr[:, j:j + 2, :], [prev8[b]],
                   sem=f"s8l{b}")
        tc = V(lambda e_, b=b: e_.tensor_copy(
            out=sW8b[b].ap().rearrange("p a b -> p (a b)"),
            in_=sW8[b].ap().rearrange("p a b -> p (a b)")), [tl, prev8b[b]])
        ts_ = P.dma("gpsimd", dgfr[:, j:j + 2, :], sW8b[b].ap(), [tc],
                    sem=f"s8s{b}")
        tl2 = P.dma("sync", sW8[b].ap(), dur[:, j:j + 2, :], [tc],
                    sem=f"s8l{b}")
        tc2 = SC(lambda e_, b=b: e_.copy(
            out=sW8b[b].ap().rearrange("p a b -> p (a b)"),
            in_=sW8[b].ap().rearrange("p a b -> p (a b)")), [tl2, ts_])
        ts2 = P.dma("scalar", dufr[:, j:j + 2, :], sW8b[b].ap(), [tc2],
                    sem=f"s8t{b}")
        prev8[b] = tc2
        prev8b[b] = ts2
        t_ffnw[("g", j)] = ts_
        t_ffnw[("g", j + 1)] = ts_
        t_ffnw[("u", j)] = ts2
        t_ffnw[("u", j + 1)] = ts2
    prevd = [prev8b[0], prev8b[1]]
    prevdb = [prev8b[0], prev8b[1]]
    for kk2 in range(32):
        k = kk2 * 2
        b = kk2 % 2
        tl = P.dma("sync", sW8[b].ap(), ddr[:, k:k + 2, :], [prevd[b]],
                   sem=f"sdl{b}")
        tc = None
        for k2 in range(2):
            tc = V(lambda e_, b=b, k2=k2: e_.tensor_tensor(
                out=sW8b[b][:, k2, :], in0=sW8[b][:, k2, :], in1=sSdnB[:],
                op=ALU.mult),
                [tl, prevdb[b], t_sdnb, tc])
        ts_ = P.dma("scalar", ddfr[:, k:k + 2, :], sW8b[b].ap(), [tc],
                    sem=f"sdt{b}")
        prevd[b] = tc
        prevdb[b] = ts_
        t_ffnw[("d", k)] = ts_
        t_ffnw[("d", k + 1)] = ts_
    st0.close()
    P.barrier()

    if upto == "p0":
        for sem, q in P.dma_sems.items():
            if P.counts.get(sem, 0) > 0:
                P.do(q, [], waits=[(sem, P.counts[sem], "none", 16)])
        st.close()
        P.emit()
        return nc

    st1 = ExitStack()
    e1 = st1.enter_context
    sXfm = e1(nc.sbuf_tensor("sXfm", [128, NKC, TS], BF16))
    sEp = e1(nc.sbuf_tensor("sEp", [128, 4, TS], BF16))
    sEm = e1(nc.sbuf_tensor("sEm", [128, 4, TS], BF16))
    sQe = e1(nc.sbuf_tensor("sQe", [128, 4, TS], BF16))
    sKd = e1(nc.sbuf_tensor("sKd", [128, 4, TS], BF16))
    sQF = e1(nc.sbuf_tensor("sQF", [128, 16, CH], F32))
    sXf32 = e1(nc.sbuf_tensor("sXf32", [128, NKC, CH], F32))
    sEpF = e1(nc.sbuf_tensor("sEpF", [128, 4, CH], F32))
    sEmF = e1(nc.sbuf_tensor("sEmF", [128, 4, CH], F32))
    sUb = e1(nc.sbuf_tensor("sUb", [128, 4, 512], BF16))
    sKF = e1(nc.sbuf_tensor("sKF", [128, 16, CH], F32))
    sIdF = e1(nc.sbuf_tensor("sIdF", [128, 128], F32))
    sQd = e1(nc.sbuf_tensor("sQd", [128, 4, TS], BF16))
    sV1 = e1(nc.sbuf_tensor("sV1", [128, NCH1, DV], BF16))
    sG1 = e1(nc.sbuf_tensor("sG1", [128, NCH1, DV], BF16))
    sOl = e1(nc.sbuf_tensor("sOl", [128, NCH1, DV], BF16))
    sGb = e1(nc.sbuf_tensor("sGb", [128, TS], F32))
    sBb = e1(nc.sbuf_tensor("sBb", [128, TS], F32))
    sSig = e1(nc.sbuf_tensor("sSig", [128, 512], F32))
    sT1 = e1(nc.sbuf_tensor("sT1", [LOW, TS], BF16))
    sAT = e1(nc.sbuf_tensor("sAT", [128, 128], BF16))
    sKkf = e1(nc.sbuf_tensor("sKkf", [128, 4, 128], BF16))
    sKkT = e1(nc.sbuf_tensor("sKkT", [128, 512], BF16))
    sDiag = e1(nc.sbuf_tensor("sDiag", [128, 4, 128], BF16))
    sWqs = [e1(nc.sbuf_tensor(f"sWqs{i}", [128, 16, 128], BF16)) for i in range(2)]
    sWks = [e1(nc.sbuf_tensor(f"sWks{i}", [128, 16, 128], BF16)) for i in range(2)]
    sWvs = [e1(nc.sbuf_tensor(f"sWvs{i}", [128, 16, 256], BF16)) for i in range(1)]
    sWgs = [e1(nc.sbuf_tensor(f"sWgs{i}", [128, 16, 256], BF16)) for i in range(1)]

    tok_xfm_free = None
    tok_wq_slot = [None, None]
    tok_wk_slot = [None, None]
    tok_wv_slot = [None, None]
    tok_wg_slot = [None, None]
    tok_spill_qd = None
    tok_spill_ol = None
    tok_spill_g = None
    tok_Sevac = [None] * 16
    tok_t1 = None
    tok_prep = None
    prev_vf = None
    prev_sf = None
    prev_tln = None

    _ntau, _nh, _stg = (lim + (99,))[:3] if lim else (NT1, H, 99)
    for tau in range(_ntau):
        t_xfm = []
        for fc in range(NKC):
            w = [t_xbf[tau * 8 + s_] for s_ in range(8)]
            if fc == 0:
                w = w + [tok_xfm_free]
            t_xfm.append(P.dma(
                "sync", sXfm[:, fc, :],
                dxbf[tau * TS:(tau + 1) * TS, fc * 128:(fc + 1) * 128],
                w, transpose=True, sem="sxf"))

        for half in range(2):
            tm = MM(pb[half][0:LOW, 0:512],
                    [(sGk1[:, k, :], sXfm[:, k, half * 512:(half + 1) * 512])
                     for k in range(NKC)],
                    t_xfm + [t_gk1, pb_free[half], tok_t1])
            tok_t1 = V(lambda e_, half=half: e_.tensor_copy(
                out=sT1[:, half * 512:(half + 1) * 512],
                in_=pb[half][0:LOW, 0:512]), [tm])
            pb_free[half] = tok_t1

        if tau == 0:
            tidf = V(lambda e_: e_.tensor_copy(out=sIdF[:], in_=sId[:]), [t_id])
            tok_qkF = [None] * 16
            if True:
                sXtmF = sGb.ap()[:, 0:512]
                sWqkF = sBb.ap().rearrange("p (k m) -> p k m", k=8)
                if True:
                    tprev_e = None
                    txl = None
                    for piece in range(4):
                        txl = P.dma("sync", sXtmF,
                                    dx[0:128, piece * 512:(piece + 1) * 512],
                                    [txl if piece == 0 else ttr], sem="sxfz")
                        for f2 in range(4):
                            fc = piece * 4 + f2
                            ttr = P.do("tensor",
                                       lambda e_, f2=f2: e_.transpose(
                                           pb[0][:, 0:128],
                                           sXtmF[:, f2 * 128:(f2 + 1) * 128],
                                           sIdF[:]),
                                       [txl, tidf, pb_free[0]], inc="pe")
                            tev = V(lambda e_, fc=fc: e_.tensor_copy(
                                out=sXf32[:, fc, :], in_=pb[0][:, 0:128]),
                                [ttr, tprev_e])
                            pb_free[0] = tev
                            tprev_e = tev
                if True:
                    twf = None
                    for (wsrc, dstT, scl) in ((dwq, sQF, SCALE), (dwk, sKF, None)):
                        for hd in range(16):
                            tmf = None
                            for kh in range(2):
                                twf = P.dma(
                                    "sync", sWqkF,
                                    wsrc.rearrange("(k p) m -> p k m", p=128)
                                    [:, kh * 8:(kh + 1) * 8,
                                     hd * 128:(hd + 1) * 128],
                                    [twf], sem="swfz")
                                tmf = P.do("tensor", [
                                    (lambda e_, k=k, kh=kh, s_=(kh == 0 and k == 0),
                                     t_=(kh == 1 and k == 7):
                                     e_.matmul(pb[1][:, 0:CH], sWqkF[:, k, :],
                                               sXf32[:, kh * 8 + k, :],
                                               start=s_, stop=t_))
                                    for k in range(8)],
                                    [twf, tprev_e, pb_free[1] if kh == 0 else None],
                                    inc="pe")
                                twf = tmf
                            if scl is None:
                                tevf = V(lambda e_, hd=hd, dstT=dstT: e_.tensor_copy(
                                    out=dstT[:, hd, :], in_=pb[1][:, 0:CH]),
                                    [tmf, tok_qkF[hd]])
                            else:
                                tevf = V(lambda e_, hd=hd, dstT=dstT, scl=scl:
                                         e_.tensor_scalar(
                                    out=dstT[:, hd, :], in0=pb[1][:, 0:CH],
                                    scalar1=scl, scalar2=None, op0=ALU.mult),
                                    [tmf, tok_qkF[hd]])
                            pb_free[1] = tevf
                            tok_qkF[hd] = tevf
            tok_prep = tok_qkF[15]

        if tau == 0 and "p1" in dump:
            P.dma("gpsimd", dumpten("xf32", [128, NKC, CH]), sXf32.ap(),
                  [tprev_e], sem="sdmp")
            P.dma("gpsimd", dumpten("qFr", [128, 16, CH]), sQF.ap(),
                  [tok_qkF[hd] for hd in range(16)], sem="sdmp")

        for h in range(_nh):
            hd0 = h * 4

            tws_q = [None] * 4
            tws_k = [None] * 4
            for d in (0, 1):
                tws_q[d] = P.dma("sync", sWqs[d % 2].ap(), dwqbf[h, d],
                                 [tok_wq_slot[d % 2]] + t_awb[("q", h)],
                                 sem=f"swq{d % 2}")
                tws_k[d] = P.dma("sync", sWks[d % 2].ap(), dwkbf[h, d],
                                 [tok_wk_slot[d % 2]] + t_awb[("k", h)],
                                 sem=f"swk{d % 2}")
            slvg = 0
            tws_v = P.dma("sync", sWvs[slvg].ap(), dwvbf[h],
                          [tok_wv_slot[slvg]] + t_awb[("v", h)], sem=f"swv{slvg}")
            tws_g = P.dma("sync", sWgs[slvg].ap(), dwgbf[h],
                          [tok_wg_slot[slvg]] + t_awb[("g", h)], sem=f"swg{slvg}")

            if _stg < 1:
                continue
            tok_ub = [None] * 4
            tok_ep = {}
            tok_em = {}
            tok_ebl = {}
            tok_bl = {}
            for d in range(4):
                hd = hd0 + d
                tg = []
                for half in range(2):
                    bank = 2 + half
                    usl = (d * 2 + half) % 4
                    tm = MM(pb[bank][:, 0:512],
                            [(sGk2[:, hd * 128:(hd + 1) * 128],
                              sT1[:, half * 512:(half + 1) * 512])],
                            [tok_t1, t_gk2, pb_free[bank]])
                    tev = V(lambda e_, bank=bank, usl=usl: e_.tensor_copy(
                        out=sUb[:, usl, :], in_=pb[bank][:, 0:512]),
                        [tm, tok_ub[usl], prev_vf, prev_sf])
                    pb_free[bank] = tev
                    tsg = SC(lambda e_, usl=usl, hd=hd: e_.activation(
                        out=sSig[:], in_=sUb[:, usl, :], func=AF.Sigmoid,
                        bias=sBgk2[:, hd:hd + 1], scale=1.0),
                        [tev, t_bgk2, prev_sf, prev_tln])
                    tok_ub[usl] = tsg
                    tln = SC(lambda e_, half=half: e_.activation(
                        out=sGb[:, half * 512:(half + 1) * 512], in_=sSig[:],
                        func=AF.Ln),
                        [tsg, tok_prep if tau == 0 else None])
                    prev_tln = tln
                    tg.append(tln)
                tsc = []
                for c in range(NCH1):
                    tsc.append(Vs(lambda e_, c=c: e_.tensor_tensor_scan(
                        out=sBb[:, c * 128:(c + 1) * 128],
                        data0=sGb[:, c * 128:(c + 1) * 128],
                        data1=sZb[:], initial=0.0, op0=ALU.add, op1=ALU.add),
                        [tg[c // 4], t_zb] + ([tsc[-1]] if tsc else [])))
                tok_ep[d] = SC(lambda e_, d=d: e_.activation(
                    out=sEp[:, d, :], in_=sBb[:], func=AF.Exp, scale=INV16), tsc)
                tok_em[d] = SC(lambda e_, d=d: e_.activation(
                    out=sEm[:, d, :], in_=sBb[:], func=AF.Exp, scale=-INV16), tsc)
                if tau == 0:
                    tok_epF = SC(lambda e_, d=d: e_.activation(
                        out=sEpF[:, d, :], in_=sBb[:, 0:CH], func=AF.Exp,
                        scale=INV16), tsc)
                    tok_emF = SC(lambda e_, d=d: e_.activation(
                        out=sEmF[:, d, :], in_=sBb[:, 0:CH], func=AF.Exp,
                        scale=-INV16), tsc)
                tbl = Vs(lambda e_, hd=hd: e_.tensor_copy(
                    out=sBl[:, hd, :], in_=sBb[:, 127::128]), tsc)
                tok_ebl[d] = SC(lambda e_, hd=hd: e_.activation(
                    out=sEbl[:, hd, :], in_=sBl[:, hd, :], func=AF.Exp,
                    scale=INV16), [tbl])
                tok_bl[d] = tbl

            blsl = sBl.ap()[:, hd0:hd0 + 4, :]
            w0 = [tok_bl[d] for d in range(4)] + [t_prev]
            tI = V(lambda e_, blsl=blsl: e_.tensor_copy(
                out=sI1.ap()[:, :, 0:1], in_=blsl[:, :, 0:1]), w0)
            tI = V(lambda e_, blsl=blsl: e_.tensor_add(
                out=sI1.ap()[:, :, 1:8], in0=blsl[:, :, 1:8],
                in1=blsl[:, :, 0:7]), w0 + [tI])
            tI = V(lambda e_: e_.tensor_copy(
                out=sI2.ap()[:, :, 0:2], in_=sI1.ap()[:, :, 0:2]), [tI])
            tI = V(lambda e_: e_.tensor_add(
                out=sI2.ap()[:, :, 2:8], in0=sI1.ap()[:, :, 2:8],
                in1=sI1.ap()[:, :, 0:6]), [tI])
            tI = V(lambda e_: e_.tensor_copy(
                out=sI1.ap()[:, :, 0:4], in_=sI2.ap()[:, :, 0:4]), [tI])
            tI = V(lambda e_: e_.tensor_add(
                out=sI1.ap()[:, :, 4:8], in0=sI2.ap()[:, :, 4:8],
                in1=sI2.ap()[:, :, 0:4]), [tI])
            tI = V(lambda e_, blsl=blsl: e_.tensor_sub(
                out=sI2.ap().rearrange("p a b -> p (a b)"),
                in0=sI1.ap().rearrange("p a b -> p (a b)"),
                in1=blsl.rearrange("p a b -> p (a b)")), [tI])
            tprevn = []
            for dd in range(4):
                hd = hd0 + dd
                tI2 = V(lambda e_, dd=dd, hd=hd: e_.tensor_scalar(
                    out=sI2.ap()[:, dd, :], in0=sI2.ap()[:, dd, :],
                    scalar1=sPrev[:, hd:hd + 1], scalar2=None, op0=ALU.add),
                    [tI])
                tprevn.append((hd, tI2))
            tok_pexp_h = SC(lambda e_, hd0=hd0: e_.activation(
                out=sPexp.ap()[:, hd0:hd0 + 4, :],
                in_=sI2.ap().rearrange("p a b -> p (a b)"),
                func=AF.Exp, scale=INV16), [t for _, t in tprevn])
            for dd in range(4):
                hd = hd0 + dd
                t_prev = V(lambda e_, dd=dd, hd=hd: e_.tensor_add(
                    out=sPrev[:, hd:hd + 1], in0=sI2.ap()[:, dd, 7:8],
                    in1=sBl[:, hd, 7:8]),
                    [tprevn[dd][1], tok_pexp_h, t_prev])

            if tau == 0:
                for d in range(4):
                    hd = hd0 + d
                    tok_qkF[hd] = V(lambda e_, hd=hd, d=d: e_.tensor_tensor(
                        out=sQF[:, hd, :], in0=sQF[:, hd, :],
                        in1=sEpF[:, d, :], op=ALU.mult),
                        [tok_qkF[hd], tok_epF])
                    tok_qkF[hd] = V(lambda e_, hd=hd, d=d: e_.tensor_tensor(
                        out=sKF[:, hd, :], in0=sKF[:, hd, :],
                        in1=sEmF[:, d, :], op=ALU.mult),
                        [tok_qkF[hd], tok_emF])

            if _stg < 2:
                continue
            tok_qe = {}
            tok_kd = {}
            last_qd = []
            for d in range(4):
                sl = d % 2
                for half in range(2):
                    bank = (d * 2 + half) % 2
                    tm = MM(pb[bank][:, 0:512],
                            [(sWqs[sl][:, k, :],
                              sXfm[:, k, half * 512:(half + 1) * 512])
                             for k in range(NKC)],
                            [tws_q[d], pb_free[bank]])
                    te = V(lambda e_, bank=bank, d=d, half=half: e_.tensor_tensor(
                        out=sQe[:, d, half * 512:(half + 1) * 512],
                        in0=pb[bank][:, 0:512],
                        in1=sEp[:, d, half * 512:(half + 1) * 512], op=ALU.mult),
                        [tm, tok_ep[d]])
                    tok_qe[(d, half)] = te
                    pb_free[bank] = te
                    for cc_ in range(4):
                        c = half * 4 + cc_
                        last_qd.append(V(lambda e_, d=d, c=c, hd=hd0 + d:
                                         e_.tensor_scalar(
                            out=sQd[:, d, c * 128:(c + 1) * 128],
                            in0=sQe[:, d, c * 128:(c + 1) * 128],
                            scalar1=sPexp[:, hd, c:c + 1],
                            scalar2=None, op0=ALU.mult),
                            [te, tok_pexp_h, tok_spill_qd]))
                    bank2 = 2 + (d * 2 + half) % 2
                    tm2 = MM(pb[bank2][:, 0:512],
                             [(sWks[sl][:, k, :],
                               sXfm[:, k, half * 512:(half + 1) * 512])
                              for k in range(NKC)],
                             [tws_k[d], pb_free[bank2]])
                    te2 = V(lambda e_, bank2=bank2, d=d, half=half: e_.tensor_tensor(
                        out=sKd[:, d, half * 512:(half + 1) * 512],
                        in0=pb[bank2][:, 0:512],
                        in1=sEm[:, d, half * 512:(half + 1) * 512], op=ALU.mult),
                        [tm2, tok_em[d]])
                    tok_kd[(d, half)] = te2
                    pb_free[bank2] = te2
                if d + 2 < 4:
                    tws_q[d + 2] = P.dma("sync", sWqs[sl].ap(), dwqbf[h, d + 2],
                                         [tm] + t_awb[("q", h)], sem=f"swq{sl}")
                    tws_k[d + 2] = P.dma("sync", sWks[sl].ap(), dwkbf[h, d + 2],
                                         [tm2] + t_awb[("k", h)], sem=f"swk{sl}")
            tok_wq_slot[0] = tok_qe[(2, 1)]
            tok_wq_slot[1] = tok_qe[(3, 1)]
            tok_wk_slot[0] = tok_kd[(2, 1)]
            tok_wk_slot[1] = tok_kd[(3, 1)]

            tok_spill_qd = P.dma("gpsimd", dqd[tau, h],
                                 sQd.ap().rearrange("p a b -> p (a b)"), last_qd,
                                 sem="sqd")

            if _stg < 3:
                continue
            tok_v = {}
            tok_gp = {}
            for c in range(NCH1):
                bank = c % 2
                tm = MM(pb[bank][:, 0:256],
                        [(sXfm[:, k, c * 128:(c + 1) * 128], sWvs[slvg][:, k, :])
                         for k in range(NKC)],
                        [tws_v, pb_free[bank]])
                tok_v[c] = SC(lambda e_, bank=bank, c=c: e_.copy(
                    out=sV1[:, c, :], in_=pb[bank][:, 0:256]), [tm])
                pb_free[bank] = tok_v[c]
                bank2 = 2 + c % 2
                tm2 = MM(pb[bank2][:, 0:256],
                         [(sXfm[:, k, c * 128:(c + 1) * 128], sWgs[slvg][:, k, :])
                          for k in range(NKC)],
                         [tws_g, pb_free[bank2]])
                tok_gp[c] = SC(lambda e_, bank2=bank2, c=c: e_.copy(
                    out=sG1[:, c, :], in_=pb[bank2][:, 0:256]),
                    [tm2, tok_spill_g])
                pb_free[bank2] = tok_gp[c]
                if c == NCH1 - 1:
                    tok_wv_slot[slvg] = tok_v[c]
                    tok_wg_slot[slvg] = tok_gp[c]
                    last_vg_mm = tm2
            tok_spill_g = P.dma("gpsimd", dgs[tau, h],
                                sG1.ap().rearrange("p a b -> p (a b)"),
                                [tok_gp[c] for c in range(NCH1)], sem="sgl")

            if _stg < 4:
                P.fence("vector"); P.fence("scalar")
                continue
            tok_olc = []
            tok_kkT_evac = None
            tok_Ard = None
            for c in range(NCH1):
                if tau == 0 and c == 0:
                    tA = MM(pb[2][:, 0:128],
                            [(sKF[:, hd0 + d, :], sQF[:, hd0 + d, :])
                             for d in range(4)],
                            [tok_qkF[hd0 + d] for d in range(4)] + [pb_free[2]])
                else:
                    tA = MM(pb[2][:, 0:128],
                            [(sKd[:, d, c * 128:(c + 1) * 128],
                              sQe[:, d, c * 128:(c + 1) * 128]) for d in range(4)],
                            [tok_kd[(d, c // 4)] for d in range(4)] +
                            [tok_qe[(d, c // 4)] for d in range(4)] + [pb_free[2]])
                tAm = V(lambda e_: e_.tensor_tensor(
                    out=sAT[:], in0=pb[2][:, 0:128], in1=sMask[:], op=ALU.mult),
                    [tA, t_mask, tok_Ard])
                pb_free[2] = tAm
                if _stg < 5:
                    continue
                tO = MM(pb[3][:, 0:256],
                        [(sAT[:], sV1[:, c, :])] +
                        [(sQe[:, d, c * 128:(c + 1) * 128], sS[:, hd0 + d, :])
                         for d in range(4)],
                        [tAm, tok_v[c], pb_free[3], t_s0] +
                        [tok_Sevac[hd0 + d] for d in range(4)])
                tOe = SC(lambda e_, c=c: e_.copy(
                    out=sOl[:, c, :], in_=pb[3][:, 0:256]),
                    [tO, tok_spill_ol])
                pb_free[3] = tOe
                tok_olc.append(tOe)
                tok_Ard = tO
                if _stg < 6:
                    continue
                tkk = []
                tdg = []
                for d in range(4):
                    tkk.append(V(lambda e_, d=d, c=c, hd=hd0 + d: e_.tensor_scalar(
                        out=sKkf[:, d, :], in0=sKd[:, d, c * 128:(c + 1) * 128],
                        scalar1=sEbl[:, hd, c:c + 1], scalar2=None, op0=ALU.mult),
                        [tok_kd[(d, c // 4)], tok_ebl[d], tok_kkT_evac]))
                    tdg.append(V(lambda e_, d=d, c=c, hd=hd0 + d: e_.tensor_scalar(
                        out=sDiag[:, d, :], in0=sId[:],
                        scalar1=sEbl[:, hd, c:c + 1], scalar2=None, op0=ALU.mult),
                        [t_id, tok_ebl[d]]))
                pbkk = pb[6].ap().bitcast(BF16)
                ttr = []
                for d in range(4):
                    ttr.append(TR(pbkk[:, d * 128:(d + 1) * 128], sKkf[:, d, :],
                                  [tkk[d], pb_free[6]]))
                tok_kkT_evac = V(lambda e_: e_.tensor_copy(
                    out=sKkT[:], in_=pbkk[:, 0:512]), [ttr[-1]])
                pb_free[6] = tok_kkT_evac
                if _stg < 7:
                    continue
                smm = []
                for d in range(4):
                    bank = 4 + d // 2
                    off = (d % 2) * 256
                    smm.append(MM(
                        pb[bank][:, off:off + 256],
                        [(sDiag[:, d, :], sS[:, hd0 + d, :]),
                         (sKkT[:, d * 128:(d + 1) * 128], sV1[:, c, :])],
                        [tdg[d], tok_kkT_evac, tO, tok_Sevac[hd0 + d],
                         pb_free[bank]]))
                for d in range(4):
                    bank = 4 + d // 2
                    off = (d % 2) * 256
                    if d < 2:
                        tok_Sevac[hd0 + d] = V(
                            lambda e_, bank=bank, off=off, hd=hd0 + d:
                            e_.tensor_copy(out=sS[:, hd, :],
                                           in_=pb[bank][:, off:off + 256]),
                            [smm[0], smm[1]])
                    else:
                        tok_Sevac[hd0 + d] = SC(
                            lambda e_, bank=bank, off=off, hd=hd0 + d:
                            e_.copy(out=sS[:, hd, :],
                                    in_=pb[bank][:, off:off + 256]),
                            [smm[2], smm[3]])
                pb_free[4] = tok_Sevac[hd0 + 1]
                pb_free[5] = tok_Sevac[hd0 + 3]

            tok_spill_ol = P.dma("gpsimd", dol[tau, h],
                                 sOl.ap().rearrange("p a b -> p (a b)"), tok_olc,
                                 sem="sol")

            prev_vf = P.fence("vector")
            prev_sf = P.fence("scalar")
            if h == H - 1:
                tok_xfm_free = last_vg_mm

        if "p1" in dump and tau == 0:
            P.dma("gpsimd", dumpten("qF", [128, 16, CH]), sQF.ap(),
                  [tok_qkF[hd] for hd in range(16)], sem="sdmp")
            P.dma("gpsimd", dumpten("kF", [128, 16, CH]), sKF.ap(),
                  [tok_qkF[hd] for hd in range(16)], sem="sdmp")
            wfin = [prev_vf, prev_sf, tok_spill_qd, tok_spill_ol, tok_spill_g]
            P.dma("gpsimd", dumpten("qe", [128, 4, TS], BF16), sQe.ap(), wfin, sem="sdmp")
            P.dma("gpsimd", dumpten("kd", [128, 4, TS], BF16), sKd.ap(), wfin, sem="sdmp")
            P.dma("gpsimd", dumpten("v1", [128, NCH1, DV], BF16), sV1.ap(), wfin, sem="sdmp")
            P.dma("gpsimd", dumpten("g1", [128, NCH1, DV], BF16), sG1.ap(), wfin, sem="sdmp")
            P.dma("gpsimd", dumpten("ol", [128, NCH1, DV], BF16), sOl.ap(), wfin, sem="sdmp")
            P.dma("gpsimd", dumpten("qd", [128, 4, TS], BF16), sQd.ap(), wfin, sem="sdmp")
            P.dma("gpsimd", dumpten("S1", [128, 16, DV], BF16), sS.ap(), wfin, sem="sdmp")
            P.dma("gpsimd", dumpten("prev", [128, 16]), sPrev[:], wfin, sem="sdmp")
            P.dma("gpsimd", dumpten("ebl", [128, 16, NCH1]), sEbl.ap(), wfin, sem="sdmp")
            P.dma("gpsimd", dumpten("pexp", [128, 16, NCH1]), sPexp.ap(), wfin, sem="sdmp")
            P.dma("gpsimd", dumpten("bl", [128, 16, NCH1]), sBl.ap(), wfin, sem="sdmp")
            P.dma("gpsimd", dumpten("bb", [128, TS]), sBb[:], wfin, sem="sdmp")

    last_S = list(tok_Sevac)
    t_cca = P.dma("gpsimd", dccinA, sPrev[:], [t_prev], sem="scc")
    t_ccz = P.dma("gpsimd", dccinZ,
                  sS.ap().rearrange("p a b -> p (a b)"), last_S + [t_cca],
                  sem="scc")

    if "z" in dump:
        P.dma("gpsimd", dumpten("A", [128, 16]), sPrev[:], [t_prev], sem="sdmp")
        P.dma("gpsimd", dumpten("Z", [128, 16, DV], BF16), sS.ap(), last_S, sem="sdmp")

    if upto == "p1":
        for sem, q in P.dma_sems.items():
            if P.counts.get(sem, 0) > 0:
                P.do(q, [], waits=[(sem, P.counts[sem], "none", 16)])
        st1.close()
        st.close()
        P.emit()
        return nc

    t_cc = P.do("gpsimd",
                lambda e_: e_.collective_compute(
                    "AllGather", ALU.bypass,
                    replica_groups=[[0, 1, 2, 3], [4, 5, 6, 7]],
                    ins=[dccinA.opt()], outs=[dccoutA.opt()]),
                [t_cca], inc="cc")
    t_cc = P.do("gpsimd",
                lambda e_: e_.collective_compute(
                    "AllGather", ALU.bypass,
                    replica_groups=[[0, 1, 2, 3], [4, 5, 6, 7]],
                    ins=[dccinZ.opt()], outs=[dccoutZ.opt()]),
                [t_ccz, t_cc], inc="cc")

    st1.close()
    P.barrier()

    stset = ExitStack()
    es = stset.enter_context
    sAga = es(nc.sbuf_tensor("sAga", [128, 4, 16], F32))
    sC = es(nc.sbuf_tensor("sC", [128, 3, 16], F32))
    sD = es(nc.sbuf_tensor("sD", [128, 16], F32))
    sDmc = es(nc.sbuf_tensor("sDmc", [128, 16], F32))
    sCoef = es(nc.sbuf_tensor("sCoef", [128, 3, 16], F32))
    sZld = [es(nc.sbuf_tensor(f"sZld{i}", [128, 4096], BF16)) for i in range(2)]

    t_aga = P.dma("gpsimd", sAga[:],
                  dccoutA.rearrange("(r p) m -> p r m", p=128), [t_cc], sem="sag")
    t_c = V(lambda e_: e_.tensor_copy(out=sC[:, 0, :], in_=sAga[:, 0, :]), [t_aga])
    t_c = V(lambda e_: e_.tensor_add(out=sC[:, 1, :], in0=sC[:, 0, :],
                                     in1=sAga[:, 1, :]), [t_c])
    t_c = V(lambda e_: e_.tensor_add(out=sC[:, 2, :], in0=sC[:, 1, :],
                                     in1=sAga[:, 2, :]), [t_c])
    t_d = V(lambda e_: e_.tensor_scalar(
        out=sD[:], in0=sAga[:, 0, :], scalar1=sSel01[:, 0:1], scalar2=None,
        op0=ALU.mult), [t_aga, t_sel])
    for i in (1, 2):
        t_d = V(lambda e_, i=i: e_.scalar_tensor_tensor(
            out=sD[:], in0=sAga[:, i, :], scalar=sSel01[:, i:i + 1], in1=sD[:],
            op0=ALU.mult, op1=ALU.add), [t_d])
    t_coef = []
    for i in range(3):
        tdm = V(lambda e_, i=i: e_.tensor_sub(
            out=sDmc[:], in0=sD[:], in1=sC[:, i, :]),
            [t_d, t_c, t_coef[-1] if t_coef else None])
        tcp = SC(lambda e_, i=i: e_.activation(
            out=sCoef[:, i, :], in_=sDmc[:], func=AF.Exp,
            scale=sSelsc[:, i:i + 1], bias=sSelbi[:, i:i + 1]),
            [tdm, t_selsc, t_selbi])
        t_coef.append(V(lambda e_, i=i: e_.tensor_scalar(
            out=sCoef[:, i, :], in0=sCoef[:, i, :],
            scalar1=sSel01[:, i:i + 1], scalar2=None, op0=ALU.mult), [tcp]))

    t_sacc = V(lambda e_: e_.memset(sS.ap().rearrange("p a b -> p (a b)"), 0.0),
               [t_cc])
    t_zld = [None, None]
    for i in range(3):
        b = i % 2
        t_zld[b] = P.dma("gpsimd", sZld[b][:],
                         dccoutZ[i * 128:(i + 1) * 128, :],
                         [t_cc, t_zld[b], t_sacc if i < 2 else None],
                         sem=f"sz{b}")
        for hd in range(16):
            t_sacc = V(lambda e_, i=i, hd=hd, b=b: e_.scalar_tensor_tensor(
                out=sS[:, hd, :], in0=sZld[b][:, hd * 256:(hd + 1) * 256],
                scalar=sCoef[:, i, hd:hd + 1], in1=sS[:, hd, :],
                op0=ALU.mult, op1=ALU.add),
                [t_zld[b], t_coef[i], t_sacc])
    tok_Sin = t_sacc

    if "sin" in dump:
        P.dma("gpsimd", dumpten("Sin", [128, 16, DV], BF16), sS.ap(), [tok_Sin], sem="sdmp")
    if upto == "cc":
        for sem, q in P.dma_sems.items():
            if P.counts.get(sem, 0) > 0:
                P.do(q, [], waits=[(sem, P.counts[sem], "none", 16)])
        stset.close()
        st.close()
        P.emit()
        return nc

    stset.close()
    P.barrier()

    st2 = ExitStack()
    e2 = st2.enter_context
    sX1tm = e2(nc.sbuf_tensor("sX1tm", [128, 4, D], BF16))
    sX1fm = e2(nc.sbuf_tensor("sX1fm", [128, 16, TF], BF16))
    sR2 = e2(nc.sbuf_tensor("sR2", [128, 4, D], BF16))
    sSqD = e2(nc.sbuf_tensor("sSqD", [128, D], BF16))
    sSsq = e2(nc.sbuf_tensor("sSsq", [128, 16], F32))
    sRstd = e2(nc.sbuf_tensor("sRstd", [128, 16], F32))
    sVar = e2(nc.sbuf_tensor("sVar", [128, 16], F32))
    sSd = e2(nc.sbuf_tensor("sSd", [128, 16], F32))
    sLNa = e2(nc.sbuf_tensor("sLNa", [128, 4, 4], F32))
    sLNs = e2(nc.sbuf_tensor("sLNs", [128, 4], F32))
    sLNq = e2(nc.sbuf_tensor("sLNq", [128, 4], F32))
    sLNm = e2(nc.sbuf_tensor("sLNm", [128, 4], F32))
    sLNr = e2(nc.sbuf_tensor("sLNr", [128, 4], F32))
    sLNt = e2(nc.sbuf_tensor("sLNt", [128, 4], F32))
    sLNv = e2(nc.sbuf_tensor("sLNv", [128, 4], F32))
    sWgj = [e2(nc.sbuf_tensor(f"sWgj{i}", [128, 16, 128], BF16)) for i in range(2)]
    sWuj = [e2(nc.sbuf_tensor(f"sWuj{i}", [128, 16, 128], BF16)) for i in range(2)]
    sWdk = [e2(nc.sbuf_tensor(f"sWdk{i}", [128, 1024], BF16)) for i in range(2)]
    sOut = [e2(nc.sbuf_tensor(f"sOut{i}", [128, D], F32)) for i in range(2)]

    slot_wgj = [None, None]
    slot_wuj = [None, None]
    slot_wdk = [None, None]
    slot_out = [None, None]
    tok_prevB = []

    for t2 in range(NT2):
        tau = t2 // 2
        lc0 = (t2 % 2) * 4
        tk0 = t2 * TF

        stA = ExitStack()
        eA = stA.enter_context
        sQds = [eA(nc.sbuf_tensor(f"sQds{i}_{t2}", [128, 4, TF], BF16)) for i in range(2)]
        sOls = [eA(nc.sbuf_tensor(f"sOls{i}_{t2}", [128, 4, DV], BF16)) for i in range(2)]
        sGs = [eA(nc.sbuf_tensor(f"sGs{i}_{t2}", [128, 4, DV], BF16)) for i in range(2)]
        sOfull = eA(nc.sbuf_tensor(f"sOfull_{t2}", [128, 16, DV], BF16))
        sSgw = eA(nc.sbuf_tensor(f"sSgw_{t2}", [128, 16, DV], BF16))
        sTmp = eA(nc.sbuf_tensor(f"sTmp_{t2}", [128, DV], BF16))
        sOtm = eA(nc.sbuf_tensor(f"sOtm_{t2}", [128, 4, VD], BF16))
        sOfm = eA(nc.sbuf_tensor(f"sOfm_{t2}", [128, 8, TF], BF16))
        sXtm2 = eA(nc.sbuf_tensor(f"sXtm2_{t2}", [128, 4, D], BF16))
        sWos = [eA(nc.sbuf_tensor(f"sWos{i}_{t2}", [128, 8, 512], BF16)) for i in range(2)]

        slot_qd2 = [None, None]
        slot_ol2 = [None, None]
        slot_g2 = [None, None]
        t_sq = []
        t_ofull = {}
        t_sgw = {}
        for h in range(H):
            b = h % 2
            tqd = P.dma("gpsimd", sQds[b].ap(),
                        dqd[tau, h][:, :, (t2 % 2) * TF:(t2 % 2) * TF + TF],
                        [slot_qd2[b], tok_spill_qd] + tok_prevB, sem=f"rq{b}")
            tol = P.dma("gpsimd", sOls[b].ap(), dol[tau, h][:, lc0:lc0 + 4, :],
                        [slot_ol2[b], tok_spill_ol] + tok_prevB, sem=f"ro{b}")
            tgl = P.dma("gpsimd", sGs[b].ap(), dgs[tau, h][:, lc0:lc0 + 4, :],
                        [slot_g2[b], tok_spill_g] + tok_prevB, sem=f"rg{b}")
            last_corr = None
            for tc in range(4):
                i = h * 4 + tc
                bank = tc % 2
                tm = MM(pb[bank][:, 0:256],
                        [(sQds[b][:, d, tc * 128:(tc + 1) * 128],
                          sS[:, h * 4 + d, :]) for d in range(4)],
                        [tqd, tok_Sin, pb_free[bank]])
                last_corr = tm
                tof = V(lambda e_, bank=bank, i=i, b=b, tc=tc:
                        e_.scalar_tensor_tensor(
                    out=sOfull[:, i, :], in0=pb[bank][:, 0:256], scalar=0.0,
                    in1=sOls[b][:, tc, :], op0=ALU.bypass, op1=ALU.add),
                    [tm, tol])
                pb_free[bank] = tof
                t_ofull[i] = tof
                t_sq.append(SC(lambda e_, i=i: e_.activation(
                    out=sSqD[:, 0:DV], in_=sOfull[:, i, :], func=AF.Square,
                    accum_out=sSsq[:, i:i + 1]), [tof]))
                tsg = SC(lambda e_, i=i, b=b, tc=tc: e_.activation(
                    out=sSgw[:, i, :], in_=sGs[b][:, tc, :], func=AF.Silu),
                    [tgl])
                t_sgw[i] = V(lambda e_, i=i: e_.tensor_tensor(
                    out=sSgw[:, i, :], in0=sSgw[:, i, :], in1=sWnB[:],
                    op=ALU.mult), [tsg, t_wnb])
            slot_qd2[b] = last_corr
            slot_ol2[b] = t_ofull[h * 4 + 3]
            slot_g2[b] = t_sgw[h * 4 + 3]

        tv_ = V(lambda e_: e_.tensor_scalar(
            out=sVar[:], in0=sSsq[:], scalar1=1.0 / DV, scalar2=None,
            op0=ALU.mult), t_sq)
        ts_ = SC(lambda e_: e_.activation(out=sSd[:], in_=sVar[:],
                                          func=AF.Ln, bias=sEps[:, 0:1], scale=1.0),
                 [tv_, t_eps])
        t_rstd = SC(lambda e_: e_.activation(out=sRstd[:], in_=sSd[:],
                                             func=AF.Exp, scale=-0.5), [ts_])

        t_otm = {}
        prev_otm = None
        for h in range(H):
            for tc in range(4):
                i = h * 4 + tc
                tt = V(lambda e_, i=i: e_.tensor_scalar(
                    out=sTmp[:], in0=sOfull[:, i, :], scalar1=sRstd[:, i:i + 1],
                    scalar2=None, op0=ALU.mult), [t_rstd, t_ofull[i], prev_otm])
                t_otm[(tc, h)] = V(lambda e_, i=i, tc=tc, h=h: e_.tensor_tensor(
                    out=sOtm[:, tc, h * DV:(h + 1) * DV], in0=sTmp[:],
                    in1=sSgw[:, i, :], op=ALU.mult), [tt, t_sgw[i]])
                prev_otm = t_otm[(tc, h)]

        t_ofm = {}
        for tc in range(4):
            pbk = pb[6 + tc % 2].ap().bitcast(BF16)
            ttr = []
            for v in range(8):
                ttr.append(TR(pbk[:, v * 128:(v + 1) * 128],
                              sOtm[:, tc, v * 128:(v + 1) * 128],
                              [t_otm[(tc, hh)] for hh in range(H)] +
                              [pb_free[6 + tc % 2]]))
            t_ofm[tc] = V(lambda e_, tc=tc, pbk=pbk: e_.tensor_copy(
                out=sOfm.ap()[:, 0:8, tc * 128:(tc + 1) * 128],
                in_=pbk[:, 0:1024].rearrange("p (a b) -> p a b", a=8)),
                [ttr[-1]])
            pb_free[6 + tc % 2] = t_ofm[tc]

        t_xtm = []
        for tc in range(4):
            t_xtm.append(P.dma(
                "sync", sXtm2[:, tc, :],
                dxbf[tk0 + tc * 128: tk0 + (tc + 1) * 128, :],
                [t_xbf[(tk0 + tc * 128) // 128]] + tok_prevB, sem="rxt"))

        t_x1 = {}
        slot_wo = [None, None]
        for og in range(4):
            b = og % 2
            two = P.dma("sync", sWos[b].ap(), dwobf[og],
                        [slot_wo[b]] + t_awb[("o", og)] + tok_prevB,
                        sem=f"rwo{b}")
            last_womm = None
            for tc in range(4):
                bank = (og * 4 + tc) % 4
                tm = MM(pb[bank][:, 0:512],
                        [(sOfm[:, k, tc * 128:(tc + 1) * 128], sWos[b][:, k, :])
                         for k in range(8)],
                        [t_ofm[tc], two, pb_free[bank]])
                last_womm = tm
                te = V(lambda e_, bank=bank, tc=tc, og=og: e_.scalar_tensor_tensor(
                    out=sX1tm[:, tc, og * 512:(og + 1) * 512],
                    in0=pb[bank][:, 0:512], scalar=0.0,
                    in1=sXtm2[:, tc, og * 512:(og + 1) * 512],
                    op0=ALU.bypass, op1=ALU.add,
                    accum_out=sLNa[:, tc, og:og + 1]),
                    [tm] + t_xtm)
                pb_free[bank] = te
                t_x1[(og, tc)] = te
            slot_wo[b] = last_womm

        t_ln1 = []
        for tc in range(4):
            tsum = V(lambda e_, tc=tc: e_.reduce_sum(
                out=sLNs[:, tc:tc + 1], in_=sLNa[:, tc, :], axis=AXX),
                [t_x1[(og2, tc)] for og2 in range(4)])
            tssq = SC(lambda e_, tc=tc: e_.activation(
                out=sSqD[:], in_=sX1tm[:, tc, :], func=AF.Square,
                accum_out=sLNq[:, tc:tc + 1]),
                [t_x1[(og2, tc)] for og2 in range(4)] + [t_sq[-1]])
            tm_ = V(lambda e_, tc=tc: e_.tensor_scalar(
                out=sLNm[:, tc:tc + 1], in0=sLNs[:, tc:tc + 1],
                scalar1=1.0 / D, scalar2=None, op0=ALU.mult), [tsum])
            tms = V(lambda e_, tc=tc: e_.tensor_mul(
                out=sLNt[:, tc:tc + 1], in0=sLNm[:, tc:tc + 1],
                in1=sLNm[:, tc:tc + 1]), [tm_])
            tvq = V(lambda e_, tc=tc: e_.tensor_scalar(
                out=sLNv[:, tc:tc + 1], in0=sLNq[:, tc:tc + 1],
                scalar1=1.0 / D, scalar2=None, op0=ALU.mult), [tssq])
            tvar = V(lambda e_, tc=tc: e_.tensor_sub(
                out=sLNr[:, tc:tc + 1], in0=sLNv[:, tc:tc + 1],
                in1=sLNt[:, tc:tc + 1]), [tvq, tms])
            tsq2 = SC(lambda e_, tc=tc: e_.activation(
                out=sLNt[:, tc:tc + 1], in_=sLNr[:, tc:tc + 1], func=AF.Ln,
                bias=sEps[:, 0:1], scale=1.0), [tvar, t_eps])
            trc = SC(lambda e_, tc=tc: e_.activation(
                out=sLNr[:, tc:tc + 1], in_=sLNt[:, tc:tc + 1], func=AF.Exp,
                scale=-0.5), [tsq2])
            t_ln1.append(V(lambda e_, tc=tc: e_.tensor_scalar(
                out=sX1tm[:, tc, :], in0=sX1tm[:, tc, :],
                scalar1=sLNm[:, tc:tc + 1], scalar2=sLNr[:, tc:tc + 1],
                op0=ALU.subtract, op1=ALU.mult), [trc]))

        if "x1" in dump and t2 == 0:
            P.dma("gpsimd", dumpten("wnb", [128, DV], BF16), sWnB[:], [t_wnb], sem="sdmp")
            P.dma("gpsimd", dumpten("ols1", [128, 4, DV], BF16), sOls[1].ap(),
                  list(t_ofull.values()), sem="sdmp")
            P.dma("gpsimd", dumpten("gs0", [128, 4, DV], BF16), sGs[0].ap(),
                  list(t_sgw.values()), sem="sdmp")
            P.dma("gpsimd", dumpten("rstd", [128, 16]), sRstd[:],
                  [t_rstd] + list(t_otm.values()), sem="sdmp")
            P.dma("gpsimd", dumpten("sgw", [128, 16, DV], BF16), sSgw.ap(),
                  list(t_sgw.values()) + list(t_otm.values()), sem="sdmp")
            P.dma("gpsimd", dumpten("ssq", [128, 16]), sSsq[:], t_sq + [t_rstd], sem="sdmp")
            P.dma("gpsimd", dumpten("x1tm", [128, 4, D], BF16), sX1tm.ap(), t_ln1, sem="sdmp")
            P.dma("gpsimd", dumpten("ofull", [128, 16, DV], BF16), sOfull.ap(),
                  list(t_otm.values()), sem="sdmp")
            P.dma("gpsimd", dumpten("otm", [128, 4, VD], BF16), sOtm.ap(),
                  list(t_otm.values()) + [t_ofm[3]], sem="sdmp")

        t_x1fm = []
        for tc in range(4):
            for half in range(2):
                pbk = pb[6 + half].ap().bitcast(BF16)
                ttr = []
                for f in range(8):
                    fc = half * 8 + f
                    ttr.append(TR(pbk[:, f * 128:(f + 1) * 128],
                                  sX1tm[:, tc, fc * 128:(fc + 1) * 128],
                                  [t_ln1[tc], pb_free[6 + half]]))
                tev = V(lambda e_, tc=tc, half=half, pbk=pbk: e_.tensor_copy(
                    out=sX1fm.ap()[:, half * 8:(half + 1) * 8,
                                   tc * 128:(tc + 1) * 128],
                    in_=pbk[:, 0:1024].rearrange("p (a b) -> p a b", a=8)),
                    [ttr[-1]])
                pb_free[6 + half] = tev
                t_x1fm.append(tev)

        stA.close()

        stB = ExitStack()
        eB = stB.enter_context
        sFuse = eB(nc.sbuf_tensor(f"sFuse_{t2}", [128, NJ, TF], BF16))
        sSilu = eB(nc.sbuf_tensor(f"sSilu_{t2}", [128, TF], BF16))

        t_fuse = {}
        prev_tfu = None
        for j in range(NJ):
            b = j % 2
            twg = P.dma("sync", sWgj[b].ap(), dwgfb[j],
                        [slot_wgj[b], t_ffnw[("g", j)]], sem=f"rwg{b}")
            twu = P.dma("sync", sWuj[b].ap(), dwufb[j],
                        [slot_wuj[b], t_ffnw[("u", j)]], sem=f"rwu{b}")
            bank = (j % 2) * 2
            bank2 = bank + 1
            tmg = MM(pb[bank][:, 0:512],
                     [(sWgj[b][:, k, :], sX1fm[:, k, :]) for k in range(NKC)],
                     t_x1fm + [twg, pb_free[bank]])
            tmu = MM(pb[bank2][:, 0:512],
                     [(sWuj[b][:, k, :], sX1fm[:, k, :]) for k in range(NKC)],
                     [twu, pb_free[bank2]])
            slot_wgj[b] = tmg
            slot_wuj[b] = tmu
            tsl = SC(lambda e_, bank=bank, j=j: e_.activation(
                out=sSilu[:], in_=pb[bank][:, 0:512], func=AF.Silu,
                scale=sSg[:, j:j + 1]), [tmg, t_sg, prev_tfu])
            pb_free[bank] = tsl
            tfu = V(lambda e_, bank2=bank2, j=j: e_.scalar_tensor_tensor(
                out=sFuse[:, j, :], in0=pb[bank2][:, 0:512],
                scalar=sSu[:, j:j + 1], in1=sSilu[:],
                op0=ALU.mult, op1=ALU.mult), [tmu, tsl, t_su])
            pb_free[bank2] = tfu
            t_fuse[j] = tfu
            prev_tfu = tfu
        all_fuse = [t_fuse[j] for j in range(NJ)]

        t_r2 = {}
        for ps in range(2):
            groups = [(tc, ps * 2 + og) for og in range(2) for tc in range(4)]
            smm = [None] * 8
            for k in range(64):
                b = k % 2
                twd = P.dma("sync", sWdk[b][:],
                            dwdfb[k][:, ps * 1024:(ps + 1) * 1024],
                            [slot_wdk[b], t_ffnw[("d", k)]], sem=f"rwd{b}")
                for gi, (tc, og) in enumerate(groups):
                    w = [twd]
                    if k == 0:
                        w = w + all_fuse + [pb_free[gi]]
                    l = sFuse[:, k, tc * 128:(tc + 1) * 128]
                    r = sWdk[b].ap()[:, (og - ps * 2) * 512:(og - ps * 2 + 1) * 512]
                    smm[gi] = P.do(
                        "tensor",
                        lambda e_, l=l, r=r, o=pb[gi], s=(k == 0), tt=(k == 63):
                        e_.matmul(o[:, 0:512], l, r, start=s, stop=tt),
                        w, inc="pe")
                slot_wdk[b] = smm[7]
            for gi, (tc, og) in enumerate(groups):
                te = V(lambda e_, gi=gi, tc=tc, og=og: e_.scalar_tensor_tensor(
                    out=sR2[:, tc, og * 512:(og + 1) * 512], in0=pb[gi][:, 0:512],
                    scalar=0.0, in1=sX1tm[:, tc, og * 512:(og + 1) * 512],
                    op0=ALU.bypass, op1=ALU.add,
                    accum_out=sLNa[:, tc, og:og + 1]),
                    [smm[gi], t_ln1[tc]])
                pb_free[gi] = te
                t_r2[(og, tc)] = te

        t_store = []
        for tc in range(4):
            tsum = V(lambda e_, tc=tc: e_.reduce_sum(
                out=sLNs[:, tc:tc + 1], in_=sLNa[:, tc, :], axis=AXX),
                [t_r2[(og2, tc)] for og2 in range(4)])
            tssq = SC(lambda e_, tc=tc: e_.activation(
                out=sSqD[:], in_=sR2[:, tc, :], func=AF.Square,
                accum_out=sLNq[:, tc:tc + 1]),
                [t_r2[(og2, tc)] for og2 in range(4)])
            tm_ = V(lambda e_, tc=tc: e_.tensor_scalar(
                out=sLNm[:, tc:tc + 1], in0=sLNs[:, tc:tc + 1],
                scalar1=1.0 / D, scalar2=None, op0=ALU.mult), [tsum])
            tms = V(lambda e_, tc=tc: e_.tensor_mul(
                out=sLNt[:, tc:tc + 1], in0=sLNm[:, tc:tc + 1],
                in1=sLNm[:, tc:tc + 1]), [tm_])
            tvq = V(lambda e_, tc=tc: e_.tensor_scalar(
                out=sLNv[:, tc:tc + 1], in0=sLNq[:, tc:tc + 1],
                scalar1=1.0 / D, scalar2=None, op0=ALU.mult), [tssq])
            tvar = V(lambda e_, tc=tc: e_.tensor_sub(
                out=sLNr[:, tc:tc + 1], in0=sLNv[:, tc:tc + 1],
                in1=sLNt[:, tc:tc + 1]), [tvq, tms])
            tsq2 = SC(lambda e_, tc=tc: e_.activation(
                out=sLNt[:, tc:tc + 1], in_=sLNr[:, tc:tc + 1], func=AF.Ln,
                bias=sEps[:, 0:1], scale=1.0), [tvar, t_eps])
            trc = SC(lambda e_, tc=tc: e_.activation(
                out=sLNr[:, tc:tc + 1], in_=sLNt[:, tc:tc + 1], func=AF.Exp,
                scale=-0.5), [tsq2])
            b = tc % 2
            tap = V(lambda e_, tc=tc, b=b: e_.tensor_scalar(
                out=sOut[b][:], in0=sR2[:, tc, :],
                scalar1=sLNm[:, tc:tc + 1], scalar2=sLNr[:, tc:tc + 1],
                op0=ALU.subtract, op1=ALU.mult), [trc, slot_out[b]])
            tst = P.dma("sync", dy[tk0 + tc * 128: tk0 + (tc + 1) * 128, :],
                        sOut[b][:], [tap], sem=f"ry{b}")
            slot_out[b] = tst
            t_store.append(tst)

        if "ln2" in dump and t2 == 0:
            P.dma("gpsimd", dumpten("r2", [128, 4, D], BF16), sR2.ap(),
                  list(t_r2.values()) + t_store, sem="sdmp")
            P.dma("gpsimd", dumpten("ln2a", [128, 4, 4]), sLNa.ap(),
                  list(t_r2.values()) + t_store, sem="sdmp")
            P.dma("gpsimd", dumpten("ln2s", [128, 4]), sLNs[:], t_store, sem="sdmp")
            P.dma("gpsimd", dumpten("ln2q", [128, 4]), sLNq[:], t_store, sem="sdmp")
            P.dma("gpsimd", dumpten("ln2m", [128, 4]), sLNm[:], t_store, sem="sdmp")
            P.dma("gpsimd", dumpten("ln2r", [128, 4]), sLNr[:], t_store, sem="sdmp")
            P.dma("gpsimd", dumpten("fuse0", [128, 8, TF], BF16),
                  sFuse.ap()[:, 0:8, :], list(t_fuse.values()), sem="sdmp")
        stB.close()
        tok_prevB = [smm[7], t_fuse[NJ - 1]]

    for sem, q in P.dma_sems.items():
        if P.counts.get(sem, 0) > 0:
            P.do(q, [], waits=[(sem, P.counts[sem], "none", 16)])
    P.do("sync", [], waits=t_store)

    st2.close()
    st.close()
    P.emit()
    return nc


_BUILD_CACHE = {}


def _get_nc():
    if "nc" not in _BUILD_CACHE:
        _BUILD_CACHE["nc"] = build("full")
    return _BUILD_CACHE["nc"]


def prep_shared(i):
    d = {}
    d["wq"] = np.ascontiguousarray(np.asarray(i["L0_attn_Wq"], dtype=np.float32))
    d["wk"] = np.ascontiguousarray(np.asarray(i["L0_attn_Wk"], dtype=np.float32))
    d["wv"] = np.ascontiguousarray(np.asarray(i["L0_attn_Wv"], dtype=np.float32))
    d["wg"] = np.ascontiguousarray(np.asarray(i["L0_attn_Wg"], dtype=np.float32))
    d["wgk1"] = np.ascontiguousarray(np.asarray(i["L0_attn_Wgk1"], dtype=np.float32))
    d["wgk2"] = np.ascontiguousarray(np.asarray(i["L0_attn_Wgk2"], dtype=np.float32))
    d["wo"] = np.ascontiguousarray(np.asarray(i["L0_attn_Wo"], dtype=np.float32))
    d["bgk2pf"] = np.ascontiguousarray(
        np.asarray(i["L0_attn_bgk2"], dtype=np.float32).reshape(16, 128).T)
    d["wnormr"] = np.ascontiguousarray(
        np.asarray(i["L0_attn_Wnorm"], dtype=np.float32).reshape(1, DV))
    d["sgpf"] = np.ascontiguousarray(
        np.asarray(i["L0_ffn_s_gate"], dtype=np.float32).reshape(NJ, 128).T)
    d["supf"] = np.ascontiguousarray(
        np.asarray(i["L0_ffn_s_up"], dtype=np.float32).reshape(NJ, 128).T)
    d["sdnr"] = np.ascontiguousarray(
        np.asarray(i["L0_ffn_s_down"], dtype=np.float32).reshape(1, D))
    wgt = np.asarray(i["L0_ffn_w_gate"])
    d["wg8t"] = np.ascontiguousarray(
        wgt.reshape(16, 128, NJ, 128).transpose(2, 1, 0, 3)
        .reshape(NJ, 128, 16 * 128))
    wut = np.asarray(i["L0_ffn_w_up"])
    d["wu8t"] = np.ascontiguousarray(
        wut.reshape(16, 128, NJ, 128).transpose(2, 1, 0, 3)
        .reshape(NJ, 128, 16 * 128))
    d["wd8"] = np.ascontiguousarray(
        np.asarray(i["L0_ffn_w_down"]).reshape(64, 128, D))
    d["identb"] = np.eye(128, dtype=NPBF)
    s_ = np.arange(128)[:, None]
    t_ = np.arange(128)[None, :]
    d["maskut"] = (s_ <= t_).astype(NPBF)
    return d


def make_in_maps(ins):
    shared = prep_shared(ins)
    x = np.asarray(ins["L0_attn_x"], dtype=np.float32)
    in_maps = []
    for c in range(8):
        r = c % 4
        m = dict(shared)
        m["xs"] = np.ascontiguousarray(x[c // 4, r * TPC:(r + 1) * TPC, :])
        sel = (np.arange(3) < r).astype(np.float32)
        m["sel01"] = np.ascontiguousarray(
            np.broadcast_to(sel, (128, 3)).astype(np.float32))
        m["selsc"] = np.ascontiguousarray(
            np.broadcast_to(sel * INV16, (128, 3)).astype(np.float32))
        m["selbi"] = np.ascontiguousarray(
            np.broadcast_to(-(1.0 - sel) * 100.0, (128, 3)).astype(np.float32))
        in_maps.append(m)
    return in_maps


def kernel(L0_attn_x, L0_attn_Wq, L0_attn_Wk, L0_attn_Wv, L0_attn_Wgk1,
           L0_attn_Wgk2, L0_attn_bgk2, L0_attn_Wg, L0_attn_Wnorm, L0_attn_Wo,
           L0_ffn_w_gate, L0_ffn_w_up, L0_ffn_w_down, L0_ffn_s_gate,
           L0_ffn_s_up, L0_ffn_s_down):
    ins = dict(L0_attn_x=L0_attn_x, L0_attn_Wq=L0_attn_Wq, L0_attn_Wk=L0_attn_Wk,
               L0_attn_Wv=L0_attn_Wv, L0_attn_Wgk1=L0_attn_Wgk1,
               L0_attn_Wgk2=L0_attn_Wgk2, L0_attn_bgk2=L0_attn_bgk2,
               L0_attn_Wg=L0_attn_Wg, L0_attn_Wnorm=L0_attn_Wnorm,
               L0_attn_Wo=L0_attn_Wo, L0_ffn_w_gate=L0_ffn_w_gate,
               L0_ffn_w_up=L0_ffn_w_up, L0_ffn_w_down=L0_ffn_w_down,
               L0_ffn_s_gate=L0_ffn_s_gate, L0_ffn_s_up=L0_ffn_s_up,
               L0_ffn_s_down=L0_ffn_s_down)
    in_maps = make_in_maps(ins)
    nc = _get_nc()
    res = run_bass_kernel_spmd(nc, in_maps, core_ids=list(range(8)))
    out = np.empty((2, 8192, D), dtype=np.float32)
    for c in range(8):
        out[c // 4, (c % 4) * TPC:(c % 4 + 1) * TPC, :] = res.results[c]["y"]
    return out
